# version 12
# baseline (speedup 1.0000x reference)
import os
import numpy as np
import concourse.bass as bass
import concourse.mybir as mybir
from concourse.bass import IndirectOffsetOnAxis
from concourse.bass_utils import run_bass_kernel_spmd
from contextlib import ExitStack

F32 = mybir.dt.float32
BF16 = mybir.dt.bfloat16
U32 = mybir.dt.uint32
AF = mybir.ActivationFunctionType
ALU = mybir.AluOpType

B, S, D = 4, 4096, 1024
H, HD = 16, 64
HP = H // 2
E, FF = 8, 4096
OWN = 1024
NCORES = 8
EPS = 1e-5
CSHIFT = 12.0
SM_SCALE = 0.125
PAIR_MASK = [i ^ 1 for i in range(32)]
SAFE_SYNC = os.environ.get("KERNEL_SAFE_SYNC") == "1"
DEBUG_DUMP = os.environ.get("KERNEL_DEBUG_DUMP") == "1"


class Sched:

    ENG_SEMS = {"tensor": "tT", "vector": "tV", "scalar": "tA", "gpsimd": "tG"}

    def __init__(self, nc, stack):
        self.nc = nc
        self.stack = stack
        self.ops = {k: [] for k in ("sync", "scalar", "vector", "tensor", "gpsimd")}
        self.sems = {}
        self.count = {}
        for s in ("tT", "tV", "tA", "tG"):
            self.sems[s] = stack.enter_context(nc.semaphore(s))
            self.count[s] = 0
        self.hwm = {k: {} for k in self.ops}

    def op(self, eng, emit, wait=(), dma=False, slot=None):
        ownsem = self.ENG_SEMS.get(eng)
        guard = SAFE_SYNC or eng in ("vector", "gpsimd")
        selfwait = ((ownsem, self.count[ownsem])
                    if (guard and ownsem and self.count[ownsem] > 0) else None)
        if dma:
            assert slot is not None, "DMA ops need a slot name"
            sem = "D_" + slot
            if sem not in self.sems:
                self.sems[sem] = self.stack.enter_context(self.nc.semaphore(sem))
                self.count[sem] = 0
        else:
            sem = self.ENG_SEMS[eng]
        self.count[sem] += 16 if dma else 1
        tok = (sem, self.count[sem])

        def _flat(ws):
            for w in ws:
                if w is None:
                    continue
                if isinstance(w, list):
                    yield from _flat(w)
                else:
                    yield w
        waits = []
        for (s, v) in _flat([selfwait] + list(wait)):
            if self.hwm[eng].get(s, 0) >= v:
                continue
            self.hwm[eng][s] = v
            waits.append((s, v))
        self.ops[eng].append((waits, emit, tok, 16 if dma else 1))
        return tok

    def barrier(self):
        return [(s, c) for s, c in self.count.items() if c > 0]

    def emit_all(self, block, final_eng, final_waits):
        sems = self.sems

        def mk(engname):
            oplist = self.ops[engname]
            extra = final_waits if engname == final_eng else ()

            def emitfn(g):
                for waits, emit, tok, amt in oplist:
                    for (s, v) in waits:
                        g.wait_ge(sems[s], v)
                    emit(g).then_inc(sems[tok[0]], amt)
                for (s, v) in extra:
                    g.wait_ge(sems[s], v)
            return emitfn

        block.sync(mk("sync"))
        block.scalar(mk("scalar"))
        block.vector(mk("vector"))
        block.tensor(mk("tensor"))
        block.gpsimd(mk("gpsimd"))


def build_program(G):
    assert 1 <= G <= 4
    nc = bass.Bass("TRN2", target_bir_lowering=False, debug=False,
                   detect_race_conditions=SAFE_SYNC)

    def I(name, shape, dt=F32):
        return nc.dram_tensor(name, list(shape), dt, kind="ExternalInput").ap()

    xin = [I("xa", (S, D)), I("xb", (S, D))]
    wqt = I("wqt", (HP, 128, D)); wkt = I("wkt", (HP, 128, D))
    wvt = I("wvt", (128, 8, D))
    bqt = I("bqt", (128, HP)); bkt = I("bkt", (128, HP))
    bvt = I("bvt", (H, 128))
    rc_hi = I("rc_hi", (128, S)); rc_lo = I("rc_lo", (128, S))
    rs_hi = I("rs_hi", (128, S)); rs_lo = I("rs_lo", (128, S))
    ident = I("ident", (128, 128))
    cw = I("cw", (OWN, E))
    cntt = I("cntt", (128, E * G))
    gidx = I("gidx", (128, E * G), U32)
    b1f = I("b1f", (E, FF)); b2f = I("b2f", (E, D))
    w1t = I("w1t", (E, 8, 128, FF))
    w2t = I("w2t", (E, 2, 32, 128, 512))

    out = nc.dram_tensor("out", [2 * OWN, D], F32, kind="ExternalOutput").ap()
    dbg = {}
    if DEBUG_DUMP:
        dbg["ctx"] = nc.dram_tensor("dbg_ctx", [128, 8 * D], F32, kind="ExternalOutput").ap()
        dbg["p"] = nc.dram_tensor("dbg_p", [128, 512], F32, kind="ExternalOutput").ap()
        dbg["ctxT"] = nc.dram_tensor("dbg_ctxT", [65, 512], F32, kind="ExternalOutput").ap()
        dbg["dn"] = nc.dram_tensor("dbg_dn", [32, 512], F32, kind="ExternalOutput").ap()
        dbg["kh"] = nc.dram_tensor("dbg_kh", [128, S], F32, kind="ExternalOutput").ap()
        dbg["qh"] = nc.dram_tensor("dbg_qh", [128, OWN], F32, kind="ExternalOutput").ap()
        dbg["xln"] = nc.dram_tensor("dbg_xln", [128, 4 * D], F32, kind="ExternalOutput").ap()
        dbg["xlnT"] = nc.dram_tensor("dbg_xlnT", [128, 8 * 512], F32, kind="ExternalOutput").ap()
        dbg["kev"] = nc.dram_tensor("dbg_kev", [128, 512], F32, kind="ExternalOutput").ap()
        dbg["krp"] = nc.dram_tensor("dbg_krp", [128, 512], F32, kind="ExternalOutput").ap()

    kspill = nc.dram_tensor("kspill", [2, H * 128, S], BF16, kind="Internal").ap()
    qspill = nc.dram_tensor("qspill", [2, H * 128, OWN], BF16, kind="Internal").ap()
    vspill = nc.dram_tensor("vspill", [2, S, D], F32, kind="Internal").ap()
    yspill = nc.dram_tensor("yspill", [OWN, D], F32, kind="Internal").ap()

    st_main = ExitStack()
    e = st_main.enter_context
    sch = Sched(nc, st_main)

    ident_sb = e(nc.sbuf_tensor("ident_sb", [128, 128], F32))
    bq_sb = e(nc.sbuf_tensor("bq_sb", [128, HP], F32))
    bk_sb = e(nc.sbuf_tensor("bk_sb", [128, HP], F32))
    ones_sb = e(nc.sbuf_tensor("ones_sb", [1, 128], F32))
    eps_sb = e(nc.sbuf_tensor("eps_sb", [128, 1], F32))
    negc_sb = e(nc.sbuf_tensor("negc_sb", [128, 1], F32))
    stats_sb = e(nc.sbuf_tensor("stats_sb", [128, 12], F32))
    mv_sb = e(nc.sbuf_tensor("mv_sb", [128, 2], F32))
    sd_sb = e(nc.sbuf_tensor("sd_sb", [128, 1], F32))
    rstd_sb = e(nc.sbuf_tensor("rstd_sb", [128, 2], F32))
    nb_sb = e(nc.sbuf_tensor("nb_sb", [128, 2], F32))

    psS = [e(nc.psum_tensor(f"psS{i}", [128, 1024], F32)) for i in range(2)]
    psC = [e(nc.psum_tensor(f"psC{i}", [128, 512], F32)) for i in range(4)]
    ps = [psS[0][:, 0:512], psS[0][:, 512:1024],
          psS[1][:, 0:512], psS[1][:, 512:1024],
          psC[0][:], psC[1][:], psC[2][:], psC[3][:]]
    bank_cons = [None] * 8

    sch.op("vector", lambda g: g.memset(eps_sb[:], EPS))
    sch.op("vector", lambda g: g.memset(negc_sb[:], -CSHIFT))
    t_ones = sch.op("vector", lambda g: g.memset(ones_sb[:], 1.0))
    t_id = sch.op("sync", lambda g: g.dma_start(out=ident_sb[:], in_=ident), dma=True, slot="ident")
    t_bq = sch.op("sync", lambda g: g.dma_start(out=bq_sb[:], in_=bqt), dma=True, slot="bq")
    t_bk = sch.op("sync", lambda g: g.dma_start(out=bk_sb[:], in_=bkt), dma=True, slot="bk")

    ln_free = [None, None]

    def ln_normalize(src_ap, dst_ap, t_src, par, extra=()):
        sch.op("vector", lambda g: g.bn_stats(out=stats_sb[:, 0:6], in_=src_ap[:, 0:512]),
               wait=[t_src])
        sch.op("vector", lambda g: g.bn_stats(out=stats_sb[:, 6:12], in_=src_ap[:, 512:1024]))
        t_ag = sch.op("vector", lambda g: g.bn_aggr(out=mv_sb[:], in_=stats_sb[:]))
        t_sd = sch.op("scalar", lambda g: g.activation(
            out=sd_sb[:], in_=mv_sb[:, 1:2], func=AF.Sqrt, bias=eps_sb[:], scale=1.0),
            wait=[t_ag])
        sch.op("vector", (lambda p: lambda g: g.reciprocal(
            out=rstd_sb[:, p:p + 1], in_=sd_sb[:]))(par), wait=[t_sd, ln_free[par]])
        t_nb = sch.op("vector", (lambda p: lambda g: g.scalar_tensor_tensor(
            out=nb_sb[:, p:p + 1], in0=mv_sb[:, 0:1], scalar=-1.0,
            in1=rstd_sb[:, p:p + 1], op0=ALU.mult, op1=ALU.mult))(par))
        t_act = sch.op("scalar", (lambda p: lambda g: g.activation(
            out=dst_ap, in_=src_ap, func=AF.Identity,
            bias=nb_sb[:, p:p + 1], scale=rstd_sb[:, p:p + 1]))(par),
            wait=[t_nb] + list(extra))
        ln_free[par] = t_act
        return t_act

    ph1 = ExitStack()
    e1 = ph1.enter_context
    wq_sb = [e1(nc.sbuf_tensor("wq_sb0", [128, 8, 128], F32))] * 2
    wk_sb = [e1(nc.sbuf_tensor("wk_sb0", [128, 8, 128], F32))] * 2
    wqh_sb = [e1(nc.sbuf_tensor(f"wqh_sb{i}", [128, 2, 8, 128], BF16)) for i in range(2)]
    wkh_sb = [e1(nc.sbuf_tensor(f"wkh_sb{i}", [128, 2, 8, 128], BF16)) for i in range(2)]
    wvh_sb = e1(nc.sbuf_tensor("wvh_sb", [128, 2, 8, D], BF16))
    ropec_sb = e1(nc.sbuf_tensor("ropec_sb", [128, S], F32))
    ropes_sb = e1(nc.sbuf_tensor("ropes_sb", [128, S], F32))
    xt_sb = [e1(nc.sbuf_tensor(f"xt_sb{i}", [128, D], F32)) for i in range(2)]
    xln_sb = e1(nc.sbuf_tensor("xln_sb", [128, 4, D], F32))
    rtmp_sb = xln_sb[:].rearrange("p a b -> p (a b)")
    xlnTh_sb2 = [e1(nc.sbuf_tensor(f"xlnTh_sb{i}", [128, 2, 8, 512], BF16)) for i in range(2)]
    kev_sb = e1(nc.sbuf_tensor("kev_sb", [128, 512], F32))
    ksh_sb = e1(nc.sbuf_tensor("ksh_sb", [128, 512], F32))
    kt1_sb = e1(nc.sbuf_tensor("kt1_sb", [128, 512], F32))
    kt2_sb = e1(nc.sbuf_tensor("kt2_sb", [128, 512], F32))
    krp_sb = [e1(nc.sbuf_tensor(f"krp_sb{i}", [128, 512], F32)) for i in range(2)]
    hilo_sb = [e1(nc.sbuf_tensor(f"hilo_sb{i}", [128, 2, 512], BF16)) for i in range(2)]
    vev_sb = [e1(nc.sbuf_tensor(f"vev_sb{i}", [128, 512], F32)) for i in range(2)]

    t_wv = None
    for ch in range(2):
        t_l = sch.op("sync", (lambda c_: lambda g: g.dma_start(
            out=xln_sb[:], in_=wvt[:, c_ * 4:(c_ + 1) * 4, :]))(ch),
            wait=[t_wv], dma=True, slot="wv")
        sch.op("vector", (lambda c_: lambda g: g.tensor_copy(
            out=wvh_sb[:, 0, c_ * 4:(c_ + 1) * 4, :], in_=xln_sb[:]))(ch), wait=[t_l])
        t_wv = sch.op("vector", (lambda c_: lambda g: g.tensor_tensor(
            out=wvh_sb[:, 1, c_ * 4:(c_ + 1) * 4, :], in0=xln_sb[:],
            in1=wvh_sb[:, 0, c_ * 4:(c_ + 1) * 4, :], op=ALU.subtract))(ch))
    ta = sch.op("sync", lambda g: g.dma_start(out=ropec_sb[:], in_=rc_hi), dma=True, slot="rc")
    tb = sch.op("sync", lambda g: g.dma_start(out=rtmp_sb, in_=rc_lo), dma=True, slot="rtmp",
                wait=[t_wv])
    t_rc = sch.op("vector", lambda g: g.tensor_tensor(
        out=ropec_sb[:], in0=ropec_sb[:], in1=rtmp_sb, op=ALU.add), wait=[ta, tb])
    ta = sch.op("sync", lambda g: g.dma_start(out=ropes_sb[:], in_=rs_hi), dma=True, slot="rs")
    tb = sch.op("sync", lambda g: g.dma_start(out=rtmp_sb, in_=rs_lo), dma=True,
                wait=[t_rc], slot="rtmp")
    t_rs = sch.op("vector", lambda g: g.tensor_tensor(
        out=ropes_sb[:], in0=ropes_sb[:], in1=rtmp_sb, op=ALU.add), wait=[ta, tb])

    fr = {"xt": [None, None], "krp": [None, None], "vev": [None, None],
          "wq": [None, None], "wk": [None, None], "xlnT": [None, None],
          "xln": [None] * 4}

    def ph1_batch(bi):
        xsrc = xin[bi]
        for tt in range(8):
            xlnTh_sb = xlnTh_sb2[tt % 2]
            last_ev = None
            for stl in range(4):
                sp = stl % 2
                rows = tt * 512 + stl * 128
                t_ld = sch.op("sync", (lambda sp_, r_, x_: lambda g: g.dma_start(
                    out=xt_sb[sp_][:], in_=x_[r_:r_ + 128, :]))(sp, rows, xsrc),
                    wait=[fr["xt"][sp]], dma=True, slot=f"xt{sp}")
                t_act = ln_normalize(xt_sb[sp][:], xln_sb[:, stl, :], t_ld, sp,
                                     extra=[fr["xln"][stl], t_wv, t_rs])
                fr["xt"][sp] = t_act
                for half in range(2):
                    tbk = 0 + half
                    tr_tok = None
                    for j in range(4):
                        dc = half * 4 + j
                        w = [t_act, t_id]
                        if j == 0:
                            w.append(bank_cons[tbk])
                        tr_tok = sch.op("tensor", (lambda s_, d_, b_, j_: lambda g: g.transpose(
                            ps[b_][:, j_ * 128:(j_ + 1) * 128],
                            xln_sb[:, s_, d_ * 128:(d_ + 1) * 128], ident_sb[:]))(stl, dc, tbk, j),
                            wait=w)
                    fr["xln"][stl] = tr_tok
                    t_hi = sch.op("scalar", (lambda s_, h_, b_: lambda g: g.activation(
                        out=xlnTh_sb[:, 0, h_ * 4:(h_ + 1) * 4, s_ * 128:(s_ + 1) * 128],
                        in_=ps[b_][:].rearrange("p (c t) -> p c t", c=4),
                        func=AF.Identity, scale=1.0))(stl, half, tbk),
                        wait=[tr_tok, fr["xlnT"][tt % 2]])
                    last_ev = sch.op("vector", (lambda s_, h_, b_: lambda g: g.tensor_tensor(
                        out=xlnTh_sb[:, 1, h_ * 4:(h_ + 1) * 4, s_ * 128:(s_ + 1) * 128],
                        in0=ps[b_][:].rearrange("p (c t) -> p c t", c=4),
                        in1=xlnTh_sb[:, 0, h_ * 4:(h_ + 1) * 4, s_ * 128:(s_ + 1) * 128],
                        op=ALU.subtract))(stl, half, tbk), wait=[t_hi])
                    bank_cons[tbk] = last_ev
            t_xlnT = last_ev
            if DEBUG_DUMP and bi == 1 and tt == 0:
                td = sch.op("gpsimd", lambda g: g.dma_start(
                    out=dbg["xln"], in_=xln_sb[:].rearrange("p a b -> p (a b)")),
                    wait=[t_xlnT], dma=True, slot="dbgxln")
                td2 = sch.op("gpsimd", lambda g: g.dma_start(
                    out=dbg["xlnT"], in_=xlnT_sb[:].rearrange("p a b -> p (a b)")),
                    wait=[t_xlnT], dma=True, slot="dbgxlnT")
                t_xlnT = [t_xlnT, td, td2]
            last_proj_mm = None

            for which in ("k", "q"):
                if which == "q" and tt >= 2:
                    continue
                ws_ = wk_sb if which == "k" else wq_sb
                wsrc = wkt if which == "k" else wqt
                bsb = bk_sb if which == "k" else bq_sb
                tbias = t_bk if which == "k" else t_bq
                spill = kspill if which == "k" else qspill
                for hp_i in range(HP):
                    par = hp_i % 2
                    t_w = sch.op("sync", (lambda p_, h_, s_, d_: lambda g: g.dma_start(
                        out=d_[p_][:], in_=s_[h_]))(par, hp_i, wsrc, ws_),
                        wait=[fr["w" + which][par], fr.get("wslab" + which)],
                        dma=True, slot=f"w{which}{par}")
                    wh_ = wkh_sb if which == "k" else wqh_sb
                    t_whi = sch.op("scalar", (lambda p_, wl_, wh2_: lambda g: g.activation(
                        out=wh2_[p_][:, 0], in_=wl_[p_][:], func=AF.Identity,
                        scale=1.0))(par, ws_, wh_), wait=[t_w])
                    t_w = sch.op("vector", (lambda p_, wl_, wh2_: lambda g: g.tensor_tensor(
                        out=wh2_[p_][:, 1], in0=wl_[p_][:], in1=wh2_[p_][:, 0],
                        op=ALU.subtract))(par, ws_, wh_), wait=[t_whi])
                    fr["wslab" + which] = t_w
                    pb = 2 + par
                    mm_tok = None
                    for dc in range(8):
                        w = [t_w, t_xlnT, bank_cons[pb]] if dc == 0 else []
                        for wi, xi in ((0, 0), (1, 0), (0, 1)):
                            mm_tok = sch.op("tensor", (lambda p_, d_, b_, wh2_, wi_, xi_, f_: lambda g: g.matmul(
                                ps[b_][:], wh2_[p_][:, wi_, d_, :],
                                xlnTh_sb[:, xi_, d_, :],
                                start=(d_ == 0 and f_), stop=(d_ == 7 and wi_ == 0 and xi_ == 1)))(
                                    par, dc, pb, wh_, wi, xi, (wi == 0 and xi == 0)),
                                wait=w if (wi == 0 and xi == 0) else ())
                    fr["w" + which][par] = mm_tok
                    last_proj_mm = mm_tok
                    t_ev = sch.op("scalar", (lambda h_, b_, bs_: lambda g: g.activation(
                        out=kev_sb[:], in_=ps[b_][:], func=AF.Identity,
                        bias=bs_[:, h_:h_ + 1], scale=1.0))(hp_i, pb, bsb),
                        wait=[mm_tok, tbias, fr.get("kev")])
                    bank_cons[pb] = t_ev
                    if DEBUG_DUMP and bi == 1 and tt == 0 and which == "k" and hp_i == 0:
                        tdk = sch.op("gpsimd", lambda g: g.dma_start(
                            out=dbg["kev"], in_=kev_sb[:]), wait=[t_ev], dma=True, slot="dbgkev")
                        bank_cons[pb] = [t_ev, tdk]
                    sch.op("vector", lambda g: g.stream_shuffle(
                        out=ksh_sb[:], in_=kev_sb[:], mask=PAIR_MASK), wait=[t_ev])
                    sch.op("vector", (lambda tt_: lambda g: g.tensor_tensor(
                        out=kt1_sb[:], in0=kev_sb[:],
                        in1=ropec_sb[:, tt_ * 512:(tt_ + 1) * 512], op=ALU.mult))(tt),
                        wait=[t_rc])
                    fr["kev"] = sch.op("vector", (lambda tt_: lambda g: g.tensor_tensor(
                        out=kt2_sb[:], in0=ksh_sb[:],
                        in1=ropes_sb[:, tt_ * 512:(tt_ + 1) * 512], op=ALU.mult))(tt),
                        wait=[t_rs])
                    kp = hp_i % 2
                    t_rp = sch.op("vector", (lambda kp_: lambda g: g.tensor_tensor(
                        out=krp_sb[kp_][:], in0=kt1_sb[:], in1=kt2_sb[:], op=ALU.add))(kp))
                    sch.op("vector", (lambda kp_: lambda g: g.tensor_copy(
                        out=hilo_sb[kp_][:, 0, :], in_=krp_sb[kp_][:]))(kp),
                        wait=[fr["krp"][kp]])
                    t_rp = sch.op("vector", (lambda kp_: lambda g: g.tensor_tensor(
                        out=hilo_sb[kp_][:, 1, :], in0=krp_sb[kp_][:],
                        in1=hilo_sb[kp_][:, 0, :], op=ALU.subtract))(kp))
                    sps = []
                    for hh in range(2):
                        sps.append(sch.op("scalar", (lambda kp_, h_, tt_, b_, sp_, hh_: lambda g: g.dma_start(
                            out=sp_[b_, (2 * h_ + hh_) * 128:(2 * h_ + hh_ + 1) * 128,
                                    tt_ * 512:(tt_ + 1) * 512]
                                .rearrange("(h p) c -> p h c", h=2),
                            in_=hilo_sb[kp_][hh_ * 64:(hh_ + 1) * 64, :, :]))(kp, hp_i, tt, bi, spill, hh),
                            wait=[t_rp], dma=True, slot=f"krp{kp}{which}{hh}"))
                    fr["krp"][kp] = sps

            for vc in range(2):
                for tc in range(4):
                    pb = 4 + tc % 2
                    mm_tok = None
                    for dc in range(8):
                        w = [t_wv, t_xlnT, bank_cons[pb]] if dc == 0 else []
                        for xi, wi in ((0, 0), (1, 0), (0, 1)):
                            mm_tok = sch.op("tensor", (lambda v_, t_, d_, b_, xi_, wi_, f_: lambda g: g.matmul(
                                ps[b_][:], xlnTh_sb[:, xi_, d_, t_ * 128:(t_ + 1) * 128],
                                wvh_sb[:, wi_, d_, v_ * 512:(v_ + 1) * 512],
                                start=(d_ == 0 and f_), stop=(d_ == 7 and xi_ == 0 and wi_ == 1)))(
                                    vc, tc, dc, pb, xi, wi, (xi == 0 and wi == 0)),
                                wait=w if (xi == 0 and wi == 0) else ())
                    last_proj_mm = mm_tok
                    vp = tc % 2
                    t_ev = sch.op("scalar", (lambda vp_, b_: lambda g: g.activation(
                        out=vev_sb[vp_][:], in_=ps[b_][:], func=AF.Identity,
                        scale=1.0))(vp, pb),
                        wait=[mm_tok, fr["vev"][vp]])
                    bank_cons[pb] = t_ev
                    t_sp = sch.op("gpsimd", (lambda vp_, v_, t_, tt_, b_: lambda g: g.dma_start(
                        out=vspill[b_, tt_ * 512 + t_ * 128: tt_ * 512 + (t_ + 1) * 128,
                                   v_ * 512:(v_ + 1) * 512],
                        in_=vev_sb[vp_][:]))(vp, vc, tc, tt, bi),
                        wait=[t_ev], dma=True, slot=f"vev{vp}")
                    fr["vev"][vp] = t_sp
            fr["xlnT"][tt % 2] = last_proj_mm

    ph1_batch(0)
    ph1_batch(1)
    ph1.close()
    bar1 = sch.barrier()

    ph2 = ExitStack()
    e2 = ph2.enter_context
    ksb = [e2(nc.sbuf_tensor(f"ksb{i}", [128, S], BF16)) for i in range(2)]
    bvt_sb = e2(nc.sbuf_tensor("bvt_sb", [1, H * 128], F32))
    ctx_sb = e2(nc.sbuf_tensor("ctx_sb", [128, 8, D], F32))
    vh_sb = [e2(nc.sbuf_tensor(f"vh_sb{i}", [128, 32, 128], F32)) for i in range(2)]
    qdh_sb = [e2(nc.sbuf_tensor(f"qdh_sb{i}", [128, OWN], BF16)) for i in range(2)]
    qdl_sb = [e2(nc.sbuf_tensor(f"qdl_sb{i}", [128, OWN], BF16)) for i in range(2)]
    pT_sb = [e2(nc.sbuf_tensor(f"pT_sb{i}", [128, 1024], F32)) for i in range(3)]
    ctxT_sb = e2(nc.sbuf_tensor("ctxT_sb", [65, 512], F32))
    bvo_sb = e2(nc.sbuf_tensor("bvo_sb", [65, 512], F32))
    dn_sb = e2(nc.sbuf_tensor("dn_sb", [32, 512], F32))
    rdn_sb = e2(nc.sbuf_tensor("rdn_sb", [128, 1], F32))
    xatt_sb = [e2(nc.sbuf_tensor(f"xatt_sb{i}", [128, D], F32)) for i in range(2)]
    y_sb = [e2(nc.sbuf_tensor(f"y_sb{i}", [128, D], F32)) for i in range(2)]

    t_bv = sch.op("sync", lambda g: g.dma_start(
        out=bvt_sb[:].rearrange("o (h c) -> o h c", h=H), in_=bvt),
        wait=bar1, dma=True, slot="bv")
    tvz = []
    for i in range(2):
        sch.op("vector", (lambda i_: lambda g: g.memset(vh_sb[i_][:], 0.0))(i),
               wait=bar1 if i == 0 else ())
        tvz.append(sch.op("vector", (lambda i_: lambda g: g.memset(
            vh_sb[i_][:, :, 64:65], 1.0))(i)))
    frq = {"k": [None, None], "v": [tvz[0], tvz[1]], "q": [None, None],
           "pT": [None, None, None], "xatt": [None, None], "y": [None, None],
           "ctxT": None, "dn": None}
    t_out_tokens = []

    def ph2_batch(bi):
        for h in range(H):
            par = h % 2
            t_kh = sch.op("sync", (lambda b_, h_, p_: lambda g: g.dma_start(
                out=ksb[p_][:], in_=kspill[b_, h_ * 128:(h_ + 1) * 128, :]))(bi, h, par),
                wait=[frq["k"][par]] + bar1, dma=True, slot=f"kh{par}")
            t_vh = sch.op("sync", (lambda b_, h_, p_: lambda g: g.dma_start(
                out=vh_sb[p_][:, :, 0:64],
                in_=vspill[b_, :, h_ * 64:(h_ + 1) * 64]
                    .rearrange("(c p) d -> p c d", p=128)))(bi, h, par),
                wait=[frq["v"][par]], dma=True, slot=f"vh{par}")
            t_qh = []
            for hh in range(2):
                dstq = (qdh_sb, qdl_sb)[hh]
                for dup in range(2):
                    t_qh.append(sch.op("sync", (lambda b_, h_, p_, hh_, d_, dq_: lambda g: g.dma_start(
                        out=dq_[p_][d_ * 64:(d_ + 1) * 64, :],
                        in_=qspill[b_, h_ * 128 + hh_ * 64: h_ * 128 + (hh_ + 1) * 64, :]))(
                            bi, h, par, hh, dup, dstq),
                        wait=[frq["q"][par]] if (hh == 0 and dup == 0) else (),
                        dma=True, slot=f"qh{par}{hh}{dup}"))
            last_s = None
            last_pv = None
            exp_toks = [None] * 32
            s_done = [None] * 32

            def emit_s(kc):
                nonlocal last_s
                sp = kc % 2
                w = [t_kh] + t_qh
                if kc >= 2:
                    w.append(exp_toks[kc - 2])
                else:
                    w += [bank_cons[2 * sp], bank_cons[2 * sp + 1]]
                for qt in range(2):
                    sch.op("tensor", (lambda p_, k_, s_, q_: lambda g: g.matmul(
                        psS[s_][:, q_ * 512:(q_ + 1) * 512],
                        ksb[p_][:, k_ * 128:(k_ + 1) * 128],
                        qdh_sb[p_][:, q_ * 512:(q_ + 1) * 512],
                        start=True, stop=False))(par, kc, sp, qt),
                        wait=w if qt == 0 else ())
                    last_s = sch.op("tensor", (lambda p_, k_, s_, q_: lambda g: g.matmul(
                        psS[s_][:, q_ * 512:(q_ + 1) * 512],
                        ksb[p_][:, k_ * 128:(k_ + 1) * 128],
                        qdl_sb[p_][:, q_ * 512:(q_ + 1) * 512],
                        start=False, stop=True))(par, kc, sp, qt))
                s_done[kc] = last_s

            def emit_exp(kc):
                sp = kc % 2
                pp = kc % 3
                exp_toks[kc] = sch.op("scalar", (lambda s_, p_: lambda g: g.activation(
                    out=pT_sb[p_][:], in_=psS[s_][:], func=AF.Exp,
                    bias=negc_sb[:], scale=SM_SCALE))(sp, pp),
                    wait=[s_done[kc], frq["pT"][pp]])

            def emit_pv(kc):
                nonlocal last_pv
                pp = kc % 3
                pvs = []
                for qt in range(2):
                    wp = [exp_toks[kc], t_vh]
                    if kc == 0:
                        wp.append(bank_cons[4 + qt])
                    pvs.append(sch.op("tensor", (lambda p_, k_, q_, pp_: lambda g: g.matmul(
                        psC[q_][:], vh_sb[p_][:, k_, :], pT_sb[pp_][:, q_ * 512:(q_ + 1) * 512],
                        start=(k_ == 0), stop=(k_ == 31)))(par, kc, qt, pp),
                        wait=wp))
                frq["pT"][pp] = pvs
                last_pv = pvs

            emit_s(0)
            emit_exp(0)
            for kc in range(1, 32):
                emit_s(kc)
                emit_exp(kc)
                emit_pv(kc - 1)
            emit_pv(31)
            bank_cons[0] = bank_cons[1] = exp_toks[30]
            bank_cons[2] = bank_cons[3] = exp_toks[31]
            for qt in range(2):
                t_dn = sch.op("vector", (lambda q_: lambda g: g.stream_shuffle(
                    out=dn_sb[:], in_=psC[q_][64:96, :], mask=[0] * 32))(qt),
                    wait=last_pv + [frq["dn"]])
                t_bvm = sch.op("tensor", (lambda h_: lambda g: g.matmul(
                    psC[2][:], bvt_sb[:, h_ * 128:(h_ + 1) * 128], dn_sb[0:1, :],
                    start=True, stop=True))(h), wait=[t_dn, t_bv, bank_cons[6]])
                frq["dn"] = t_bvm
                t_bve = sch.op("vector", lambda g: g.tensor_copy(
                    out=bvo_sb[:], in_=psC[2][0:65, :]), wait=[t_bvm, frq.get("bvo")])
                bank_cons[6] = t_bve
                t_cev = sch.op("vector", (lambda q_: lambda g: g.tensor_tensor(
                    out=ctxT_sb[:], in0=psC[q_][0:65, :], in1=bvo_sb[:],
                    op=ALU.add))(qt),
                    wait=[t_bve, frq["ctxT"]])
                frq["bvo"] = t_cev
                bank_cons[4 + qt] = t_cev
                if DEBUG_DUMP and bi == 1 and h == 0 and qt == 0:
                    tdmp = sch.op("gpsimd", lambda g: g.dma_start(
                        out=dbg["ctxT"], in_=ctxT_sb[:]), wait=[t_cev], dma=True, slot="dbgctxT")
                    frq["ctxT"] = [frq["ctxT"], tdmp]
                for qc in range(4):
                    w = [t_cev, bank_cons[7]]
                    t_tr = sch.op("tensor", (lambda q_: lambda g: g.transpose(
                        psC[3][:, q_ * 72:q_ * 72 + 65],
                        ctxT_sb[:, q_ * 128:(q_ + 1) * 128], ident_sb[0:65, 0:65]))(qc),
                        wait=w)
                    frq["ctxT"] = t_tr
                    sch.op("vector", (lambda q_: lambda g: g.reciprocal(
                        out=rdn_sb[:], in_=psC[3][:, q_ * 72 + 64:q_ * 72 + 65]))(qc),
                        wait=[t_tr])
                    t_cx = sch.op("vector", (lambda q_, qt_, h_: lambda g: g.tensor_scalar(
                        out=ctx_sb[:, qt_ * 4 + q_, h_ * 64:(h_ + 1) * 64],
                        in0=psC[3][:, q_ * 72:q_ * 72 + 64],
                        scalar1=rdn_sb[:], scalar2=None, op0=ALU.mult))(qc, qt, h))
                    bank_cons[7] = t_cx
            frq["k"][par] = last_s
            frq["v"][par] = last_pv
            frq["q"][par] = last_s
        return ("tV", sch.count["tV"])

    def ph3_batch(bi, t_ctx_ready, do_ln2):
        last_tok = None
        for stl in range(8):
            sp = stl % 2
            t_ld = sch.op("sync", (lambda s_, b_, p_: lambda g: g.dma_start(
                out=xatt_sb[p_][:], in_=xin[b_][s_ * 128:(s_ + 1) * 128, :]))(stl, bi, sp),
                wait=[frq["xatt"][sp]], dma=True, slot=f"xatt{sp}")
            t_add = sch.op("vector", (lambda s_, p_: lambda g: g.tensor_tensor(
                out=xatt_sb[p_][:], in0=xatt_sb[p_][:], in1=ctx_sb[:, s_, :],
                op=ALU.add))(stl, sp), wait=[t_ld, t_ctx_ready])
            t_st = sch.op("gpsimd", (lambda s_, b_, p_: lambda g: g.dma_start(
                out=out[b_ * OWN + s_ * 128: b_ * OWN + (s_ + 1) * 128, :],
                in_=xatt_sb[p_][:]))(stl, bi, sp), wait=[t_add], dma=True, slot=f"ost{sp}")
            t_out_tokens.append(t_st)
            if do_ln2:
                t_y = ln_normalize(xatt_sb[sp][:], y_sb[sp][:], t_add, sp,
                                   extra=[frq["y"][sp]])
                t_ys = sch.op("gpsimd", (lambda s_, p_: lambda g: g.dma_start(
                    out=yspill[s_ * 128:(s_ + 1) * 128, :], in_=y_sb[p_][:]))(stl, sp),
                    wait=[t_y, frq["y"][sp]], dma=True, slot=f"ys{sp}")
                frq["y"][sp] = t_ys
                frq["xatt"][sp] = [t_st, t_y]
                last_tok = t_ys
            else:
                frq["xatt"][sp] = t_st
                last_tok = t_st
        return last_tok, list(t_out_tokens)

    tA = ph2_batch(0)
    t_y_last, t_ph3a_dG = ph3_batch(0, tA, True)
    tB = ph2_batch(1)
    if DEBUG_DUMP:
        sch.op("gpsimd", lambda g: g.dma_start(
            out=dbg["ctx"], in_=ctx_sb[:].rearrange("p t d -> p (t d)")),
            wait=[tB], dma=True, slot="dbgctx")
    _, t_ph3b_dG = ph3_batch(1, tB, False)
    t_out_base = t_ph3a_dG + t_ph3b_dG
    ph2.close()
    bar2 = sch.barrier()

    ph4 = ExitStack()
    e4 = ph4.enter_context
    gidx_sb = e4(nc.sbuf_tensor("gidx_sb", [128, E * G], U32))
    cntt_sb = e4(nc.sbuf_tensor("cntt_sb", [128, E * G], F32))
    cwg_sb = e4(nc.sbuf_tensor("cwg_sb", [128, E], F32))
    m_sb = e4(nc.sbuf_tensor("m_sb", [128, 2, G], F32))
    gy_sb = [e4(nc.sbuf_tensor(f"gy_sb{i}", [128, G, D], F32)) for i in range(2)]
    gyT_sb = [e4(nc.sbuf_tensor(f"gyT_sb{i}", [128, 2, 8, G * 128], BF16)) for i in range(2)]
    gT_sb = e4(nc.sbuf_tensor("gT_sb", [128, 2, 32, G * 128], BF16))
    w1s_sb = e4(nc.sbuf_tensor("w1s_sb", [128, 8, 512], F32))
    w1_sb = [e4(nc.sbuf_tensor(f"w1_sb{i}", [128, 2, 8, 512], BF16)) for i in range(2)]
    w2s_sb = [e4(nc.sbuf_tensor(f"w2s_sb{i}", [128, 2, 512], F32)) for i in range(2)]
    w2_sb = [e4(nc.sbuf_tensor(f"w2_sb{i}", [128, 2, 2, 512], BF16)) for i in range(2)]
    b1_sb = [e4(nc.sbuf_tensor("b1_sb0", [1, FF], F32))] * 2
    b2_sb = [e4(nc.sbuf_tensor("b2_sb0", [1, D], F32))] * 2
    gt_sb = e4(nc.sbuf_tensor("gt_sb", [128, 512], F32))
    sc_sb = [e4(nc.sbuf_tensor(f"sc_sb{i}", [128, D], F32)) for i in range(G)]

    t_gi = sch.op("sync", lambda g: g.dma_start(out=gidx_sb[:], in_=gidx),
                  wait=bar2, dma=True, slot="gidx")
    t_cn = sch.op("sync", lambda g: g.dma_start(out=cntt_sb[:], in_=cntt), dma=True, slot="cntt")

    frf = {"w1": [None, None], "w2": [None, None], "gy": [[None], [None]],
           "gyT": [None, None], "gT": None, "b": [None, None],
           "sc": [None] * G, "cwg": None, "m": [[None] * G, [None] * G]}
    t_scatters = []
    t_gath_e = [None] * (E + 1)
    t_m_e = [None] * (E + 1)

    def emit_gathers(ee):
        par = ee % 2
        t_gath = []
        t_m = []
        for gg in range(G):
            t_g = sch.op("gpsimd", (lambda e_, g_, p_: lambda g: g.indirect_dma_start(
                out=gy_sb[p_][:, g_, :], out_offset=None, in_=yspill,
                in_offset=IndirectOffsetOnAxis(gidx_sb[:, e_ * G + g_:e_ * G + g_ + 1], 0)))(ee, gg, par),
                wait=[t_gi, t_y_last] + frf["gy"][par] + bar2, dma=True, slot=f"gy{par}{gg}")
            t_gath.append(t_g)
            t_w = sch.op("gpsimd", (lambda e_, g_: lambda g: g.indirect_dma_start(
                out=cwg_sb[:], out_offset=None, in_=cw,
                in_offset=IndirectOffsetOnAxis(gidx_sb[:, e_ * G + g_:e_ * G + g_ + 1], 0)))(ee, gg),
                wait=[frf["cwg"]], dma=True, slot="cwg")
            t_m.append(sch.op("vector", (lambda e_, g_, p_: lambda g: g.tensor_tensor(
                out=m_sb[:, p_, g_:g_ + 1], in0=cntt_sb[:, e_ * G + g_:e_ * G + g_ + 1],
                in1=cwg_sb[:, e_:e_ + 1], op=ALU.mult))(ee, gg, par),
                wait=[t_w, t_cn, frf["m"][par][gg]]))
            frf["cwg"] = t_m[-1]
        t_gath_e[ee] = t_gath
        t_m_e[ee] = t_m

    def emit_gyT(ee):
        par = ee % 2
        t_gath = t_gath_e[ee]
        t_gyT = None
        gy_cons = []
        for gg in range(G):
            for half in range(2):
                tbk = 6 + half
                tr_tok = None
                for j in range(4):
                    dc = half * 4 + j
                    w = [t_gath[gg], bank_cons[tbk]] if j == 0 else [t_gath[gg]]
                    tr_tok = sch.op("tensor", (lambda g_, d_, b_, j_, p_: lambda g: g.transpose(
                        ps[b_][:, j_ * 128:(j_ + 1) * 128],
                        gy_sb[p_][:, g_, d_ * 128:(d_ + 1) * 128], ident_sb[:]))(gg, dc, tbk, j, par),
                        wait=w)
                gy_cons.append(tr_tok)
                t_hi = sch.op("scalar", (lambda g_, h_, b_, p_: lambda g: g.activation(
                    out=gyT_sb[p_][:, 0, h_ * 4:(h_ + 1) * 4, g_ * 128:(g_ + 1) * 128],
                    in_=ps[b_][:].rearrange("p (c t) -> p c t", c=4),
                    func=AF.Identity, scale=1.0))(gg, half, tbk, par),
                    wait=[tr_tok, frf["gyT"][par]])
                t_gyT = sch.op("vector", (lambda g_, h_, b_, p_: lambda g: g.tensor_tensor(
                    out=gyT_sb[p_][:, 1, h_ * 4:(h_ + 1) * 4, g_ * 128:(g_ + 1) * 128],
                    in0=ps[b_][:].rearrange("p (c t) -> p c t", c=4),
                    in1=gyT_sb[p_][:, 0, h_ * 4:(h_ + 1) * 4, g_ * 128:(g_ + 1) * 128],
                    op=ALU.subtract))(gg, half, tbk, par),
                    wait=[t_hi])
                bank_cons[tbk] = t_gyT
        frf["gy"][par] = gy_cons
        return t_gyT

    emit_gathers(0)
    t_gyT_par = [None, None]
    t_gyT_par[0] = emit_gyT(0)

    for ee in range(E):
        par = ee % 2
        t_gyT = t_gyT_par[par]
        t_m = t_m_e[ee]
        t_b1 = sch.op("sync", (lambda e_: lambda g: g.dma_start(
            out=b1_sb[0][:], in_=b1f[e_:e_ + 1, :]))(ee),
            wait=[frf.get("b1last")] + bar2, dma=True, slot="b1")
        t_b2 = sch.op("sync", (lambda e_: lambda g: g.dma_start(
            out=b2_sb[0][:], in_=b2f[e_:e_ + 1, :]))(ee),
            wait=[frf.get("b2last")], dma=True, slot="b2")
        if ee + 1 < E:
            emit_gathers(ee + 1)

        t_gT = None
        for fc in range(8):
            wpar = fc % 2
            t_ws = sch.op("sync", (lambda e_, f_: lambda g: g.dma_start(
                out=w1s_sb[:],
                in_=w1t[e_, f_].rearrange("p (d c) -> p d c", d=8)))(ee, fc),
                wait=[frf.get("w1s")], dma=True, slot="w1s")
            t_wh = sch.op("scalar", (lambda p_: lambda g: g.activation(
                out=w1_sb[p_][:, 0], in_=w1s_sb[:], func=AF.Identity, scale=1.0))(wpar),
                wait=[t_ws, frf["w1"][wpar]])
            t_w1 = sch.op("vector", (lambda p_: lambda g: g.tensor_tensor(
                out=w1_sb[p_][:, 1], in0=w1s_sb[:], in1=w1_sb[p_][:, 0],
                op=ALU.subtract))(wpar), wait=[t_wh])
            frf["w1s"] = t_w1
            for tg in range(G):
                pb = tg
                t_bias = sch.op("tensor", (lambda f_, b_, p_: lambda g: g.matmul(
                    ps[b_][:], ones_sb[:], b1_sb[p_][:, f_ * 512:(f_ + 1) * 512],
                    start=True, stop=False))(fc, pb, par),
                    wait=[t_b1, t_ones, bank_cons[pb]])
                frf["b1last"] = t_bias
                last_mm = None
                for dc in range(8):
                    w = [t_w1, t_gyT] if dc == 0 else []
                    for gi, wi in ((0, 0), (1, 0), (0, 1)):
                        last_mm = sch.op("tensor", (lambda t_, d_, b_, p_, ep_, gi_, wi_: lambda g: g.matmul(
                            ps[b_][:], gyT_sb[ep_][:, gi_, d_, t_ * 128:(t_ + 1) * 128],
                            w1_sb[p_][:, wi_, d_, :],
                            start=False, stop=(d_ == 7 and gi_ == 0 and wi_ == 1)))(
                                tg, dc, pb, wpar, par, gi, wi),
                            wait=w if (gi == 0 and wi == 0) else ())
                frf["w1"][wpar] = last_mm
                t_ge = sch.op("scalar", (lambda b_: lambda g: g.activation(
                    out=gt_sb[:], in_=ps[b_][:], func=AF.Gelu_apprx_tanh))(pb),
                    wait=[last_mm, frf.get("gt")])
                bank_cons[pb] = t_ge
                tr_tok = None
                for j in range(4):
                    w = [t_ge, bank_cons[5]] if j == 0 else [t_ge]
                    tr_tok = sch.op("tensor", (lambda j_: lambda g: g.transpose(
                        ps[5][:, j_ * 128:(j_ + 1) * 128],
                        gt_sb[:, j_ * 128:(j_ + 1) * 128], ident_sb[:]))(j),
                        wait=w)
                frf["gt"] = tr_tok
                t_hi2 = sch.op("scalar", (lambda f_, t_: lambda g: g.activation(
                    out=gT_sb[:, 0, f_ * 4:(f_ + 1) * 4, t_ * 128:(t_ + 1) * 128],
                    in_=ps[5][:].rearrange("p (c t) -> p c t", c=4),
                    func=AF.Identity, scale=1.0))(fc, tg),
                    wait=[tr_tok, frf["gT"]])
                t_gT = sch.op("vector", (lambda f_, t_: lambda g: g.tensor_tensor(
                    out=gT_sb[:, 1, f_ * 4:(f_ + 1) * 4, t_ * 128:(t_ + 1) * 128],
                    in0=ps[5][:].rearrange("p (c t) -> p c t", c=4),
                    in1=gT_sb[:, 0, f_ * 4:(f_ + 1) * 4, t_ * 128:(t_ + 1) * 128],
                    op=ALU.subtract))(fc, tg),
                    wait=[t_hi2])
                bank_cons[5] = t_gT
        frf["gyT"][par] = t_gT

        if ee + 1 < E:
            t_gyT_par[(ee + 1) % 2] = emit_gyT(ee + 1)

        t_ev = None
        for oc in range(2):
            mm_done = [None] * G
            for fp_ in range(16):
                wpar = fp_ % 2
                t_ws2 = sch.op("sync", (lambda e_, o_, f_, p_: lambda g: g.dma_start(
                    out=w2s_sb[p_][:],
                    in_=w2t[e_, o_, 2 * f_:2 * f_ + 2].rearrange("a p c -> p a c")))(ee, oc, fp_, wpar),
                    wait=[frf.get(f"w2s{wpar}")], dma=True, slot=f"w2s{wpar}")
                t_wh2 = sch.op("scalar", (lambda p_: lambda g: g.activation(
                    out=w2_sb[p_][:, 0], in_=w2s_sb[p_][:], func=AF.Identity, scale=1.0))(wpar),
                    wait=[t_ws2, frf["w2"][wpar]])
                t_w2 = sch.op("vector", (lambda p_: lambda g: g.tensor_tensor(
                    out=w2_sb[p_][:, 1], in0=w2s_sb[p_][:], in1=w2_sb[p_][:, 0],
                    op=ALU.subtract))(wpar), wait=[t_wh2])
                frf[f"w2s{wpar}"] = t_w2
                for sub in range(2):
                    fc2 = 2 * fp_ + sub
                    for tg in range(G):
                        pb = tg
                        if fc2 == 0:
                            frf["b2last"] = sch.op("tensor", (lambda o_, b_, p_: lambda g: g.matmul(
                                ps[b_][:], ones_sb[:], b2_sb[p_][:, o_ * 512:(o_ + 1) * 512],
                                start=True, stop=False))(oc, pb, par),
                                wait=[t_b2, t_ones, bank_cons[pb]])
                        for gi, wi in ((0, 0), (1, 0), (0, 1)):
                            mm_done[tg] = sch.op("tensor", (lambda f_, t_, b_, p_, gi_, wi_, s_: lambda g: g.matmul(
                                ps[b_][:], gT_sb[:, gi_, f_, t_ * 128:(t_ + 1) * 128],
                                w2_sb[p_][:, wi_, s_, :],
                                start=False, stop=(f_ == 31 and gi_ == 0 and wi_ == 1)))(
                                    fc2, tg, pb, wpar, gi, wi, sub),
                                wait=([t_w2, t_gT] if fc2 == 0 else [t_w2]) if (gi == 0 and wi == 0 and sub == 0) else ())
                    frf["w2"][wpar] = mm_done[G - 1]
            for tg in range(G):
                pb = tg
                t_ev = sch.op("vector", (lambda t_, o_, b_, p_: lambda g: g.tensor_scalar(
                    out=sc_sb[t_][:, o_ * 512:(o_ + 1) * 512], in0=ps[b_][:],
                    scalar1=m_sb[:, p_, t_:t_ + 1], scalar2=None, op0=ALU.mult))(tg, oc, pb, par),
                    wait=[mm_done[tg], t_m[tg], frf["sc"][tg]])
                bank_cons[pb] = t_ev
                frf["m"][par][tg] = t_ev
                if oc == 1:
                    t_sc = sch.op("gpsimd", (lambda e_, t_: lambda g: g.indirect_dma_start(
                        out=out, out_offset=IndirectOffsetOnAxis(
                            gidx_sb[:, e_ * G + t_:e_ * G + t_ + 1], 0),
                        in_=sc_sb[t_][:], in_offset=None,
                        compute_op=ALU.add))(ee, tg),
                        wait=[t_ev] + t_out_base, dma=True, slot=f"sc{tg}")
                    frf["sc"][tg] = t_sc
                    t_scatters.append(t_sc)
        frf["gT"] = t_ev

    ph4.close()

    final_waits = sch.barrier()
    with st_main:
        block = st_main.enter_context(nc.Block())
        sch.emit_all(block, "gpsimd", final_waits)
    return nc


def _split_hi_lo(a64):
    hi = a64.astype(np.float32)
    lo = (a64 - hi.astype(np.float64)).astype(np.float32)
    return hi, lo


def _rope_tables_f64(qoff):
    pos = np.arange(S, dtype=np.float64)
    inv = np.power(10000.0, -2.0 * np.arange(HD // 2, dtype=np.float64) / HD)
    ang = pos[None, :] * inv[:, None]
    cos64 = np.empty((HD, S), np.float64)
    sin64 = np.empty((HD, S), np.float64)
    cos64[0::2] = np.cos(ang)
    cos64[1::2] = np.cos(ang)
    sin64[0::2] = -np.sin(ang)
    sin64[1::2] = np.sin(ang)
    cos64 = np.roll(cos64, -qoff, axis=1)
    sin64 = np.roll(sin64, -qoff, axis=1)
    cos64 = np.concatenate([cos64, cos64], axis=0)
    sin64 = np.concatenate([sin64, sin64], axis=0)
    return cos64, sin64


def _plan_from_top_idx(TI):
    tok = TI.T
    plans = []
    maxd = 1
    for c in range(NCORES):
        bA = c // 4
        qoff = (c % 4) * OWN
        lo, hi_ = bA * S + qoff, bA * S + qoff + OWN
        per_e = []
        for ee in range(E):
            te = tok[ee]
            sel = te[(te >= lo) & (te < hi_)] - lo
            rows, cnts = np.unique(sel, return_counts=True)
            per_e.append((rows.astype(np.int64), cnts.astype(np.int64)))
            maxd = max(maxd, len(rows))
        plans.append(per_e)
    G = (maxd + 127) // 128
    return plans, G


def kernel(L0_attn_hidden_states, L0_attn_Wq, L0_attn_bq, L0_attn_Wk, L0_attn_bk,
           L0_attn_Wv, L0_attn_bv, L0_ffn_combine_weights, L0_ffn_top_idx,
           L0_ffn_W1, L0_ffn_b1, L0_ffn_W2, L0_ffn_b2):
    X = np.ascontiguousarray(np.asarray(L0_attn_hidden_states, dtype=np.float32))
    Wq = np.asarray(L0_attn_Wq, np.float32); Wk = np.asarray(L0_attn_Wk, np.float32)
    Wv = np.asarray(L0_attn_Wv, np.float32)
    bq = np.asarray(L0_attn_bq, np.float32); bk = np.asarray(L0_attn_bk, np.float32)
    bv = np.asarray(L0_attn_bv, np.float32)
    CWf = np.ascontiguousarray(np.asarray(L0_ffn_combine_weights, np.float32))
    TI = np.asarray(L0_ffn_top_idx).astype(np.int64)
    W1 = np.asarray(L0_ffn_W1, np.float32); b1 = np.asarray(L0_ffn_b1, np.float32)
    W2 = np.asarray(L0_ffn_W2, np.float32); b2 = np.asarray(L0_ffn_b2, np.float32)

    plans, G = _plan_from_top_idx(TI)

    wqt = Wq.reshape(8, 128, HP, 128).transpose(2, 1, 0, 3).reshape(HP, 128, D).copy()
    wkt = Wk.reshape(8, 128, HP, 128).transpose(2, 1, 0, 3).reshape(HP, 128, D).copy()
    wvt = Wv.reshape(8, 128, D).transpose(1, 0, 2).copy()
    bqt = bq.reshape(HP, 128).T.copy(); bkt = bk.reshape(HP, 128).T.copy()
    bvtm = np.zeros((H, 128), np.float32); bvtm[:, 0:64] = bv.reshape(H, 64)
    identm = np.eye(128, dtype=np.float32)
    w1t = np.ascontiguousarray(
        W1.reshape(E, 8, 128, 8, 512).transpose(0, 3, 2, 1, 4)).reshape(E, 8, 128, FF)
    w2t = np.ascontiguousarray(W2.reshape(E, 32, 128, 2, 512).transpose(0, 3, 1, 2, 4))

    in_maps = []
    for c in range(NCORES):
        bA = c // 4
        bB = 2 + c // 4
        qoff = (c % 4) * OWN
        xa = np.ascontiguousarray(np.roll(X[bA], -qoff, axis=0))
        xb = np.ascontiguousarray(np.roll(X[bB], -qoff, axis=0))
        cos64, sin64 = _rope_tables_f64(qoff)
        rc_hi, rc_lo = _split_hi_lo(cos64)
        rs_hi, rs_lo = _split_hi_lo(sin64)
        cwc = np.ascontiguousarray(CWf[bA * S + qoff: bA * S + qoff + OWN, :])
        gidxm = np.zeros((E * G, 128), np.uint32)
        cnttm = np.zeros((E * G, 128), np.float32)
        for ee in range(E):
            rows, cnts = plans[c][ee]
            n = len(rows)
            used = np.zeros(OWN, bool)
            used[rows] = True
            padrows = np.nonzero(~used)[0][: G * 128 - n]
            allrows = np.concatenate([rows, padrows])
            allcnt = np.concatenate([cnts, np.zeros(len(padrows), np.int64)])
            gidxm[ee * G:(ee + 1) * G, :] = allrows.reshape(G, 128)
            cnttm[ee * G:(ee + 1) * G, :] = allcnt.reshape(G, 128)
        in_maps.append(dict(
            xa=xa, xb=xb, wqt=wqt, wkt=wkt, wvt=wvt, bqt=bqt, bkt=bkt, bvt=bvtm,
            rc_hi=rc_hi, rc_lo=rc_lo, rs_hi=rs_hi, rs_lo=rs_lo, ident=identm,
            cw=cwc, cntt=np.ascontiguousarray(cnttm.T), gidx=np.ascontiguousarray(gidxm.T),
            b1f=b1, b2f=b2, w1t=w1t, w2t=w2t))

    nc = build_program(G)
    res = run_bass_kernel_spmd(nc, in_maps, core_ids=list(range(NCORES)))

    result = np.empty((B, S, D), np.float32)
    for c in range(NCORES):
        o = res.results[c]["out"]
        bA = c // 4
        bB = 2 + c // 4
        qoff = (c % 4) * OWN
        result[bA, qoff:qoff + OWN] = o[0:OWN]
        result[bB, qoff:qoff + OWN] = o[OWN:2 * OWN]
    return result


# version 13
# speedup vs baseline: 1.1549x; 1.1549x over previous
import os
import numpy as np
import concourse.bass as bass
import concourse.mybir as mybir
from concourse.bass import IndirectOffsetOnAxis
from concourse.bass_utils import run_bass_kernel_spmd
from contextlib import ExitStack

F32 = mybir.dt.float32
BF16 = mybir.dt.bfloat16
U32 = mybir.dt.uint32
AF = mybir.ActivationFunctionType
ALU = mybir.AluOpType

B, S, D = 4, 4096, 1024
H, HD = 16, 64
HP = H // 2
E, FF = 8, 4096
OWN = 1024
NCORES = 8
EPS = 1e-5
CSHIFT = 12.0
SM_SCALE = 0.125
PAIR_MASK = [i ^ 1 for i in range(32)]
SAFE_SYNC = os.environ.get("KERNEL_SAFE_SYNC") == "1"
DEBUG_DUMP = os.environ.get("KERNEL_DEBUG_DUMP") == "1"


class Sched:

    ENG_SEMS = {"tensor": "tT", "vector": "tV", "scalar": "tA", "gpsimd": "tG"}

    def __init__(self, nc, stack):
        self.nc = nc
        self.stack = stack
        self.ops = {k: [] for k in ("sync", "scalar", "vector", "tensor", "gpsimd")}
        self.sems = {}
        self.count = {}
        for s in ("tT", "tV", "tA", "tG"):
            self.sems[s] = stack.enter_context(nc.semaphore(s))
            self.count[s] = 0
        self.hwm = {k: {} for k in self.ops}

    def op(self, eng, emit, wait=(), dma=False, slot=None):
        ownsem = self.ENG_SEMS.get(eng)
        guard = SAFE_SYNC or eng in ("vector", "gpsimd")
        selfwait = ((ownsem, self.count[ownsem])
                    if (guard and ownsem and self.count[ownsem] > 0) else None)
        if dma:
            assert slot is not None, "DMA ops need a slot name"
            sem = "D_" + slot
            if sem not in self.sems:
                self.sems[sem] = self.stack.enter_context(self.nc.semaphore(sem))
                self.count[sem] = 0
        else:
            sem = self.ENG_SEMS[eng]
        self.count[sem] += 16 if dma else 1
        tok = (sem, self.count[sem])

        def _flat(ws):
            for w in ws:
                if w is None:
                    continue
                if isinstance(w, list):
                    yield from _flat(w)
                else:
                    yield w
        waits = []
        for (s, v) in _flat([selfwait] + list(wait)):
            if self.hwm[eng].get(s, 0) >= v:
                continue
            self.hwm[eng][s] = v
            waits.append((s, v))
        self.ops[eng].append((waits, emit, tok, 16 if dma else 1))
        return tok

    def barrier(self):
        return [(s, c) for s, c in self.count.items() if c > 0]

    def emit_all(self, block, final_eng, final_waits):
        sems = self.sems

        def mk(engname):
            oplist = self.ops[engname]
            extra = final_waits if engname == final_eng else ()

            def emitfn(g):
                for waits, emit, tok, amt in oplist:
                    for (s, v) in waits:
                        g.wait_ge(sems[s], v)
                    emit(g).then_inc(sems[tok[0]], amt)
                for (s, v) in extra:
                    g.wait_ge(sems[s], v)
            return emitfn

        block.sync(mk("sync"))
        block.scalar(mk("scalar"))
        block.vector(mk("vector"))
        block.tensor(mk("tensor"))
        block.gpsimd(mk("gpsimd"))


def build_program(G):
    assert 1 <= G <= 4
    nc = bass.Bass("TRN2", target_bir_lowering=False, debug=False,
                   detect_race_conditions=SAFE_SYNC)

    def I(name, shape, dt=F32):
        return nc.dram_tensor(name, list(shape), dt, kind="ExternalInput").ap()

    xin = [I("xa", (OWN, D)), I("xb", (OWN, D))]
    wqt = I("wqt", (HP, 128, D)); wkt = I("wkt", (HP, 128, D))
    wvt = I("wvt", (128, 8, D))
    bqt = I("bqt", (128, HP)); bkt = I("bkt", (128, HP))
    bvt = I("bvt", (H, 128))
    rc_hi = I("rc_hi", (128, S)); rc_lo = I("rc_lo", (128, S))
    rs_hi = I("rs_hi", (128, S)); rs_lo = I("rs_lo", (128, S))
    ident = I("ident", (128, 128))
    cw = I("cw", (OWN, E))
    cntt = I("cntt", (128, E * G))
    gidx = I("gidx", (128, E * G), U32)
    b1f = I("b1f", (E, FF)); b2f = I("b2f", (E, D))
    w1t = I("w1t", (E, 8, 128, FF))
    w2t = I("w2t", (E, 2, 32, 128, 512))

    out = nc.dram_tensor("out", [2 * OWN, D], F32, kind="ExternalOutput").ap()
    dbg = {}
    if DEBUG_DUMP:
        dbg["ctx"] = nc.dram_tensor("dbg_ctx", [128, 8 * D], F32, kind="ExternalOutput").ap()
        dbg["p"] = nc.dram_tensor("dbg_p", [128, 512], F32, kind="ExternalOutput").ap()
        dbg["ctxT"] = nc.dram_tensor("dbg_ctxT", [65, 512], F32, kind="ExternalOutput").ap()
        dbg["dn"] = nc.dram_tensor("dbg_dn", [32, 512], F32, kind="ExternalOutput").ap()
        dbg["kh"] = nc.dram_tensor("dbg_kh", [128, S], F32, kind="ExternalOutput").ap()
        dbg["qh"] = nc.dram_tensor("dbg_qh", [128, OWN], F32, kind="ExternalOutput").ap()
        dbg["xln"] = nc.dram_tensor("dbg_xln", [128, 4 * D], F32, kind="ExternalOutput").ap()
        dbg["xlnT"] = nc.dram_tensor("dbg_xlnT", [128, 8 * 512], F32, kind="ExternalOutput").ap()
        dbg["kev"] = nc.dram_tensor("dbg_kev", [128, 512], F32, kind="ExternalOutput").ap()
        dbg["krp"] = nc.dram_tensor("dbg_krp", [128, 512], F32, kind="ExternalOutput").ap()

    kcon = nc.dram_tensor("kcon", [2, H * 128, OWN], BF16, kind="Internal").ap()
    qspill = nc.dram_tensor("qspill", [2, H * 128, OWN], BF16, kind="Internal").ap()
    vcon = nc.dram_tensor("vcon", [2, OWN, D], F32, kind="Internal").ap()
    kall = nc.dram_tensor("kall", [4 * 2 * H * 128, OWN], BF16, kind="Internal").ap()
    vall = nc.dram_tensor("vall", [4 * 2 * OWN, D], F32, kind="Internal").ap()
    yspill = nc.dram_tensor("yspill", [OWN, D], F32, kind="Internal").ap()

    st_main = ExitStack()
    e = st_main.enter_context
    sch = Sched(nc, st_main)

    ident_sb = e(nc.sbuf_tensor("ident_sb", [128, 128], F32))
    bq_sb = e(nc.sbuf_tensor("bq_sb", [128, HP], F32))
    bk_sb = e(nc.sbuf_tensor("bk_sb", [128, HP], F32))
    ones_sb = e(nc.sbuf_tensor("ones_sb", [1, 128], F32))
    eps_sb = e(nc.sbuf_tensor("eps_sb", [128, 1], F32))
    negc_sb = e(nc.sbuf_tensor("negc_sb", [128, 1], F32))
    stats_sb = e(nc.sbuf_tensor("stats_sb", [128, 12], F32))
    mv_sb = e(nc.sbuf_tensor("mv_sb", [128, 2], F32))
    sd_sb = e(nc.sbuf_tensor("sd_sb", [128, 1], F32))
    rstd_sb = e(nc.sbuf_tensor("rstd_sb", [128, 2], F32))
    nb_sb = e(nc.sbuf_tensor("nb_sb", [128, 2], F32))

    psS = [e(nc.psum_tensor(f"psS{i}", [128, 1024], F32)) for i in range(2)]
    psC = [e(nc.psum_tensor(f"psC{i}", [128, 512], F32)) for i in range(4)]
    ps = [psS[0][:, 0:512], psS[0][:, 512:1024],
          psS[1][:, 0:512], psS[1][:, 512:1024],
          psC[0][:], psC[1][:], psC[2][:], psC[3][:]]
    bank_cons = [None] * 8

    sch.op("vector", lambda g: g.memset(eps_sb[:], EPS))
    sch.op("vector", lambda g: g.memset(negc_sb[:], -CSHIFT))
    t_ones = sch.op("vector", lambda g: g.memset(ones_sb[:], 1.0))
    t_id = sch.op("sync", lambda g: g.dma_start(out=ident_sb[:], in_=ident), dma=True, slot="ident")
    t_bq = sch.op("sync", lambda g: g.dma_start(out=bq_sb[:], in_=bqt), dma=True, slot="bq")
    t_bk = sch.op("sync", lambda g: g.dma_start(out=bk_sb[:], in_=bkt), dma=True, slot="bk")

    ln_free = [None, None]

    def ln_normalize(src_ap, dst_ap, t_src, par, extra=()):
        sch.op("vector", lambda g: g.bn_stats(out=stats_sb[:, 0:6], in_=src_ap[:, 0:512]),
               wait=[t_src])
        sch.op("vector", lambda g: g.bn_stats(out=stats_sb[:, 6:12], in_=src_ap[:, 512:1024]))
        t_ag = sch.op("vector", lambda g: g.bn_aggr(out=mv_sb[:], in_=stats_sb[:]))
        t_sd = sch.op("scalar", lambda g: g.activation(
            out=sd_sb[:], in_=mv_sb[:, 1:2], func=AF.Sqrt, bias=eps_sb[:], scale=1.0),
            wait=[t_ag])
        sch.op("vector", (lambda p: lambda g: g.reciprocal(
            out=rstd_sb[:, p:p + 1], in_=sd_sb[:]))(par), wait=[t_sd, ln_free[par]])
        t_nb = sch.op("vector", (lambda p: lambda g: g.scalar_tensor_tensor(
            out=nb_sb[:, p:p + 1], in0=mv_sb[:, 0:1], scalar=-1.0,
            in1=rstd_sb[:, p:p + 1], op0=ALU.mult, op1=ALU.mult))(par))
        t_act = sch.op("scalar", (lambda p: lambda g: g.activation(
            out=dst_ap, in_=src_ap, func=AF.Identity,
            bias=nb_sb[:, p:p + 1], scale=rstd_sb[:, p:p + 1]))(par),
            wait=[t_nb] + list(extra))
        ln_free[par] = t_act
        return t_act

    ph1 = ExitStack()
    e1 = ph1.enter_context
    wq_sb = [e1(nc.sbuf_tensor("wq_sb0", [128, 8, 128], F32))] * 2
    wk_sb = [e1(nc.sbuf_tensor("wk_sb0", [128, 8, 128], F32))] * 2
    wqh_sb = [e1(nc.sbuf_tensor(f"wqh_sb{i}", [128, 2, 8, 128], BF16)) for i in range(2)]
    wkh_sb = [e1(nc.sbuf_tensor(f"wkh_sb{i}", [128, 2, 8, 128], BF16)) for i in range(2)]
    wvh_sb = e1(nc.sbuf_tensor("wvh_sb", [128, 2, 8, D], BF16))
    ropec_sb = e1(nc.sbuf_tensor("ropec_sb", [128, S], F32))
    ropes_sb = e1(nc.sbuf_tensor("ropes_sb", [128, S], F32))
    xt_sb = [e1(nc.sbuf_tensor(f"xt_sb{i}", [128, D], F32)) for i in range(2)]
    xln_sb = e1(nc.sbuf_tensor("xln_sb", [128, 4, D], F32))
    rtmp_sb = xln_sb[:].rearrange("p a b -> p (a b)")
    xlnTh_sb2 = [e1(nc.sbuf_tensor(f"xlnTh_sb{i}", [128, 2, 8, 512], BF16)) for i in range(2)]
    kev_sb = e1(nc.sbuf_tensor("kev_sb", [128, 512], F32))
    ksh_sb = e1(nc.sbuf_tensor("ksh_sb", [128, 512], F32))
    kt1_sb = e1(nc.sbuf_tensor("kt1_sb", [128, 512], F32))
    kt2_sb = e1(nc.sbuf_tensor("kt2_sb", [128, 512], F32))
    krp_sb = [e1(nc.sbuf_tensor(f"krp_sb{i}", [128, 512], F32)) for i in range(2)]
    hilo_sb = [e1(nc.sbuf_tensor(f"hilo_sb{i}", [128, 2, 512], BF16)) for i in range(2)]
    vev_sb = [e1(nc.sbuf_tensor(f"vev_sb{i}", [128, 512], F32)) for i in range(2)]

    t_wv = None
    for ch in range(2):
        t_l = sch.op("sync", (lambda c_: lambda g: g.dma_start(
            out=xln_sb[:], in_=wvt[:, c_ * 4:(c_ + 1) * 4, :]))(ch),
            wait=[t_wv], dma=True, slot="wv")
        sch.op("vector", (lambda c_: lambda g: g.tensor_copy(
            out=wvh_sb[:, 0, c_ * 4:(c_ + 1) * 4, :], in_=xln_sb[:]))(ch), wait=[t_l])
        t_wv = sch.op("vector", (lambda c_: lambda g: g.tensor_tensor(
            out=wvh_sb[:, 1, c_ * 4:(c_ + 1) * 4, :], in0=xln_sb[:],
            in1=wvh_sb[:, 0, c_ * 4:(c_ + 1) * 4, :], op=ALU.subtract))(ch))
    ta = sch.op("sync", lambda g: g.dma_start(out=ropec_sb[:], in_=rc_hi), dma=True, slot="rc")
    tb = sch.op("sync", lambda g: g.dma_start(out=rtmp_sb, in_=rc_lo), dma=True, slot="rtmp",
                wait=[t_wv])
    t_rc = sch.op("vector", lambda g: g.tensor_tensor(
        out=ropec_sb[:], in0=ropec_sb[:], in1=rtmp_sb, op=ALU.add), wait=[ta, tb])
    ta = sch.op("sync", lambda g: g.dma_start(out=ropes_sb[:], in_=rs_hi), dma=True, slot="rs")
    tb = sch.op("sync", lambda g: g.dma_start(out=rtmp_sb, in_=rs_lo), dma=True,
                wait=[t_rc], slot="rtmp")
    t_rs = sch.op("vector", lambda g: g.tensor_tensor(
        out=ropes_sb[:], in0=ropes_sb[:], in1=rtmp_sb, op=ALU.add), wait=[ta, tb])

    fr = {"xt": [None, None], "krp": [None, None], "vev": [None, None],
          "wq": [None, None], "wk": [None, None], "xlnT": [None, None],
          "xln": [None] * 4}

    spill_toks = []

    def ph1_batch(bi):
        xsrc = xin[bi]
        for tt in range(2):
            xlnTh_sb = xlnTh_sb2[tt % 2]
            last_ev = None
            for stl in range(4):
                sp = stl % 2
                rows = tt * 512 + stl * 128
                t_ld = sch.op("sync", (lambda sp_, r_, x_: lambda g: g.dma_start(
                    out=xt_sb[sp_][:], in_=x_[r_:r_ + 128, :]))(sp, rows, xsrc),
                    wait=[fr["xt"][sp]], dma=True, slot=f"xt{sp}")
                t_act = ln_normalize(xt_sb[sp][:], xln_sb[:, stl, :], t_ld, sp,
                                     extra=[fr["xln"][stl], t_wv, t_rs])
                fr["xt"][sp] = t_act
                for half in range(2):
                    tbk = 0 + half
                    tr_tok = None
                    for j in range(4):
                        dc = half * 4 + j
                        w = [t_act, t_id]
                        if j == 0:
                            w.append(bank_cons[tbk])
                        tr_tok = sch.op("tensor", (lambda s_, d_, b_, j_: lambda g: g.transpose(
                            ps[b_][:, j_ * 128:(j_ + 1) * 128],
                            xln_sb[:, s_, d_ * 128:(d_ + 1) * 128], ident_sb[:]))(stl, dc, tbk, j),
                            wait=w)
                    fr["xln"][stl] = tr_tok
                    t_hi = sch.op("scalar", (lambda s_, h_, b_: lambda g: g.activation(
                        out=xlnTh_sb[:, 0, h_ * 4:(h_ + 1) * 4, s_ * 128:(s_ + 1) * 128],
                        in_=ps[b_][:].rearrange("p (c t) -> p c t", c=4),
                        func=AF.Identity, scale=1.0))(stl, half, tbk),
                        wait=[tr_tok, fr["xlnT"][tt % 2]])
                    last_ev = sch.op("vector", (lambda s_, h_, b_: lambda g: g.tensor_tensor(
                        out=xlnTh_sb[:, 1, h_ * 4:(h_ + 1) * 4, s_ * 128:(s_ + 1) * 128],
                        in0=ps[b_][:].rearrange("p (c t) -> p c t", c=4),
                        in1=xlnTh_sb[:, 0, h_ * 4:(h_ + 1) * 4, s_ * 128:(s_ + 1) * 128],
                        op=ALU.subtract))(stl, half, tbk), wait=[t_hi])
                    bank_cons[tbk] = last_ev
            t_xlnT = last_ev
            if DEBUG_DUMP and bi == 1 and tt == 0:
                td = sch.op("gpsimd", lambda g: g.dma_start(
                    out=dbg["xln"], in_=xln_sb[:].rearrange("p a b -> p (a b)")),
                    wait=[t_xlnT], dma=True, slot="dbgxln")
                td2 = sch.op("gpsimd", lambda g: g.dma_start(
                    out=dbg["xlnT"], in_=xlnT_sb[:].rearrange("p a b -> p (a b)")),
                    wait=[t_xlnT], dma=True, slot="dbgxlnT")
                t_xlnT = [t_xlnT, td, td2]
            last_proj_mm = None

            for which in ("k", "q"):
                ws_ = wk_sb if which == "k" else wq_sb
                wsrc = wkt if which == "k" else wqt
                bsb = bk_sb if which == "k" else bq_sb
                tbias = t_bk if which == "k" else t_bq
                spill = kcon if which == "k" else qspill
                for hp_i in range(HP):
                    par = hp_i % 2
                    t_w = sch.op("sync", (lambda p_, h_, s_, d_: lambda g: g.dma_start(
                        out=d_[p_][:], in_=s_[h_]))(par, hp_i, wsrc, ws_),
                        wait=[fr["w" + which][par], fr.get("wslab" + which)],
                        dma=True, slot=f"w{which}{par}")
                    wh_ = wkh_sb if which == "k" else wqh_sb
                    sch.op("gpsimd", (lambda p_, wl_, wh2_: lambda g: g.tensor_copy(
                        out=wh2_[p_][:, 0], in_=wl_[p_][:]))(par, ws_, wh_), wait=[t_w])
                    t_w = sch.op("gpsimd", (lambda p_, wl_, wh2_: lambda g: g.tensor_tensor(
                        out=wh2_[p_][:, 1], in0=wl_[p_][:], in1=wh2_[p_][:, 0],
                        op=ALU.subtract))(par, ws_, wh_))
                    fr["wslab" + which] = t_w
                    pb = 2 + par
                    mm_tok = None
                    for dc in range(8):
                        w = [t_w, t_xlnT, bank_cons[pb]] if dc == 0 else []
                        for wi, xi in ((0, 0), (1, 0), (0, 1)):
                            mm_tok = sch.op("tensor", (lambda p_, d_, b_, wh2_, wi_, xi_, f_: lambda g: g.matmul(
                                ps[b_][:], wh2_[p_][:, wi_, d_, :],
                                xlnTh_sb[:, xi_, d_, :],
                                start=(d_ == 0 and f_), stop=(d_ == 7 and wi_ == 0 and xi_ == 1)))(
                                    par, dc, pb, wh_, wi, xi, (wi == 0 and xi == 0)),
                                wait=w if (wi == 0 and xi == 0) else ())
                    fr["w" + which][par] = mm_tok
                    last_proj_mm = mm_tok
                    t_ev = sch.op("scalar", (lambda h_, b_, bs_: lambda g: g.activation(
                        out=kev_sb[:], in_=ps[b_][:], func=AF.Identity,
                        bias=bs_[:, h_:h_ + 1], scale=1.0))(hp_i, pb, bsb),
                        wait=[mm_tok, tbias, fr.get("kev")])
                    bank_cons[pb] = t_ev
                    if DEBUG_DUMP and bi == 1 and tt == 0 and which == "k" and hp_i == 0:
                        tdk = sch.op("gpsimd", lambda g: g.dma_start(
                            out=dbg["kev"], in_=kev_sb[:]), wait=[t_ev], dma=True, slot="dbgkev")
                        bank_cons[pb] = [t_ev, tdk]
                    sch.op("vector", lambda g: g.stream_shuffle(
                        out=ksh_sb[:], in_=kev_sb[:], mask=PAIR_MASK), wait=[t_ev])
                    sch.op("vector", (lambda tt_: lambda g: g.tensor_tensor(
                        out=kt1_sb[:], in0=kev_sb[:],
                        in1=ropec_sb[:, tt_ * 512:(tt_ + 1) * 512], op=ALU.mult))(tt),
                        wait=[t_rc])
                    fr["kev"] = sch.op("vector", (lambda tt_: lambda g: g.tensor_tensor(
                        out=kt2_sb[:], in0=ksh_sb[:],
                        in1=ropes_sb[:, tt_ * 512:(tt_ + 1) * 512], op=ALU.mult))(tt),
                        wait=[t_rs])
                    kp = hp_i % 2
                    t_rp = sch.op("vector", (lambda kp_: lambda g: g.tensor_tensor(
                        out=krp_sb[kp_][:], in0=kt1_sb[:], in1=kt2_sb[:], op=ALU.add))(kp))
                    sch.op("vector", (lambda kp_: lambda g: g.tensor_copy(
                        out=hilo_sb[kp_][:, 0, :], in_=krp_sb[kp_][:]))(kp),
                        wait=[fr["krp"][kp]])
                    t_rp = sch.op("vector", (lambda kp_: lambda g: g.tensor_tensor(
                        out=hilo_sb[kp_][:, 1, :], in0=krp_sb[kp_][:],
                        in1=hilo_sb[kp_][:, 0, :], op=ALU.subtract))(kp))
                    sps = []
                    for hh in range(2):
                        sps.append(sch.op("scalar", (lambda kp_, h_, tt_, b_, sp_, hh_: lambda g: g.dma_start(
                            out=sp_[b_, (2 * h_ + hh_) * 128:(2 * h_ + hh_ + 1) * 128,
                                    tt_ * 512:(tt_ + 1) * 512]
                                .rearrange("(h p) c -> p h c", h=2),
                            in_=hilo_sb[kp_][hh_ * 64:(hh_ + 1) * 64, :, :]))(kp, hp_i, tt, bi, spill, hh),
                            wait=[t_rp], dma=True, slot=f"krp{kp}{which}{hh}"))
                    spill_toks.extend(sps)
                    fr["krp"][kp] = sps

            for vc in range(2):
                for tc in range(4):
                    pb = 4 + tc % 2
                    mm_tok = None
                    for dc in range(8):
                        w = [t_wv, t_xlnT, bank_cons[pb]] if dc == 0 else []
                        for xi, wi in ((0, 0), (1, 0), (0, 1)):
                            mm_tok = sch.op("tensor", (lambda v_, t_, d_, b_, xi_, wi_, f_: lambda g: g.matmul(
                                ps[b_][:], xlnTh_sb[:, xi_, d_, t_ * 128:(t_ + 1) * 128],
                                wvh_sb[:, wi_, d_, v_ * 512:(v_ + 1) * 512],
                                start=(d_ == 0 and f_), stop=(d_ == 7 and xi_ == 0 and wi_ == 1)))(
                                    vc, tc, dc, pb, xi, wi, (xi == 0 and wi == 0)),
                                wait=w if (xi == 0 and wi == 0) else ())
                    last_proj_mm = mm_tok
                    vp = tc % 2
                    t_ev = sch.op("scalar", (lambda vp_, b_: lambda g: g.activation(
                        out=vev_sb[vp_][:], in_=ps[b_][:], func=AF.Identity,
                        scale=1.0))(vp, pb),
                        wait=[mm_tok, fr["vev"][vp]])
                    bank_cons[pb] = t_ev
                    t_sp = sch.op("gpsimd", (lambda vp_, v_, t_, tt_, b_: lambda g: g.dma_start(
                        out=vcon[b_, tt_ * 512 + t_ * 128: tt_ * 512 + (t_ + 1) * 128,
                                 v_ * 512:(v_ + 1) * 512],
                        in_=vev_sb[vp_][:]))(vp, vc, tc, tt, bi),
                        wait=[t_ev], dma=True, slot=f"vev{vp}")
                    spill_toks.append(t_sp)
                    fr["vev"][vp] = t_sp
            fr["xlnT"][tt % 2] = last_proj_mm

    ph1_batch(0)
    ph1_batch(1)
    RG = [[0, 1, 2, 3], [4, 5, 6, 7]]
    t_agk = sch.op("gpsimd", lambda g: g.collective_compute(
        "AllGather", ALU.bypass, replica_groups=RG,
        ins=[kcon.opt()], outs=[kall.opt()]), wait=spill_toks)
    t_agv = sch.op("gpsimd", lambda g: g.collective_compute(
        "AllGather", ALU.bypass, replica_groups=RG,
        ins=[vcon.opt()], outs=[vall.opt()]))
    ph1.close()
    bar1 = sch.barrier()

    ph2 = ExitStack()
    e2 = ph2.enter_context
    ksb = [e2(nc.sbuf_tensor(f"ksb{i}", [128, S], BF16)) for i in range(2)]
    bvt_sb = e2(nc.sbuf_tensor("bvt_sb", [1, H * 128], F32))
    ctx_sb = e2(nc.sbuf_tensor("ctx_sb", [128, 8, D], F32))
    vh_sb = [e2(nc.sbuf_tensor(f"vh_sb{i}", [128, 32, 128], F32)) for i in range(2)]
    qdh_sb = [e2(nc.sbuf_tensor(f"qdh_sb{i}", [128, OWN], BF16)) for i in range(2)]
    qdl_sb = [e2(nc.sbuf_tensor(f"qdl_sb{i}", [128, OWN], BF16)) for i in range(2)]
    pT_sb = [e2(nc.sbuf_tensor(f"pT_sb{i}", [128, 1024], F32)) for i in range(3)]
    ctxT_sb = e2(nc.sbuf_tensor("ctxT_sb", [65, 512], F32))
    bvo_sb = e2(nc.sbuf_tensor("bvo_sb", [65, 512], F32))
    dn_sb = e2(nc.sbuf_tensor("dn_sb", [32, 512], F32))
    rdn_sb = e2(nc.sbuf_tensor("rdn_sb", [128, 1], F32))
    xatt_sb = [e2(nc.sbuf_tensor(f"xatt_sb{i}", [128, D], F32)) for i in range(2)]
    y_sb = [e2(nc.sbuf_tensor(f"y_sb{i}", [128, D], F32)) for i in range(2)]

    t_bv = sch.op("sync", lambda g: g.dma_start(
        out=bvt_sb[:].rearrange("o (h c) -> o h c", h=H), in_=bvt),
        wait=bar1, dma=True, slot="bv")
    tvz = []
    for i in range(2):
        sch.op("vector", (lambda i_: lambda g: g.memset(vh_sb[i_][:], 0.0))(i),
               wait=bar1 if i == 0 else ())
        tvz.append(sch.op("vector", (lambda i_: lambda g: g.memset(
            vh_sb[i_][:, :, 64:65], 1.0))(i)))
    frq = {"k": [None, None], "v": [tvz[0], tvz[1]], "q": [None, None],
           "pT": [None, None, None], "xatt": [None, None], "y": [None, None],
           "ctxT": None, "dn": None}
    t_out_tokens = []

    def ph2_batch(bi):
        for h in range(H):
            par = h % 2
            t_kh = []
            t_vh = []
            for r in range(4):
                t_kh.append(sch.op("sync", (lambda b_, h_, p_, r_: lambda g: g.dma_start(
                    out=ksb[p_][:, r_ * 1024:(r_ + 1) * 1024],
                    in_=kall[(r_ * 2 + b_) * H * 128 + h_ * 128:
                             (r_ * 2 + b_) * H * 128 + (h_ + 1) * 128, :]))(bi, h, par, r),
                    wait=[frq["k"][par], t_agk] + bar1, dma=True, slot=f"kh{par}{r}"))
                t_vh.append(sch.op("sync", (lambda b_, h_, p_, r_: lambda g: g.dma_start(
                    out=vh_sb[p_][:, r_ * 8:(r_ + 1) * 8, 0:64],
                    in_=vall[(r_ * 2 + b_) * OWN:(r_ * 2 + b_ + 1) * OWN, h_ * 64:(h_ + 1) * 64]
                        .rearrange("(c p) d -> p c d", p=128)))(bi, h, par, r),
                    wait=[frq["v"][par], t_agv], dma=True, slot=f"vh{par}{r}"))
            t_qh = []
            for hh in range(2):
                dstq = (qdh_sb, qdl_sb)[hh]
                for dup in range(2):
                    t_qh.append(sch.op("sync", (lambda b_, h_, p_, hh_, d_, dq_: lambda g: g.dma_start(
                        out=dq_[p_][d_ * 64:(d_ + 1) * 64, :],
                        in_=qspill[b_, h_ * 128 + hh_ * 64: h_ * 128 + (hh_ + 1) * 64, :]))(
                            bi, h, par, hh, dup, dstq),
                        wait=[frq["q"][par]] if (hh == 0 and dup == 0) else (),
                        dma=True, slot=f"qh{par}{hh}{dup}"))
            last_s = None
            last_pv = None
            exp_toks = [None] * 32
            s_done = [None] * 32

            def emit_s(kc):
                nonlocal last_s
                sp = kc % 2
                w = t_kh + t_qh
                if kc >= 2:
                    w.append(exp_toks[kc - 2])
                else:
                    w += [bank_cons[2 * sp], bank_cons[2 * sp + 1]]
                for qt in range(2):
                    sch.op("tensor", (lambda p_, k_, s_, q_: lambda g: g.matmul(
                        psS[s_][:, q_ * 512:(q_ + 1) * 512],
                        ksb[p_][:, k_ * 128:(k_ + 1) * 128],
                        qdh_sb[p_][:, q_ * 512:(q_ + 1) * 512],
                        start=True, stop=False))(par, kc, sp, qt),
                        wait=w if qt == 0 else ())
                    last_s = sch.op("tensor", (lambda p_, k_, s_, q_: lambda g: g.matmul(
                        psS[s_][:, q_ * 512:(q_ + 1) * 512],
                        ksb[p_][:, k_ * 128:(k_ + 1) * 128],
                        qdl_sb[p_][:, q_ * 512:(q_ + 1) * 512],
                        start=False, stop=True))(par, kc, sp, qt))
                s_done[kc] = last_s

            def emit_exp(kc):
                sp = kc % 2
                pp = kc % 3
                exp_toks[kc] = sch.op("scalar", (lambda s_, p_: lambda g: g.activation(
                    out=pT_sb[p_][:], in_=psS[s_][:], func=AF.Exp,
                    bias=negc_sb[:], scale=SM_SCALE))(sp, pp),
                    wait=[s_done[kc], frq["pT"][pp]])

            def emit_pv(kc):
                nonlocal last_pv
                pp = kc % 3
                pvs = []
                for qt in range(2):
                    wp = [exp_toks[kc]] + t_vh
                    if kc == 0:
                        wp.append(bank_cons[4 + qt])
                    pvs.append(sch.op("tensor", (lambda p_, k_, q_, pp_: lambda g: g.matmul(
                        psC[q_][:], vh_sb[p_][:, k_, :], pT_sb[pp_][:, q_ * 512:(q_ + 1) * 512],
                        start=(k_ == 0), stop=(k_ == 31)))(par, kc, qt, pp),
                        wait=wp))
                frq["pT"][pp] = pvs
                last_pv = pvs

            emit_s(0)
            emit_exp(0)
            for kc in range(1, 32):
                emit_s(kc)
                emit_exp(kc)
                emit_pv(kc - 1)
            emit_pv(31)
            bank_cons[0] = bank_cons[1] = exp_toks[30]
            bank_cons[2] = bank_cons[3] = exp_toks[31]
            for qt in range(2):
                t_dn = sch.op("vector", (lambda q_: lambda g: g.stream_shuffle(
                    out=dn_sb[:], in_=psC[q_][64:96, :], mask=[0] * 32))(qt),
                    wait=last_pv + [frq["dn"]])
                t_bvm = sch.op("tensor", (lambda h_: lambda g: g.matmul(
                    psC[2][:], bvt_sb[:, h_ * 128:(h_ + 1) * 128], dn_sb[0:1, :],
                    start=True, stop=True))(h), wait=[t_dn, t_bv, bank_cons[6]])
                frq["dn"] = t_bvm
                t_bve = sch.op("vector", lambda g: g.tensor_copy(
                    out=bvo_sb[:], in_=psC[2][0:65, :]), wait=[t_bvm, frq.get("bvo")])
                bank_cons[6] = t_bve
                t_cev = sch.op("vector", (lambda q_: lambda g: g.tensor_tensor(
                    out=ctxT_sb[:], in0=psC[q_][0:65, :], in1=bvo_sb[:],
                    op=ALU.add))(qt),
                    wait=[t_bve, frq["ctxT"]])
                frq["bvo"] = t_cev
                bank_cons[4 + qt] = t_cev
                if DEBUG_DUMP and bi == 1 and h == 0 and qt == 0:
                    tdmp = sch.op("gpsimd", lambda g: g.dma_start(
                        out=dbg["ctxT"], in_=ctxT_sb[:]), wait=[t_cev], dma=True, slot="dbgctxT")
                    frq["ctxT"] = [frq["ctxT"], tdmp]
                for qc in range(4):
                    w = [t_cev, bank_cons[7]]
                    t_tr = sch.op("tensor", (lambda q_: lambda g: g.transpose(
                        psC[3][:, q_ * 72:q_ * 72 + 65],
                        ctxT_sb[:, q_ * 128:(q_ + 1) * 128], ident_sb[0:65, 0:65]))(qc),
                        wait=w)
                    frq["ctxT"] = t_tr
                    sch.op("vector", (lambda q_: lambda g: g.reciprocal(
                        out=rdn_sb[:], in_=psC[3][:, q_ * 72 + 64:q_ * 72 + 65]))(qc),
                        wait=[t_tr])
                    t_cx = sch.op("vector", (lambda q_, qt_, h_: lambda g: g.tensor_scalar(
                        out=ctx_sb[:, qt_ * 4 + q_, h_ * 64:(h_ + 1) * 64],
                        in0=psC[3][:, q_ * 72:q_ * 72 + 64],
                        scalar1=rdn_sb[:], scalar2=None, op0=ALU.mult))(qc, qt, h))
                    bank_cons[7] = t_cx
            frq["k"][par] = last_s
            frq["v"][par] = last_pv
            frq["q"][par] = last_s
        return ("tV", sch.count["tV"])

    def ph3_batch(bi, t_ctx_ready, do_ln2):
        last_tok = None
        for stl in range(8):
            sp = stl % 2
            t_ld = sch.op("sync", (lambda s_, b_, p_: lambda g: g.dma_start(
                out=xatt_sb[p_][:], in_=xin[b_][s_ * 128:(s_ + 1) * 128, :]))(stl, bi, sp),
                wait=[frq["xatt"][sp]], dma=True, slot=f"xatt{sp}")
            t_add = sch.op("vector", (lambda s_, p_: lambda g: g.tensor_tensor(
                out=xatt_sb[p_][:], in0=xatt_sb[p_][:], in1=ctx_sb[:, s_, :],
                op=ALU.add))(stl, sp), wait=[t_ld, t_ctx_ready])
            t_st = sch.op("gpsimd", (lambda s_, b_, p_: lambda g: g.dma_start(
                out=out[b_ * OWN + s_ * 128: b_ * OWN + (s_ + 1) * 128, :],
                in_=xatt_sb[p_][:]))(stl, bi, sp), wait=[t_add], dma=True, slot=f"ost{sp}")
            t_out_tokens.append(t_st)
            if do_ln2:
                t_y = ln_normalize(xatt_sb[sp][:], y_sb[sp][:], t_add, sp,
                                   extra=[frq["y"][sp]])
                t_ys = sch.op("gpsimd", (lambda s_, p_: lambda g: g.dma_start(
                    out=yspill[s_ * 128:(s_ + 1) * 128, :], in_=y_sb[p_][:]))(stl, sp),
                    wait=[t_y, frq["y"][sp]], dma=True, slot=f"ys{sp}")
                frq["y"][sp] = t_ys
                frq["xatt"][sp] = [t_st, t_y]
                last_tok = t_ys
            else:
                frq["xatt"][sp] = t_st
                last_tok = t_st
        return last_tok, list(t_out_tokens)

    tA = ph2_batch(0)
    t_y_last, t_ph3a_dG = ph3_batch(0, tA, True)
    tB = ph2_batch(1)
    if DEBUG_DUMP:
        sch.op("gpsimd", lambda g: g.dma_start(
            out=dbg["ctx"], in_=ctx_sb[:].rearrange("p t d -> p (t d)")),
            wait=[tB], dma=True, slot="dbgctx")
    _, t_ph3b_dG = ph3_batch(1, tB, False)
    t_out_base = t_ph3a_dG + t_ph3b_dG
    ph2.close()
    bar2 = sch.barrier()

    ph4 = ExitStack()
    e4 = ph4.enter_context
    gidx_sb = e4(nc.sbuf_tensor("gidx_sb", [128, E * G], U32))
    cntt_sb = e4(nc.sbuf_tensor("cntt_sb", [128, E * G], F32))
    cwg_sb = e4(nc.sbuf_tensor("cwg_sb", [128, E], F32))
    m_sb = e4(nc.sbuf_tensor("m_sb", [128, 2, G], F32))
    gy_sb = [e4(nc.sbuf_tensor(f"gy_sb{i}", [128, G, D], F32)) for i in range(2)]
    gyT_sb = [e4(nc.sbuf_tensor(f"gyT_sb{i}", [128, 2, 8, G * 128], BF16)) for i in range(2)]
    gT_sb = e4(nc.sbuf_tensor("gT_sb", [128, 2, 32, G * 128], BF16))
    w1s_sb = e4(nc.sbuf_tensor("w1s_sb", [128, 8, 512], F32))
    w1_sb = [e4(nc.sbuf_tensor(f"w1_sb{i}", [128, 2, 8, 512], BF16)) for i in range(2)]
    w2s_sb = [e4(nc.sbuf_tensor(f"w2s_sb{i}", [128, 2, 512], F32)) for i in range(2)]
    w2_sb = [e4(nc.sbuf_tensor(f"w2_sb{i}", [128, 2, 2, 512], BF16)) for i in range(2)]
    b1_sb = [e4(nc.sbuf_tensor("b1_sb0", [1, FF], F32))] * 2
    b2_sb = [e4(nc.sbuf_tensor("b2_sb0", [1, D], F32))] * 2
    gt_sb = e4(nc.sbuf_tensor("gt_sb", [128, 512], F32))
    sc_sb = [e4(nc.sbuf_tensor(f"sc_sb{i}", [128, D], F32)) for i in range(G)]

    t_gi = sch.op("sync", lambda g: g.dma_start(out=gidx_sb[:], in_=gidx),
                  wait=bar2, dma=True, slot="gidx")
    t_cn = sch.op("sync", lambda g: g.dma_start(out=cntt_sb[:], in_=cntt), dma=True, slot="cntt")

    frf = {"w1": [None, None], "w2": [None, None], "gy": [[None], [None]],
           "gyT": [None, None], "gT": None, "b": [None, None],
           "sc": [None] * G, "cwg": None, "m": [[None] * G, [None] * G]}
    t_scatters = []
    t_gath_e = [None] * (E + 1)
    t_m_e = [None] * (E + 1)

    def emit_gathers(ee):
        par = ee % 2
        t_gath = []
        t_m = []
        for gg in range(G):
            t_g = sch.op("gpsimd", (lambda e_, g_, p_: lambda g: g.indirect_dma_start(
                out=gy_sb[p_][:, g_, :], out_offset=None, in_=yspill,
                in_offset=IndirectOffsetOnAxis(gidx_sb[:, e_ * G + g_:e_ * G + g_ + 1], 0)))(ee, gg, par),
                wait=[t_gi, t_y_last] + frf["gy"][par] + bar2, dma=True, slot=f"gy{par}{gg}")
            t_gath.append(t_g)
            t_w = sch.op("gpsimd", (lambda e_, g_: lambda g: g.indirect_dma_start(
                out=cwg_sb[:], out_offset=None, in_=cw,
                in_offset=IndirectOffsetOnAxis(gidx_sb[:, e_ * G + g_:e_ * G + g_ + 1], 0)))(ee, gg),
                wait=[frf["cwg"]], dma=True, slot="cwg")
            t_m.append(sch.op("vector", (lambda e_, g_, p_: lambda g: g.tensor_tensor(
                out=m_sb[:, p_, g_:g_ + 1], in0=cntt_sb[:, e_ * G + g_:e_ * G + g_ + 1],
                in1=cwg_sb[:, e_:e_ + 1], op=ALU.mult))(ee, gg, par),
                wait=[t_w, t_cn, frf["m"][par][gg]]))
            frf["cwg"] = t_m[-1]
        t_gath_e[ee] = t_gath
        t_m_e[ee] = t_m

    def emit_gyT(ee):
        par = ee % 2
        t_gath = t_gath_e[ee]
        t_gyT = None
        gy_cons = []
        for gg in range(G):
            for half in range(2):
                tbk = 6 + half
                tr_tok = None
                for j in range(4):
                    dc = half * 4 + j
                    w = [t_gath[gg], bank_cons[tbk]] if j == 0 else [t_gath[gg]]
                    tr_tok = sch.op("tensor", (lambda g_, d_, b_, j_, p_: lambda g: g.transpose(
                        ps[b_][:, j_ * 128:(j_ + 1) * 128],
                        gy_sb[p_][:, g_, d_ * 128:(d_ + 1) * 128], ident_sb[:]))(gg, dc, tbk, j, par),
                        wait=w)
                gy_cons.append(tr_tok)
                t_hi = sch.op("scalar", (lambda g_, h_, b_, p_: lambda g: g.activation(
                    out=gyT_sb[p_][:, 0, h_ * 4:(h_ + 1) * 4, g_ * 128:(g_ + 1) * 128],
                    in_=ps[b_][:].rearrange("p (c t) -> p c t", c=4),
                    func=AF.Identity, scale=1.0))(gg, half, tbk, par),
                    wait=[tr_tok, frf["gyT"][par]])
                t_gyT = sch.op("vector", (lambda g_, h_, b_, p_: lambda g: g.tensor_tensor(
                    out=gyT_sb[p_][:, 1, h_ * 4:(h_ + 1) * 4, g_ * 128:(g_ + 1) * 128],
                    in0=ps[b_][:].rearrange("p (c t) -> p c t", c=4),
                    in1=gyT_sb[p_][:, 0, h_ * 4:(h_ + 1) * 4, g_ * 128:(g_ + 1) * 128],
                    op=ALU.subtract))(gg, half, tbk, par),
                    wait=[t_hi])
                bank_cons[tbk] = t_gyT
        frf["gy"][par] = gy_cons
        return t_gyT

    emit_gathers(0)
    t_gyT_par = [None, None]
    t_gyT_par[0] = emit_gyT(0)

    for ee in range(E):
        par = ee % 2
        t_gyT = t_gyT_par[par]
        t_m = t_m_e[ee]
        t_b1 = sch.op("sync", (lambda e_: lambda g: g.dma_start(
            out=b1_sb[0][:], in_=b1f[e_:e_ + 1, :]))(ee),
            wait=[frf.get("b1last")] + bar2, dma=True, slot="b1")
        t_b2 = sch.op("sync", (lambda e_: lambda g: g.dma_start(
            out=b2_sb[0][:], in_=b2f[e_:e_ + 1, :]))(ee),
            wait=[frf.get("b2last")], dma=True, slot="b2")
        if ee + 1 < E:
            emit_gathers(ee + 1)

        t_gT = None
        for fc in range(8):
            wpar = fc % 2
            t_ws = sch.op("sync", (lambda e_, f_: lambda g: g.dma_start(
                out=w1s_sb[:],
                in_=w1t[e_, f_].rearrange("p (d c) -> p d c", d=8)))(ee, fc),
                wait=[frf.get("w1s")], dma=True, slot="w1s")
            t_wh = sch.op("scalar", (lambda p_: lambda g: g.activation(
                out=w1_sb[p_][:, 0], in_=w1s_sb[:], func=AF.Identity, scale=1.0))(wpar),
                wait=[t_ws, frf["w1"][wpar]])
            t_w1 = sch.op("vector", (lambda p_: lambda g: g.tensor_tensor(
                out=w1_sb[p_][:, 1], in0=w1s_sb[:], in1=w1_sb[p_][:, 0],
                op=ALU.subtract))(wpar), wait=[t_wh])
            frf["w1s"] = t_w1
            for tg in range(G):
                pb = tg
                t_bias = sch.op("tensor", (lambda f_, b_, p_: lambda g: g.matmul(
                    ps[b_][:], ones_sb[:], b1_sb[p_][:, f_ * 512:(f_ + 1) * 512],
                    start=True, stop=False))(fc, pb, par),
                    wait=[t_b1, t_ones, bank_cons[pb]])
                frf["b1last"] = t_bias
                last_mm = None
                for dc in range(8):
                    w = [t_w1, t_gyT] if dc == 0 else []
                    for gi, wi in ((0, 0), (1, 0), (0, 1)):
                        last_mm = sch.op("tensor", (lambda t_, d_, b_, p_, ep_, gi_, wi_: lambda g: g.matmul(
                            ps[b_][:], gyT_sb[ep_][:, gi_, d_, t_ * 128:(t_ + 1) * 128],
                            w1_sb[p_][:, wi_, d_, :],
                            start=False, stop=(d_ == 7 and gi_ == 0 and wi_ == 1)))(
                                tg, dc, pb, wpar, par, gi, wi),
                            wait=w if (gi == 0 and wi == 0) else ())
                frf["w1"][wpar] = last_mm
                t_ge = sch.op("scalar", (lambda b_: lambda g: g.activation(
                    out=gt_sb[:], in_=ps[b_][:], func=AF.Gelu_apprx_tanh))(pb),
                    wait=[last_mm, frf.get("gt")])
                bank_cons[pb] = t_ge
                tr_tok = None
                for j in range(4):
                    w = [t_ge, bank_cons[5]] if j == 0 else [t_ge]
                    tr_tok = sch.op("tensor", (lambda j_: lambda g: g.transpose(
                        ps[5][:, j_ * 128:(j_ + 1) * 128],
                        gt_sb[:, j_ * 128:(j_ + 1) * 128], ident_sb[:]))(j),
                        wait=w)
                frf["gt"] = tr_tok
                t_hi2 = sch.op("scalar", (lambda f_, t_: lambda g: g.activation(
                    out=gT_sb[:, 0, f_ * 4:(f_ + 1) * 4, t_ * 128:(t_ + 1) * 128],
                    in_=ps[5][:].rearrange("p (c t) -> p c t", c=4),
                    func=AF.Identity, scale=1.0))(fc, tg),
                    wait=[tr_tok, frf["gT"]])
                t_gT = sch.op("vector", (lambda f_, t_: lambda g: g.tensor_tensor(
                    out=gT_sb[:, 1, f_ * 4:(f_ + 1) * 4, t_ * 128:(t_ + 1) * 128],
                    in0=ps[5][:].rearrange("p (c t) -> p c t", c=4),
                    in1=gT_sb[:, 0, f_ * 4:(f_ + 1) * 4, t_ * 128:(t_ + 1) * 128],
                    op=ALU.subtract))(fc, tg),
                    wait=[t_hi2])
                bank_cons[5] = t_gT
        frf["gyT"][par] = t_gT

        if ee + 1 < E:
            t_gyT_par[(ee + 1) % 2] = emit_gyT(ee + 1)

        t_ev = None
        for oc in range(2):
            mm_done = [None] * G
            for fp_ in range(16):
                wpar = fp_ % 2
                t_ws2 = sch.op("sync", (lambda e_, o_, f_, p_: lambda g: g.dma_start(
                    out=w2s_sb[p_][:],
                    in_=w2t[e_, o_, 2 * f_:2 * f_ + 2].rearrange("a p c -> p a c")))(ee, oc, fp_, wpar),
                    wait=[frf.get(f"w2s{wpar}")], dma=True, slot=f"w2s{wpar}")
                t_wh2 = sch.op("scalar", (lambda p_: lambda g: g.activation(
                    out=w2_sb[p_][:, 0], in_=w2s_sb[p_][:], func=AF.Identity, scale=1.0))(wpar),
                    wait=[t_ws2, frf["w2"][wpar]])
                t_w2 = sch.op("vector", (lambda p_: lambda g: g.tensor_tensor(
                    out=w2_sb[p_][:, 1], in0=w2s_sb[p_][:], in1=w2_sb[p_][:, 0],
                    op=ALU.subtract))(wpar), wait=[t_wh2])
                frf[f"w2s{wpar}"] = t_w2
                for sub in range(2):
                    fc2 = 2 * fp_ + sub
                    for tg in range(G):
                        pb = tg
                        if fc2 == 0:
                            frf["b2last"] = sch.op("tensor", (lambda o_, b_, p_: lambda g: g.matmul(
                                ps[b_][:], ones_sb[:], b2_sb[p_][:, o_ * 512:(o_ + 1) * 512],
                                start=True, stop=False))(oc, pb, par),
                                wait=[t_b2, t_ones, bank_cons[pb]])
                        for gi, wi in ((0, 0), (1, 0), (0, 1)):
                            mm_done[tg] = sch.op("tensor", (lambda f_, t_, b_, p_, gi_, wi_, s_: lambda g: g.matmul(
                                ps[b_][:], gT_sb[:, gi_, f_, t_ * 128:(t_ + 1) * 128],
                                w2_sb[p_][:, wi_, s_, :],
                                start=False, stop=(f_ == 31 and gi_ == 0 and wi_ == 1)))(
                                    fc2, tg, pb, wpar, gi, wi, sub),
                                wait=([t_w2, t_gT] if fc2 == 0 else [t_w2]) if (gi == 0 and wi == 0 and sub == 0) else ())
                    frf["w2"][wpar] = mm_done[G - 1]
            for tg in range(G):
                pb = tg
                t_ev = sch.op("vector", (lambda t_, o_, b_, p_: lambda g: g.tensor_scalar(
                    out=sc_sb[t_][:, o_ * 512:(o_ + 1) * 512], in0=ps[b_][:],
                    scalar1=m_sb[:, p_, t_:t_ + 1], scalar2=None, op0=ALU.mult))(tg, oc, pb, par),
                    wait=[mm_done[tg], t_m[tg], frf["sc"][tg]])
                bank_cons[pb] = t_ev
                frf["m"][par][tg] = t_ev
                if oc == 1:
                    t_sc = sch.op("gpsimd", (lambda e_, t_: lambda g: g.indirect_dma_start(
                        out=out, out_offset=IndirectOffsetOnAxis(
                            gidx_sb[:, e_ * G + t_:e_ * G + t_ + 1], 0),
                        in_=sc_sb[t_][:], in_offset=None,
                        compute_op=ALU.add))(ee, tg),
                        wait=[t_ev] + t_out_base, dma=True, slot=f"sc{tg}")
                    frf["sc"][tg] = t_sc
                    t_scatters.append(t_sc)
        frf["gT"] = t_ev

    ph4.close()

    final_waits = sch.barrier()
    with st_main:
        block = st_main.enter_context(nc.Block())
        sch.emit_all(block, "gpsimd", final_waits)
    return nc


def _split_hi_lo(a64):
    hi = a64.astype(np.float32)
    lo = (a64 - hi.astype(np.float64)).astype(np.float32)
    return hi, lo


def _rope_tables_f64(qoff):
    pos = np.arange(S, dtype=np.float64)
    inv = np.power(10000.0, -2.0 * np.arange(HD // 2, dtype=np.float64) / HD)
    ang = pos[None, :] * inv[:, None]
    cos64 = np.empty((HD, S), np.float64)
    sin64 = np.empty((HD, S), np.float64)
    cos64[0::2] = np.cos(ang)
    cos64[1::2] = np.cos(ang)
    sin64[0::2] = -np.sin(ang)
    sin64[1::2] = np.sin(ang)
    cos64 = np.roll(cos64, -qoff, axis=1)
    sin64 = np.roll(sin64, -qoff, axis=1)
    cos64 = np.concatenate([cos64, cos64], axis=0)
    sin64 = np.concatenate([sin64, sin64], axis=0)
    return cos64, sin64


def _plan_from_top_idx(TI):
    tok = TI.T
    plans = []
    maxd = 1
    for c in range(NCORES):
        bA = c // 4
        qoff = (c % 4) * OWN
        lo, hi_ = bA * S + qoff, bA * S + qoff + OWN
        per_e = []
        for ee in range(E):
            te = tok[ee]
            sel = te[(te >= lo) & (te < hi_)] - lo
            rows, cnts = np.unique(sel, return_counts=True)
            per_e.append((rows.astype(np.int64), cnts.astype(np.int64)))
            maxd = max(maxd, len(rows))
        plans.append(per_e)
    G = (maxd + 127) // 128
    return plans, G


def kernel(L0_attn_hidden_states, L0_attn_Wq, L0_attn_bq, L0_attn_Wk, L0_attn_bk,
           L0_attn_Wv, L0_attn_bv, L0_ffn_combine_weights, L0_ffn_top_idx,
           L0_ffn_W1, L0_ffn_b1, L0_ffn_W2, L0_ffn_b2):
    X = np.ascontiguousarray(np.asarray(L0_attn_hidden_states, dtype=np.float32))
    Wq = np.asarray(L0_attn_Wq, np.float32); Wk = np.asarray(L0_attn_Wk, np.float32)
    Wv = np.asarray(L0_attn_Wv, np.float32)
    bq = np.asarray(L0_attn_bq, np.float32); bk = np.asarray(L0_attn_bk, np.float32)
    bv = np.asarray(L0_attn_bv, np.float32)
    CWf = np.ascontiguousarray(np.asarray(L0_ffn_combine_weights, np.float32))
    TI = np.asarray(L0_ffn_top_idx).astype(np.int64)
    W1 = np.asarray(L0_ffn_W1, np.float32); b1 = np.asarray(L0_ffn_b1, np.float32)
    W2 = np.asarray(L0_ffn_W2, np.float32); b2 = np.asarray(L0_ffn_b2, np.float32)

    plans, G = _plan_from_top_idx(TI)

    wqt = Wq.reshape(8, 128, HP, 128).transpose(2, 1, 0, 3).reshape(HP, 128, D).copy()
    wkt = Wk.reshape(8, 128, HP, 128).transpose(2, 1, 0, 3).reshape(HP, 128, D).copy()
    wvt = Wv.reshape(8, 128, D).transpose(1, 0, 2).copy()
    bqt = bq.reshape(HP, 128).T.copy(); bkt = bk.reshape(HP, 128).T.copy()
    bvtm = np.zeros((H, 128), np.float32); bvtm[:, 0:64] = bv.reshape(H, 64)
    identm = np.eye(128, dtype=np.float32)
    w1t = np.ascontiguousarray(
        W1.reshape(E, 8, 128, 8, 512).transpose(0, 3, 2, 1, 4)).reshape(E, 8, 128, FF)
    w2t = np.ascontiguousarray(W2.reshape(E, 32, 128, 2, 512).transpose(0, 3, 1, 2, 4))

    in_maps = []
    for c in range(NCORES):
        bA = c // 4
        bB = 2 + c // 4
        qoff = (c % 4) * OWN
        xa = np.ascontiguousarray(X[bA, qoff:qoff + OWN])
        xb = np.ascontiguousarray(X[bB, qoff:qoff + OWN])
        cos64, sin64 = _rope_tables_f64(qoff)
        rc_hi, rc_lo = _split_hi_lo(cos64)
        rs_hi, rs_lo = _split_hi_lo(sin64)
        cwc = np.ascontiguousarray(CWf[bA * S + qoff: bA * S + qoff + OWN, :])
        gidxm = np.zeros((E * G, 128), np.uint32)
        cnttm = np.zeros((E * G, 128), np.float32)
        for ee in range(E):
            rows, cnts = plans[c][ee]
            n = len(rows)
            used = np.zeros(OWN, bool)
            used[rows] = True
            padrows = np.nonzero(~used)[0][: G * 128 - n]
            allrows = np.concatenate([rows, padrows])
            allcnt = np.concatenate([cnts, np.zeros(len(padrows), np.int64)])
            gidxm[ee * G:(ee + 1) * G, :] = allrows.reshape(G, 128)
            cnttm[ee * G:(ee + 1) * G, :] = allcnt.reshape(G, 128)
        in_maps.append(dict(
            xa=xa, xb=xb, wqt=wqt, wkt=wkt, wvt=wvt, bqt=bqt, bkt=bkt, bvt=bvtm,
            rc_hi=rc_hi, rc_lo=rc_lo, rs_hi=rs_hi, rs_lo=rs_lo, ident=identm,
            cw=cwc, cntt=np.ascontiguousarray(cnttm.T), gidx=np.ascontiguousarray(gidxm.T),
            b1f=b1, b2f=b2, w1t=w1t, w2t=w2t))

    nc = build_program(G)
    res = run_bass_kernel_spmd(nc, in_maps, core_ids=list(range(NCORES)))

    result = np.empty((B, S, D), np.float32)
    for c in range(NCORES):
        o = res.results[c]["out"]
        bA = c // 4
        bB = 2 + c // 4
        qoff = (c % 4) * OWN
        result[bA, qoff:qoff + OWN] = o[0:OWN]
        result[bB, qoff:qoff + OWN] = o[OWN:2 * OWN]
    return result


# version 14
# speedup vs baseline: 1.1995x; 1.0386x over previous
import os
import numpy as np
import concourse.bass as bass
import concourse.mybir as mybir
from concourse.bass import IndirectOffsetOnAxis
from concourse.bass_utils import run_bass_kernel_spmd
from contextlib import ExitStack

F32 = mybir.dt.float32
BF16 = mybir.dt.bfloat16
U32 = mybir.dt.uint32
AF = mybir.ActivationFunctionType
ALU = mybir.AluOpType

B, S, D = 4, 4096, 1024
H, HD = 16, 64
HP = H // 2
E, FF = 8, 4096
OWN = 1024
NCORES = 8
EPS = 1e-5
CSHIFT = 12.0
SM_SCALE = 0.125
PAIR_MASK = [i ^ 1 for i in range(32)]
SAFE_SYNC = os.environ.get("KERNEL_SAFE_SYNC") == "1"
DEBUG_DUMP = os.environ.get("KERNEL_DEBUG_DUMP") == "1"


class Sched:

    ENG_SEMS = {"tensor": "tT", "vector": "tV", "scalar": "tA", "gpsimd": "tG"}

    def __init__(self, nc, stack):
        self.nc = nc
        self.stack = stack
        self.ops = {k: [] for k in ("sync", "scalar", "vector", "tensor", "gpsimd")}
        self.sems = {}
        self.count = {}
        for s in ("tT", "tV", "tA", "tG"):
            self.sems[s] = stack.enter_context(nc.semaphore(s))
            self.count[s] = 0
        self.hwm = {k: {} for k in self.ops}

    def op(self, eng, emit, wait=(), dma=False, slot=None):
        ownsem = self.ENG_SEMS.get(eng)
        guard = SAFE_SYNC or eng in ("vector", "gpsimd")
        selfwait = ((ownsem, self.count[ownsem])
                    if (guard and ownsem and self.count[ownsem] > 0) else None)
        if dma:
            assert slot is not None, "DMA ops need a slot name"
            sem = "D_" + slot
            if sem not in self.sems:
                self.sems[sem] = self.stack.enter_context(self.nc.semaphore(sem))
                self.count[sem] = 0
        else:
            sem = self.ENG_SEMS[eng]
        self.count[sem] += 16 if dma else 1
        tok = (sem, self.count[sem])

        def _flat(ws):
            for w in ws:
                if w is None:
                    continue
                if isinstance(w, list):
                    yield from _flat(w)
                else:
                    yield w
        waits = []
        for (s, v) in _flat([selfwait] + list(wait)):
            if self.hwm[eng].get(s, 0) >= v:
                continue
            self.hwm[eng][s] = v
            waits.append((s, v))
        self.ops[eng].append((waits, emit, tok, 16 if dma else 1))
        return tok

    def barrier(self):
        return [(s, c) for s, c in self.count.items() if c > 0]

    def emit_all(self, block, final_eng, final_waits):
        sems = self.sems

        def mk(engname):
            oplist = self.ops[engname]
            extra = final_waits if engname == final_eng else ()

            def emitfn(g):
                for waits, emit, tok, amt in oplist:
                    for (s, v) in waits:
                        g.wait_ge(sems[s], v)
                    emit(g).then_inc(sems[tok[0]], amt)
                for (s, v) in extra:
                    g.wait_ge(sems[s], v)
            return emitfn

        block.sync(mk("sync"))
        block.scalar(mk("scalar"))
        block.vector(mk("vector"))
        block.tensor(mk("tensor"))
        block.gpsimd(mk("gpsimd"))


def build_program(G):
    assert 1 <= G <= 4
    nc = bass.Bass("TRN2", target_bir_lowering=False, debug=False,
                   detect_race_conditions=SAFE_SYNC)

    def I(name, shape, dt=F32):
        return nc.dram_tensor(name, list(shape), dt, kind="ExternalInput").ap()

    xin = [I("xa", (OWN, D)), I("xb", (OWN, D))]
    wqt = I("wqt", (HP, 128, D)); wkt = I("wkt", (HP, 128, D))
    wvt = I("wvt", (128, 8, D))
    bqt = I("bqt", (128, HP)); bkt = I("bkt", (128, HP))
    bvt = I("bvt", (H, 128))
    rc_hi = I("rc_hi", (128, S)); rc_lo = I("rc_lo", (128, S))
    rs_hi = I("rs_hi", (128, S)); rs_lo = I("rs_lo", (128, S))
    ident = I("ident", (128, 128))
    cw = I("cw", (OWN, E))
    cntt = I("cntt", (128, E * G))
    gidx = I("gidx", (128, E * G), U32)
    b1f = I("b1f", (E, FF)); b2f = I("b2f", (E, D))
    w1t = I("w1t", (E, 8, 128, FF))
    w2t = I("w2t", (E, 2, 32, 128, 512))

    out = nc.dram_tensor("out", [2 * OWN, D], F32, kind="ExternalOutput").ap()
    dbg = {}
    if DEBUG_DUMP:
        dbg["ctx"] = nc.dram_tensor("dbg_ctx", [128, 8 * D], F32, kind="ExternalOutput").ap()
        dbg["p"] = nc.dram_tensor("dbg_p", [128, 512], F32, kind="ExternalOutput").ap()
        dbg["ctxT"] = nc.dram_tensor("dbg_ctxT", [65, 512], F32, kind="ExternalOutput").ap()
        dbg["dn"] = nc.dram_tensor("dbg_dn", [32, 512], F32, kind="ExternalOutput").ap()
        dbg["kh"] = nc.dram_tensor("dbg_kh", [128, S], F32, kind="ExternalOutput").ap()
        dbg["qh"] = nc.dram_tensor("dbg_qh", [128, OWN], F32, kind="ExternalOutput").ap()
        dbg["xln"] = nc.dram_tensor("dbg_xln", [128, 4 * D], F32, kind="ExternalOutput").ap()
        dbg["xlnT"] = nc.dram_tensor("dbg_xlnT", [128, 8 * 512], F32, kind="ExternalOutput").ap()
        dbg["kev"] = nc.dram_tensor("dbg_kev", [128, 512], F32, kind="ExternalOutput").ap()
        dbg["krp"] = nc.dram_tensor("dbg_krp", [128, 512], F32, kind="ExternalOutput").ap()

    kcon = nc.dram_tensor("kcon", [2, H * 128, OWN], BF16, kind="Internal").ap()
    qspill = nc.dram_tensor("qspill", [2, H * 128, OWN], BF16, kind="Internal").ap()
    vcon = nc.dram_tensor("vcon", [2, OWN, D], F32, kind="Internal").ap()
    kall = [nc.dram_tensor(f"kall{b}", [4 * H * 128, OWN], BF16, kind="Internal").ap()
            for b in range(2)]
    vall = [nc.dram_tensor(f"vall{b}", [4 * OWN, D], F32, kind="Internal").ap()
            for b in range(2)]
    yspill = nc.dram_tensor("yspill", [OWN, D], F32, kind="Internal").ap()

    st_main = ExitStack()
    e = st_main.enter_context
    sch = Sched(nc, st_main)

    ident_sb = e(nc.sbuf_tensor("ident_sb", [128, 128], F32))
    bq_sb = e(nc.sbuf_tensor("bq_sb", [128, HP], F32))
    bk_sb = e(nc.sbuf_tensor("bk_sb", [128, HP], F32))
    ones_sb = e(nc.sbuf_tensor("ones_sb", [1, 128], F32))
    eps_sb = e(nc.sbuf_tensor("eps_sb", [128, 1], F32))
    negc_sb = e(nc.sbuf_tensor("negc_sb", [128, 1], F32))
    stats_sb = e(nc.sbuf_tensor("stats_sb", [128, 12], F32))
    mv_sb = e(nc.sbuf_tensor("mv_sb", [128, 2], F32))
    sd_sb = e(nc.sbuf_tensor("sd_sb", [128, 1], F32))
    rstd_sb = e(nc.sbuf_tensor("rstd_sb", [128, 2], F32))
    nb_sb = e(nc.sbuf_tensor("nb_sb", [128, 2], F32))

    psS = [e(nc.psum_tensor(f"psS{i}", [128, 1024], F32)) for i in range(2)]
    psC = [e(nc.psum_tensor(f"psC{i}", [128, 512], F32)) for i in range(4)]
    ps = [psS[0][:, 0:512], psS[0][:, 512:1024],
          psS[1][:, 0:512], psS[1][:, 512:1024],
          psC[0][:], psC[1][:], psC[2][:], psC[3][:]]
    bank_cons = [None] * 8

    sch.op("vector", lambda g: g.memset(eps_sb[:], EPS))
    sch.op("vector", lambda g: g.memset(negc_sb[:], -CSHIFT))
    t_ones = sch.op("vector", lambda g: g.memset(ones_sb[:], 1.0))
    t_id = sch.op("sync", lambda g: g.dma_start(out=ident_sb[:], in_=ident), dma=True, slot="ident")
    t_bq = sch.op("sync", lambda g: g.dma_start(out=bq_sb[:], in_=bqt), dma=True, slot="bq")
    t_bk = sch.op("sync", lambda g: g.dma_start(out=bk_sb[:], in_=bkt), dma=True, slot="bk")

    ln_free = [None, None]

    def ln_normalize(src_ap, dst_ap, t_src, par, extra=()):
        sch.op("vector", lambda g: g.bn_stats(out=stats_sb[:, 0:6], in_=src_ap[:, 0:512]),
               wait=[t_src])
        sch.op("vector", lambda g: g.bn_stats(out=stats_sb[:, 6:12], in_=src_ap[:, 512:1024]))
        t_ag = sch.op("vector", lambda g: g.bn_aggr(out=mv_sb[:], in_=stats_sb[:]))
        t_sd = sch.op("scalar", lambda g: g.activation(
            out=sd_sb[:], in_=mv_sb[:, 1:2], func=AF.Sqrt, bias=eps_sb[:], scale=1.0),
            wait=[t_ag])
        sch.op("vector", (lambda p: lambda g: g.reciprocal(
            out=rstd_sb[:, p:p + 1], in_=sd_sb[:]))(par), wait=[t_sd, ln_free[par]])
        t_nb = sch.op("vector", (lambda p: lambda g: g.scalar_tensor_tensor(
            out=nb_sb[:, p:p + 1], in0=mv_sb[:, 0:1], scalar=-1.0,
            in1=rstd_sb[:, p:p + 1], op0=ALU.mult, op1=ALU.mult))(par))
        t_act = sch.op("scalar", (lambda p: lambda g: g.activation(
            out=dst_ap, in_=src_ap, func=AF.Identity,
            bias=nb_sb[:, p:p + 1], scale=rstd_sb[:, p:p + 1]))(par),
            wait=[t_nb] + list(extra))
        ln_free[par] = t_act
        return t_act

    ph1 = ExitStack()
    e1 = ph1.enter_context
    wq_sb = [e1(nc.sbuf_tensor("wq_sb0", [128, 8, 128], F32))] * 2
    wk_sb = [e1(nc.sbuf_tensor("wk_sb0", [128, 8, 128], F32))] * 2
    wqh_sb = [e1(nc.sbuf_tensor(f"wqh_sb{i}", [128, 2, 8, 128], BF16)) for i in range(2)]
    wkh_sb = [e1(nc.sbuf_tensor(f"wkh_sb{i}", [128, 2, 8, 128], BF16)) for i in range(2)]
    wvh_sb = e1(nc.sbuf_tensor("wvh_sb", [128, 2, 8, D], BF16))
    ropec_sb = e1(nc.sbuf_tensor("ropec_sb", [128, S], F32))
    ropes_sb = e1(nc.sbuf_tensor("ropes_sb", [128, S], F32))
    xt_sb = [e1(nc.sbuf_tensor(f"xt_sb{i}", [128, D], F32)) for i in range(2)]
    xln_sb = e1(nc.sbuf_tensor("xln_sb", [128, 4, D], F32))
    rtmp_sb = xln_sb[:].rearrange("p a b -> p (a b)")
    xlnTh_sb2 = [e1(nc.sbuf_tensor(f"xlnTh_sb{i}", [128, 2, 8, 512], BF16)) for i in range(2)]
    kev_sb = e1(nc.sbuf_tensor("kev_sb", [128, 512], F32))
    ksh_sb = e1(nc.sbuf_tensor("ksh_sb", [128, 512], F32))
    kt1_sb = e1(nc.sbuf_tensor("kt1_sb", [128, 512], F32))
    kt2_sb = e1(nc.sbuf_tensor("kt2_sb", [128, 512], F32))
    krp_sb = [e1(nc.sbuf_tensor(f"krp_sb{i}", [128, 512], F32)) for i in range(2)]
    hilo_sb = [e1(nc.sbuf_tensor(f"hilo_sb{i}", [128, 2, 512], BF16)) for i in range(2)]
    vev_sb = [e1(nc.sbuf_tensor(f"vev_sb{i}", [128, 512], F32)) for i in range(2)]

    t_wv = None
    for ch in range(2):
        t_l = sch.op("sync", (lambda c_: lambda g: g.dma_start(
            out=xln_sb[:], in_=wvt[:, c_ * 4:(c_ + 1) * 4, :]))(ch),
            wait=[t_wv], dma=True, slot="wv")
        sch.op("vector", (lambda c_: lambda g: g.tensor_copy(
            out=wvh_sb[:, 0, c_ * 4:(c_ + 1) * 4, :], in_=xln_sb[:]))(ch), wait=[t_l])
        t_wv = sch.op("vector", (lambda c_: lambda g: g.tensor_tensor(
            out=wvh_sb[:, 1, c_ * 4:(c_ + 1) * 4, :], in0=xln_sb[:],
            in1=wvh_sb[:, 0, c_ * 4:(c_ + 1) * 4, :], op=ALU.subtract))(ch))
    ta = sch.op("sync", lambda g: g.dma_start(out=ropec_sb[:], in_=rc_hi), dma=True, slot="rc")
    tb = sch.op("sync", lambda g: g.dma_start(out=rtmp_sb, in_=rc_lo), dma=True, slot="rtmp",
                wait=[t_wv])
    t_rc = sch.op("vector", lambda g: g.tensor_tensor(
        out=ropec_sb[:], in0=ropec_sb[:], in1=rtmp_sb, op=ALU.add), wait=[ta, tb])
    ta = sch.op("sync", lambda g: g.dma_start(out=ropes_sb[:], in_=rs_hi), dma=True, slot="rs")
    tb = sch.op("sync", lambda g: g.dma_start(out=rtmp_sb, in_=rs_lo), dma=True,
                wait=[t_rc], slot="rtmp")
    t_rs = sch.op("vector", lambda g: g.tensor_tensor(
        out=ropes_sb[:], in0=ropes_sb[:], in1=rtmp_sb, op=ALU.add), wait=[ta, tb])

    fr = {"xt": [None, None], "krp": [None, None], "vev": [None, None],
          "wq": [None, None], "wk": [None, None], "xlnT": [None, None],
          "xln": [None] * 4}

    spill_toks = [[], []]

    def ph1_batch(bi):
        xsrc = xin[bi]
        for tt in range(2):
            xlnTh_sb = xlnTh_sb2[tt % 2]
            last_ev = None
            for stl in range(4):
                sp = stl % 2
                rows = tt * 512 + stl * 128
                t_ld = sch.op("sync", (lambda sp_, r_, x_: lambda g: g.dma_start(
                    out=xt_sb[sp_][:], in_=x_[r_:r_ + 128, :]))(sp, rows, xsrc),
                    wait=[fr["xt"][sp]], dma=True, slot=f"xt{sp}")
                t_act = ln_normalize(xt_sb[sp][:], xln_sb[:, stl, :], t_ld, sp,
                                     extra=[fr["xln"][stl], t_wv, t_rs])
                fr["xt"][sp] = t_act
                for half in range(2):
                    tbk = 0 + half
                    tr_tok = None
                    for j in range(4):
                        dc = half * 4 + j
                        w = [t_act, t_id]
                        if j == 0:
                            w.append(bank_cons[tbk])
                        tr_tok = sch.op("tensor", (lambda s_, d_, b_, j_: lambda g: g.transpose(
                            ps[b_][:, j_ * 128:(j_ + 1) * 128],
                            xln_sb[:, s_, d_ * 128:(d_ + 1) * 128], ident_sb[:]))(stl, dc, tbk, j),
                            wait=w)
                    fr["xln"][stl] = tr_tok
                    t_hi = sch.op("scalar", (lambda s_, h_, b_: lambda g: g.activation(
                        out=xlnTh_sb[:, 0, h_ * 4:(h_ + 1) * 4, s_ * 128:(s_ + 1) * 128],
                        in_=ps[b_][:].rearrange("p (c t) -> p c t", c=4),
                        func=AF.Identity, scale=1.0))(stl, half, tbk),
                        wait=[tr_tok, fr["xlnT"][tt % 2]])
                    last_ev = sch.op("vector", (lambda s_, h_, b_: lambda g: g.tensor_tensor(
                        out=xlnTh_sb[:, 1, h_ * 4:(h_ + 1) * 4, s_ * 128:(s_ + 1) * 128],
                        in0=ps[b_][:].rearrange("p (c t) -> p c t", c=4),
                        in1=xlnTh_sb[:, 0, h_ * 4:(h_ + 1) * 4, s_ * 128:(s_ + 1) * 128],
                        op=ALU.subtract))(stl, half, tbk), wait=[t_hi])
                    bank_cons[tbk] = last_ev
            t_xlnT = last_ev
            if DEBUG_DUMP and bi == 1 and tt == 0:
                td = sch.op("gpsimd", lambda g: g.dma_start(
                    out=dbg["xln"], in_=xln_sb[:].rearrange("p a b -> p (a b)")),
                    wait=[t_xlnT], dma=True, slot="dbgxln")
                td2 = sch.op("gpsimd", lambda g: g.dma_start(
                    out=dbg["xlnT"], in_=xlnT_sb[:].rearrange("p a b -> p (a b)")),
                    wait=[t_xlnT], dma=True, slot="dbgxlnT")
                t_xlnT = [t_xlnT, td, td2]
            last_proj_mm = None

            for which in ("k", "q"):
                ws_ = wk_sb if which == "k" else wq_sb
                wsrc = wkt if which == "k" else wqt
                bsb = bk_sb if which == "k" else bq_sb
                tbias = t_bk if which == "k" else t_bq
                spill = kcon if which == "k" else qspill
                for hp_i in range(HP):
                    par = hp_i % 2
                    t_w = sch.op("sync", (lambda p_, h_, s_, d_: lambda g: g.dma_start(
                        out=d_[p_][:], in_=s_[h_]))(par, hp_i, wsrc, ws_),
                        wait=[fr["w" + which][par], fr.get("wslab" + which)],
                        dma=True, slot=f"w{which}{par}")
                    wh_ = wkh_sb if which == "k" else wqh_sb
                    sch.op("gpsimd", (lambda p_, wl_, wh2_: lambda g: g.tensor_copy(
                        out=wh2_[p_][:, 0], in_=wl_[p_][:]))(par, ws_, wh_), wait=[t_w])
                    t_w = sch.op("gpsimd", (lambda p_, wl_, wh2_: lambda g: g.tensor_tensor(
                        out=wh2_[p_][:, 1], in0=wl_[p_][:], in1=wh2_[p_][:, 0],
                        op=ALU.subtract))(par, ws_, wh_))
                    fr["wslab" + which] = t_w
                    pb = 2 + par
                    mm_tok = None
                    for dc in range(8):
                        w = [t_w, t_xlnT, bank_cons[pb]] if dc == 0 else []
                        for wi, xi in ((0, 0), (1, 0), (0, 1)):
                            mm_tok = sch.op("tensor", (lambda p_, d_, b_, wh2_, wi_, xi_, f_: lambda g: g.matmul(
                                ps[b_][:], wh2_[p_][:, wi_, d_, :],
                                xlnTh_sb[:, xi_, d_, :],
                                start=(d_ == 0 and f_), stop=(d_ == 7 and wi_ == 0 and xi_ == 1)))(
                                    par, dc, pb, wh_, wi, xi, (wi == 0 and xi == 0)),
                                wait=w if (wi == 0 and xi == 0) else ())
                    fr["w" + which][par] = mm_tok
                    last_proj_mm = mm_tok
                    t_ev = sch.op("scalar", (lambda h_, b_, bs_: lambda g: g.activation(
                        out=kev_sb[:], in_=ps[b_][:], func=AF.Identity,
                        bias=bs_[:, h_:h_ + 1], scale=1.0))(hp_i, pb, bsb),
                        wait=[mm_tok, tbias, fr.get("kev")])
                    bank_cons[pb] = t_ev
                    if DEBUG_DUMP and bi == 1 and tt == 0 and which == "k" and hp_i == 0:
                        tdk = sch.op("gpsimd", lambda g: g.dma_start(
                            out=dbg["kev"], in_=kev_sb[:]), wait=[t_ev], dma=True, slot="dbgkev")
                        bank_cons[pb] = [t_ev, tdk]
                    sch.op("vector", lambda g: g.stream_shuffle(
                        out=ksh_sb[:], in_=kev_sb[:], mask=PAIR_MASK), wait=[t_ev])
                    sch.op("vector", (lambda tt_: lambda g: g.tensor_tensor(
                        out=kt1_sb[:], in0=kev_sb[:],
                        in1=ropec_sb[:, tt_ * 512:(tt_ + 1) * 512], op=ALU.mult))(tt),
                        wait=[t_rc])
                    fr["kev"] = sch.op("vector", (lambda tt_: lambda g: g.tensor_tensor(
                        out=kt2_sb[:], in0=ksh_sb[:],
                        in1=ropes_sb[:, tt_ * 512:(tt_ + 1) * 512], op=ALU.mult))(tt),
                        wait=[t_rs])
                    kp = hp_i % 2
                    t_rp = sch.op("vector", (lambda kp_: lambda g: g.tensor_tensor(
                        out=krp_sb[kp_][:], in0=kt1_sb[:], in1=kt2_sb[:], op=ALU.add))(kp))
                    sch.op("vector", (lambda kp_: lambda g: g.tensor_copy(
                        out=hilo_sb[kp_][:, 0, :], in_=krp_sb[kp_][:]))(kp),
                        wait=[fr["krp"][kp]])
                    t_rp = sch.op("vector", (lambda kp_: lambda g: g.tensor_tensor(
                        out=hilo_sb[kp_][:, 1, :], in0=krp_sb[kp_][:],
                        in1=hilo_sb[kp_][:, 0, :], op=ALU.subtract))(kp))
                    sps = []
                    for hh in range(2):
                        sps.append(sch.op("scalar", (lambda kp_, h_, tt_, b_, sp_, hh_: lambda g: g.dma_start(
                            out=sp_[b_, (2 * h_ + hh_) * 128:(2 * h_ + hh_ + 1) * 128,
                                    tt_ * 512:(tt_ + 1) * 512]
                                .rearrange("(h p) c -> p h c", h=2),
                            in_=hilo_sb[kp_][hh_ * 64:(hh_ + 1) * 64, :, :]))(kp, hp_i, tt, bi, spill, hh),
                            wait=[t_rp], dma=True, slot=f"krp{kp}{which}{hh}"))
                    if which == "k":
                        spill_toks[bi].extend(sps)
                    fr["krp"][kp] = sps

            for vc in range(2):
                for tc in range(4):
                    pb = 4 + tc % 2
                    mm_tok = None
                    for dc in range(8):
                        w = [t_wv, t_xlnT, bank_cons[pb]] if dc == 0 else []
                        for xi, wi in ((0, 0), (1, 0), (0, 1)):
                            mm_tok = sch.op("tensor", (lambda v_, t_, d_, b_, xi_, wi_, f_: lambda g: g.matmul(
                                ps[b_][:], xlnTh_sb[:, xi_, d_, t_ * 128:(t_ + 1) * 128],
                                wvh_sb[:, wi_, d_, v_ * 512:(v_ + 1) * 512],
                                start=(d_ == 0 and f_), stop=(d_ == 7 and xi_ == 0 and wi_ == 1)))(
                                    vc, tc, dc, pb, xi, wi, (xi == 0 and wi == 0)),
                                wait=w if (xi == 0 and wi == 0) else ())
                    last_proj_mm = mm_tok
                    vp = tc % 2
                    t_ev = sch.op("scalar", (lambda vp_, b_: lambda g: g.activation(
                        out=vev_sb[vp_][:], in_=ps[b_][:], func=AF.Identity,
                        scale=1.0))(vp, pb),
                        wait=[mm_tok, fr["vev"][vp]])
                    bank_cons[pb] = t_ev
                    t_sp = sch.op("gpsimd", (lambda vp_, v_, t_, tt_, b_: lambda g: g.dma_start(
                        out=vcon[b_, tt_ * 512 + t_ * 128: tt_ * 512 + (t_ + 1) * 128,
                                 v_ * 512:(v_ + 1) * 512],
                        in_=vev_sb[vp_][:]))(vp, vc, tc, tt, bi),
                        wait=[t_ev], dma=True, slot=f"vev{vp}")
                    spill_toks[bi].append(t_sp)
                    fr["vev"][vp] = t_sp
            fr["xlnT"][tt % 2] = last_proj_mm

    RG = [[0, 1, 2, 3], [4, 5, 6, 7]]
    t_agk = [None, None]
    t_agv = [None, None]

    def emit_ag(bi):
        t_agk[bi] = sch.op("gpsimd", (lambda b_: lambda g: g.collective_compute(
            "AllGather", ALU.bypass, replica_groups=RG,
            ins=[kcon[b_].opt()], outs=[kall[b_].opt()]))(bi), wait=spill_toks[bi])
        t_agv[bi] = sch.op("gpsimd", (lambda b_: lambda g: g.collective_compute(
            "AllGather", ALU.bypass, replica_groups=RG,
            ins=[vcon[b_].opt()], outs=[vall[b_].opt()]))(bi))

    ph1_batch(0)
    emit_ag(0)
    ph1_batch(1)
    bar1 = sch.barrier()
    emit_ag(1)
    ph1.close()

    ph2 = ExitStack()
    e2 = ph2.enter_context
    ksb = [e2(nc.sbuf_tensor(f"ksb{i}", [128, S], BF16)) for i in range(2)]
    bvt_sb = e2(nc.sbuf_tensor("bvt_sb", [1, H * 128], F32))
    ctx_sb = e2(nc.sbuf_tensor("ctx_sb", [128, 8, D], F32))
    vh_sb = [e2(nc.sbuf_tensor(f"vh_sb{i}", [128, 32, 128], F32)) for i in range(2)]
    qdh_sb = [e2(nc.sbuf_tensor(f"qdh_sb{i}", [128, OWN], BF16)) for i in range(2)]
    qdl_sb = [e2(nc.sbuf_tensor(f"qdl_sb{i}", [128, OWN], BF16)) for i in range(2)]
    pT_sb = [e2(nc.sbuf_tensor(f"pT_sb{i}", [128, 1024], F32)) for i in range(3)]
    ctxT_sb = e2(nc.sbuf_tensor("ctxT_sb", [65, 512], F32))
    bvo_sb = e2(nc.sbuf_tensor("bvo_sb", [65, 512], F32))
    dn_sb = e2(nc.sbuf_tensor("dn_sb", [32, 512], F32))
    rdn_sb = e2(nc.sbuf_tensor("rdn_sb", [128, 1], F32))
    xatt_sb = [e2(nc.sbuf_tensor(f"xatt_sb{i}", [128, D], F32)) for i in range(2)]
    y_sb = [e2(nc.sbuf_tensor(f"y_sb{i}", [128, D], F32)) for i in range(2)]

    t_bv = sch.op("sync", lambda g: g.dma_start(
        out=bvt_sb[:].rearrange("o (h c) -> o h c", h=H), in_=bvt),
        wait=bar1, dma=True, slot="bv")
    tvz = []
    for i in range(2):
        sch.op("vector", (lambda i_: lambda g: g.memset(vh_sb[i_][:], 0.0))(i),
               wait=bar1 if i == 0 else ())
        tvz.append(sch.op("vector", (lambda i_: lambda g: g.memset(
            vh_sb[i_][:, :, 64:65], 1.0))(i)))
    frq = {"k": [None, None], "v": [tvz[0], tvz[1]], "q": [None, None],
           "pT": [None, None, None], "xatt": [None, None], "y": [None, None],
           "ctxT": None, "dn": None}
    t_out_tokens = []

    def ph2_batch(bi):
        for h in range(H):
            par = h % 2
            t_kh = []
            t_vh = []
            for r in range(4):
                t_kh.append(sch.op("sync", (lambda b_, h_, p_, r_: lambda g: g.dma_start(
                    out=ksb[p_][:, r_ * 1024:(r_ + 1) * 1024],
                    in_=kall[b_][r_ * H * 128 + h_ * 128:
                                 r_ * H * 128 + (h_ + 1) * 128, :]))(bi, h, par, r),
                    wait=[frq["k"][par], t_agk[bi]] + bar1, dma=True, slot=f"kh{par}{r}"))
                t_vh.append(sch.op("sync", (lambda b_, h_, p_, r_: lambda g: g.dma_start(
                    out=vh_sb[p_][:, r_ * 8:(r_ + 1) * 8, 0:64],
                    in_=vall[b_][r_ * OWN:(r_ + 1) * OWN, h_ * 64:(h_ + 1) * 64]
                        .rearrange("(c p) d -> p c d", p=128)))(bi, h, par, r),
                    wait=[frq["v"][par], t_agv[bi]], dma=True, slot=f"vh{par}{r}"))
            t_qh = []
            for hh in range(2):
                dstq = (qdh_sb, qdl_sb)[hh]
                for dup in range(2):
                    t_qh.append(sch.op("sync", (lambda b_, h_, p_, hh_, d_, dq_: lambda g: g.dma_start(
                        out=dq_[p_][d_ * 64:(d_ + 1) * 64, :],
                        in_=qspill[b_, h_ * 128 + hh_ * 64: h_ * 128 + (hh_ + 1) * 64, :]))(
                            bi, h, par, hh, dup, dstq),
                        wait=[frq["q"][par]] if (hh == 0 and dup == 0) else (),
                        dma=True, slot=f"qh{par}{hh}{dup}"))
            last_s = None
            last_pv = None
            exp_toks = [None] * 32
            s_done = [None] * 32

            def emit_s(kc):
                nonlocal last_s
                sp = kc % 2
                w = t_kh + t_qh
                if kc >= 2:
                    w.append(exp_toks[kc - 2])
                else:
                    w += [bank_cons[2 * sp], bank_cons[2 * sp + 1]]
                for qt in range(2):
                    sch.op("tensor", (lambda p_, k_, s_, q_: lambda g: g.matmul(
                        psS[s_][:, q_ * 512:(q_ + 1) * 512],
                        ksb[p_][:, k_ * 128:(k_ + 1) * 128],
                        qdh_sb[p_][:, q_ * 512:(q_ + 1) * 512],
                        start=True, stop=False))(par, kc, sp, qt),
                        wait=w if qt == 0 else ())
                    last_s = sch.op("tensor", (lambda p_, k_, s_, q_: lambda g: g.matmul(
                        psS[s_][:, q_ * 512:(q_ + 1) * 512],
                        ksb[p_][:, k_ * 128:(k_ + 1) * 128],
                        qdl_sb[p_][:, q_ * 512:(q_ + 1) * 512],
                        start=False, stop=True))(par, kc, sp, qt))
                s_done[kc] = last_s

            def emit_exp(kc):
                sp = kc % 2
                pp = kc % 3
                exp_toks[kc] = sch.op("scalar", (lambda s_, p_: lambda g: g.activation(
                    out=pT_sb[p_][:], in_=psS[s_][:], func=AF.Exp,
                    bias=negc_sb[:], scale=SM_SCALE))(sp, pp),
                    wait=[s_done[kc], frq["pT"][pp]])

            def emit_pv(kc):
                nonlocal last_pv
                pp = kc % 3
                pvs = []
                for qt in range(2):
                    wp = [exp_toks[kc]] + t_vh
                    if kc == 0:
                        wp.append(bank_cons[4 + qt])
                    pvs.append(sch.op("tensor", (lambda p_, k_, q_, pp_: lambda g: g.matmul(
                        psC[q_][:], vh_sb[p_][:, k_, :], pT_sb[pp_][:, q_ * 512:(q_ + 1) * 512],
                        start=(k_ == 0), stop=(k_ == 31)))(par, kc, qt, pp),
                        wait=wp))
                frq["pT"][pp] = pvs
                last_pv = pvs

            emit_s(0)
            emit_exp(0)
            for kc in range(1, 32):
                emit_s(kc)
                emit_exp(kc)
                emit_pv(kc - 1)
            emit_pv(31)
            bank_cons[0] = bank_cons[1] = exp_toks[30]
            bank_cons[2] = bank_cons[3] = exp_toks[31]
            for qt in range(2):
                t_dn = sch.op("vector", (lambda q_: lambda g: g.stream_shuffle(
                    out=dn_sb[:], in_=psC[q_][64:96, :], mask=[0] * 32))(qt),
                    wait=last_pv + [frq["dn"]])
                t_bvm = sch.op("tensor", (lambda h_: lambda g: g.matmul(
                    psC[2][:], bvt_sb[:, h_ * 128:(h_ + 1) * 128], dn_sb[0:1, :],
                    start=True, stop=True))(h), wait=[t_dn, t_bv, bank_cons[6]])
                frq["dn"] = t_bvm
                t_bve = sch.op("vector", lambda g: g.tensor_copy(
                    out=bvo_sb[:], in_=psC[2][0:65, :]), wait=[t_bvm, frq.get("bvo")])
                bank_cons[6] = t_bve
                t_cev = sch.op("vector", (lambda q_: lambda g: g.tensor_tensor(
                    out=ctxT_sb[:], in0=psC[q_][0:65, :], in1=bvo_sb[:],
                    op=ALU.add))(qt),
                    wait=[t_bve, frq["ctxT"]])
                frq["bvo"] = t_cev
                bank_cons[4 + qt] = t_cev
                if DEBUG_DUMP and bi == 1 and h == 0 and qt == 0:
                    tdmp = sch.op("gpsimd", lambda g: g.dma_start(
                        out=dbg["ctxT"], in_=ctxT_sb[:]), wait=[t_cev], dma=True, slot="dbgctxT")
                    frq["ctxT"] = [frq["ctxT"], tdmp]
                for qc in range(4):
                    w = [t_cev, bank_cons[7]]
                    t_tr = sch.op("tensor", (lambda q_: lambda g: g.transpose(
                        psC[3][:, q_ * 72:q_ * 72 + 65],
                        ctxT_sb[:, q_ * 128:(q_ + 1) * 128], ident_sb[0:65, 0:65]))(qc),
                        wait=w)
                    frq["ctxT"] = t_tr
                    sch.op("vector", (lambda q_: lambda g: g.reciprocal(
                        out=rdn_sb[:], in_=psC[3][:, q_ * 72 + 64:q_ * 72 + 65]))(qc),
                        wait=[t_tr])
                    t_cx = sch.op("vector", (lambda q_, qt_, h_: lambda g: g.tensor_scalar(
                        out=ctx_sb[:, qt_ * 4 + q_, h_ * 64:(h_ + 1) * 64],
                        in0=psC[3][:, q_ * 72:q_ * 72 + 64],
                        scalar1=rdn_sb[:], scalar2=None, op0=ALU.mult))(qc, qt, h))
                    bank_cons[7] = t_cx
            frq["k"][par] = last_s
            frq["v"][par] = last_pv
            frq["q"][par] = last_s
        return ("tV", sch.count["tV"])

    def ph3_batch(bi, t_ctx_ready, do_ln2):
        last_tok = None
        for stl in range(8):
            sp = stl % 2
            t_ld = sch.op("sync", (lambda s_, b_, p_: lambda g: g.dma_start(
                out=xatt_sb[p_][:], in_=xin[b_][s_ * 128:(s_ + 1) * 128, :]))(stl, bi, sp),
                wait=[frq["xatt"][sp]], dma=True, slot=f"xatt{sp}")
            t_add = sch.op("vector", (lambda s_, p_: lambda g: g.tensor_tensor(
                out=xatt_sb[p_][:], in0=xatt_sb[p_][:], in1=ctx_sb[:, s_, :],
                op=ALU.add))(stl, sp), wait=[t_ld, t_ctx_ready])
            t_st = sch.op("gpsimd", (lambda s_, b_, p_: lambda g: g.dma_start(
                out=out[b_ * OWN + s_ * 128: b_ * OWN + (s_ + 1) * 128, :],
                in_=xatt_sb[p_][:]))(stl, bi, sp), wait=[t_add], dma=True, slot=f"ost{sp}")
            t_out_tokens.append(t_st)
            if do_ln2:
                t_y = ln_normalize(xatt_sb[sp][:], y_sb[sp][:], t_add, sp,
                                   extra=[frq["y"][sp]])
                t_ys = sch.op("gpsimd", (lambda s_, p_: lambda g: g.dma_start(
                    out=yspill[s_ * 128:(s_ + 1) * 128, :], in_=y_sb[p_][:]))(stl, sp),
                    wait=[t_y, frq["y"][sp]], dma=True, slot=f"ys{sp}")
                frq["y"][sp] = t_ys
                frq["xatt"][sp] = [t_st, t_y]
                last_tok = t_ys
            else:
                frq["xatt"][sp] = t_st
                last_tok = t_st
        return last_tok, list(t_out_tokens)

    tA = ph2_batch(0)
    t_y_last, t_ph3a_dG = ph3_batch(0, tA, True)
    tB = ph2_batch(1)
    if DEBUG_DUMP:
        sch.op("gpsimd", lambda g: g.dma_start(
            out=dbg["ctx"], in_=ctx_sb[:].rearrange("p t d -> p (t d)")),
            wait=[tB], dma=True, slot="dbgctx")
    _, t_ph3b_dG = ph3_batch(1, tB, False)
    t_out_base = t_ph3a_dG + t_ph3b_dG
    ph2.close()
    bar2 = sch.barrier()

    ph4 = ExitStack()
    e4 = ph4.enter_context
    gidx_sb = e4(nc.sbuf_tensor("gidx_sb", [128, E * G], U32))
    cntt_sb = e4(nc.sbuf_tensor("cntt_sb", [128, E * G], F32))
    cwg_sb = e4(nc.sbuf_tensor("cwg_sb", [128, E], F32))
    m_sb = e4(nc.sbuf_tensor("m_sb", [128, 2, G], F32))
    gy_sb = [e4(nc.sbuf_tensor(f"gy_sb{i}", [128, G, D], F32)) for i in range(2)]
    gyT_sb = [e4(nc.sbuf_tensor(f"gyT_sb{i}", [128, 2, 8, G * 128], BF16)) for i in range(2)]
    gT_sb = e4(nc.sbuf_tensor("gT_sb", [128, 2, 32, G * 128], BF16))
    w1s_sb = e4(nc.sbuf_tensor("w1s_sb", [128, 8, 512], F32))
    w1_sb = [e4(nc.sbuf_tensor(f"w1_sb{i}", [128, 2, 8, 512], BF16)) for i in range(2)]
    w2s_sb = [e4(nc.sbuf_tensor(f"w2s_sb{i}", [128, 2, 512], F32)) for i in range(2)]
    w2_sb = [e4(nc.sbuf_tensor(f"w2_sb{i}", [128, 2, 2, 512], BF16)) for i in range(2)]
    b1_sb = [e4(nc.sbuf_tensor("b1_sb0", [1, FF], F32))] * 2
    b2_sb = [e4(nc.sbuf_tensor("b2_sb0", [1, D], F32))] * 2
    gt_sb = e4(nc.sbuf_tensor("gt_sb", [128, 512], F32))
    sc_sb = [e4(nc.sbuf_tensor(f"sc_sb{i}", [128, D], F32)) for i in range(G)]

    t_gi = sch.op("sync", lambda g: g.dma_start(out=gidx_sb[:], in_=gidx),
                  wait=bar2, dma=True, slot="gidx")
    t_cn = sch.op("sync", lambda g: g.dma_start(out=cntt_sb[:], in_=cntt), dma=True, slot="cntt")

    frf = {"w1": [None, None], "w2": [None, None], "gy": [[None], [None]],
           "gyT": [None, None], "gT": None, "b": [None, None],
           "sc": [None] * G, "cwg": None, "m": [[None] * G, [None] * G]}
    t_scatters = []
    t_gath_e = [None] * (E + 1)
    t_m_e = [None] * (E + 1)

    def emit_gathers(ee):
        par = ee % 2
        t_gath = []
        t_m = []
        for gg in range(G):
            t_g = sch.op("gpsimd", (lambda e_, g_, p_: lambda g: g.indirect_dma_start(
                out=gy_sb[p_][:, g_, :], out_offset=None, in_=yspill,
                in_offset=IndirectOffsetOnAxis(gidx_sb[:, e_ * G + g_:e_ * G + g_ + 1], 0)))(ee, gg, par),
                wait=[t_gi, t_y_last] + frf["gy"][par] + bar2, dma=True, slot=f"gy{par}{gg}")
            t_gath.append(t_g)
            t_w = sch.op("gpsimd", (lambda e_, g_: lambda g: g.indirect_dma_start(
                out=cwg_sb[:], out_offset=None, in_=cw,
                in_offset=IndirectOffsetOnAxis(gidx_sb[:, e_ * G + g_:e_ * G + g_ + 1], 0)))(ee, gg),
                wait=[frf["cwg"]], dma=True, slot="cwg")
            t_m.append(sch.op("vector", (lambda e_, g_, p_: lambda g: g.tensor_tensor(
                out=m_sb[:, p_, g_:g_ + 1], in0=cntt_sb[:, e_ * G + g_:e_ * G + g_ + 1],
                in1=cwg_sb[:, e_:e_ + 1], op=ALU.mult))(ee, gg, par),
                wait=[t_w, t_cn, frf["m"][par][gg]]))
            frf["cwg"] = t_m[-1]
        t_gath_e[ee] = t_gath
        t_m_e[ee] = t_m

    def emit_gyT(ee):
        par = ee % 2
        t_gath = t_gath_e[ee]
        t_gyT = None
        gy_cons = []
        for gg in range(G):
            for half in range(2):
                tbk = 6 + half
                tr_tok = None
                for j in range(4):
                    dc = half * 4 + j
                    w = [t_gath[gg], bank_cons[tbk]] if j == 0 else [t_gath[gg]]
                    tr_tok = sch.op("tensor", (lambda g_, d_, b_, j_, p_: lambda g: g.transpose(
                        ps[b_][:, j_ * 128:(j_ + 1) * 128],
                        gy_sb[p_][:, g_, d_ * 128:(d_ + 1) * 128], ident_sb[:]))(gg, dc, tbk, j, par),
                        wait=w)
                gy_cons.append(tr_tok)
                t_hi = sch.op("scalar", (lambda g_, h_, b_, p_: lambda g: g.activation(
                    out=gyT_sb[p_][:, 0, h_ * 4:(h_ + 1) * 4, g_ * 128:(g_ + 1) * 128],
                    in_=ps[b_][:].rearrange("p (c t) -> p c t", c=4),
                    func=AF.Identity, scale=1.0))(gg, half, tbk, par),
                    wait=[tr_tok, frf["gyT"][par]])
                t_gyT = sch.op("vector", (lambda g_, h_, b_, p_: lambda g: g.tensor_tensor(
                    out=gyT_sb[p_][:, 1, h_ * 4:(h_ + 1) * 4, g_ * 128:(g_ + 1) * 128],
                    in0=ps[b_][:].rearrange("p (c t) -> p c t", c=4),
                    in1=gyT_sb[p_][:, 0, h_ * 4:(h_ + 1) * 4, g_ * 128:(g_ + 1) * 128],
                    op=ALU.subtract))(gg, half, tbk, par),
                    wait=[t_hi])
                bank_cons[tbk] = t_gyT
        frf["gy"][par] = gy_cons
        return t_gyT

    emit_gathers(0)
    t_gyT_par = [None, None]
    t_gyT_par[0] = emit_gyT(0)

    for ee in range(E):
        par = ee % 2
        t_gyT = t_gyT_par[par]
        t_m = t_m_e[ee]
        t_b1 = sch.op("sync", (lambda e_: lambda g: g.dma_start(
            out=b1_sb[0][:], in_=b1f[e_:e_ + 1, :]))(ee),
            wait=[frf.get("b1last")] + bar2, dma=True, slot="b1")
        t_b2 = sch.op("sync", (lambda e_: lambda g: g.dma_start(
            out=b2_sb[0][:], in_=b2f[e_:e_ + 1, :]))(ee),
            wait=[frf.get("b2last")], dma=True, slot="b2")
        if ee + 1 < E:
            emit_gathers(ee + 1)

        t_gT = None
        for fc in range(8):
            wpar = fc % 2
            t_ws = sch.op("sync", (lambda e_, f_: lambda g: g.dma_start(
                out=w1s_sb[:],
                in_=w1t[e_, f_].rearrange("p (d c) -> p d c", d=8)))(ee, fc),
                wait=[frf.get("w1s")], dma=True, slot="w1s")
            t_wh = sch.op("scalar", (lambda p_: lambda g: g.activation(
                out=w1_sb[p_][:, 0], in_=w1s_sb[:], func=AF.Identity, scale=1.0))(wpar),
                wait=[t_ws, frf["w1"][wpar]])
            t_w1 = sch.op("vector", (lambda p_: lambda g: g.tensor_tensor(
                out=w1_sb[p_][:, 1], in0=w1s_sb[:], in1=w1_sb[p_][:, 0],
                op=ALU.subtract))(wpar), wait=[t_wh])
            frf["w1s"] = t_w1
            for tg in range(G):
                pb = tg
                t_bias = sch.op("tensor", (lambda f_, b_, p_: lambda g: g.matmul(
                    ps[b_][:], ones_sb[:], b1_sb[p_][:, f_ * 512:(f_ + 1) * 512],
                    start=True, stop=False))(fc, pb, par),
                    wait=[t_b1, t_ones, bank_cons[pb]])
                frf["b1last"] = t_bias
                last_mm = None
                for dc in range(8):
                    w = [t_w1, t_gyT] if dc == 0 else []
                    for gi, wi in ((0, 0), (1, 0), (0, 1)):
                        last_mm = sch.op("tensor", (lambda t_, d_, b_, p_, ep_, gi_, wi_: lambda g: g.matmul(
                            ps[b_][:], gyT_sb[ep_][:, gi_, d_, t_ * 128:(t_ + 1) * 128],
                            w1_sb[p_][:, wi_, d_, :],
                            start=False, stop=(d_ == 7 and gi_ == 0 and wi_ == 1)))(
                                tg, dc, pb, wpar, par, gi, wi),
                            wait=w if (gi == 0 and wi == 0) else ())
                frf["w1"][wpar] = last_mm
                t_ge = sch.op("scalar", (lambda b_: lambda g: g.activation(
                    out=gt_sb[:], in_=ps[b_][:], func=AF.Gelu_apprx_tanh))(pb),
                    wait=[last_mm, frf.get("gt")])
                bank_cons[pb] = t_ge
                tr_tok = None
                for j in range(4):
                    w = [t_ge, bank_cons[5]] if j == 0 else [t_ge]
                    tr_tok = sch.op("tensor", (lambda j_: lambda g: g.transpose(
                        ps[5][:, j_ * 128:(j_ + 1) * 128],
                        gt_sb[:, j_ * 128:(j_ + 1) * 128], ident_sb[:]))(j),
                        wait=w)
                frf["gt"] = tr_tok
                t_hi2 = sch.op("scalar", (lambda f_, t_: lambda g: g.activation(
                    out=gT_sb[:, 0, f_ * 4:(f_ + 1) * 4, t_ * 128:(t_ + 1) * 128],
                    in_=ps[5][:].rearrange("p (c t) -> p c t", c=4),
                    func=AF.Identity, scale=1.0))(fc, tg),
                    wait=[tr_tok, frf["gT"]])
                t_gT = sch.op("vector", (lambda f_, t_: lambda g: g.tensor_tensor(
                    out=gT_sb[:, 1, f_ * 4:(f_ + 1) * 4, t_ * 128:(t_ + 1) * 128],
                    in0=ps[5][:].rearrange("p (c t) -> p c t", c=4),
                    in1=gT_sb[:, 0, f_ * 4:(f_ + 1) * 4, t_ * 128:(t_ + 1) * 128],
                    op=ALU.subtract))(fc, tg),
                    wait=[t_hi2])
                bank_cons[5] = t_gT
        frf["gyT"][par] = t_gT

        if ee + 1 < E:
            t_gyT_par[(ee + 1) % 2] = emit_gyT(ee + 1)

        t_ev = None
        for oc in range(2):
            mm_done = [None] * G
            for fp_ in range(16):
                wpar = fp_ % 2
                t_ws2 = sch.op("sync", (lambda e_, o_, f_, p_: lambda g: g.dma_start(
                    out=w2s_sb[p_][:],
                    in_=w2t[e_, o_, 2 * f_:2 * f_ + 2].rearrange("a p c -> p a c")))(ee, oc, fp_, wpar),
                    wait=[frf.get(f"w2s{wpar}")], dma=True, slot=f"w2s{wpar}")
                t_wh2 = sch.op("scalar", (lambda p_: lambda g: g.activation(
                    out=w2_sb[p_][:, 0], in_=w2s_sb[p_][:], func=AF.Identity, scale=1.0))(wpar),
                    wait=[t_ws2, frf["w2"][wpar]])
                t_w2 = sch.op("vector", (lambda p_: lambda g: g.tensor_tensor(
                    out=w2_sb[p_][:, 1], in0=w2s_sb[p_][:], in1=w2_sb[p_][:, 0],
                    op=ALU.subtract))(wpar), wait=[t_wh2])
                frf[f"w2s{wpar}"] = t_w2
                for sub in range(2):
                    fc2 = 2 * fp_ + sub
                    for tg in range(G):
                        pb = tg
                        if fc2 == 0:
                            frf["b2last"] = sch.op("tensor", (lambda o_, b_, p_: lambda g: g.matmul(
                                ps[b_][:], ones_sb[:], b2_sb[p_][:, o_ * 512:(o_ + 1) * 512],
                                start=True, stop=False))(oc, pb, par),
                                wait=[t_b2, t_ones, bank_cons[pb]])
                        for gi, wi in ((0, 0), (1, 0), (0, 1)):
                            mm_done[tg] = sch.op("tensor", (lambda f_, t_, b_, p_, gi_, wi_, s_: lambda g: g.matmul(
                                ps[b_][:], gT_sb[:, gi_, f_, t_ * 128:(t_ + 1) * 128],
                                w2_sb[p_][:, wi_, s_, :],
                                start=False, stop=(f_ == 31 and gi_ == 0 and wi_ == 1)))(
                                    fc2, tg, pb, wpar, gi, wi, sub),
                                wait=([t_w2, t_gT] if fc2 == 0 else [t_w2]) if (gi == 0 and wi == 0 and sub == 0) else ())
                    frf["w2"][wpar] = mm_done[G - 1]
            for tg in range(G):
                pb = tg
                t_ev = sch.op("vector", (lambda t_, o_, b_, p_: lambda g: g.tensor_scalar(
                    out=sc_sb[t_][:, o_ * 512:(o_ + 1) * 512], in0=ps[b_][:],
                    scalar1=m_sb[:, p_, t_:t_ + 1], scalar2=None, op0=ALU.mult))(tg, oc, pb, par),
                    wait=[mm_done[tg], t_m[tg], frf["sc"][tg]])
                bank_cons[pb] = t_ev
                frf["m"][par][tg] = t_ev
                if oc == 1:
                    t_sc = sch.op("gpsimd", (lambda e_, t_: lambda g: g.indirect_dma_start(
                        out=out, out_offset=IndirectOffsetOnAxis(
                            gidx_sb[:, e_ * G + t_:e_ * G + t_ + 1], 0),
                        in_=sc_sb[t_][:], in_offset=None,
                        compute_op=ALU.add))(ee, tg),
                        wait=[t_ev] + t_out_base, dma=True, slot=f"sc{tg}")
                    frf["sc"][tg] = t_sc
                    t_scatters.append(t_sc)
        frf["gT"] = t_ev

    ph4.close()

    final_waits = sch.barrier()
    with st_main:
        block = st_main.enter_context(nc.Block())
        sch.emit_all(block, "gpsimd", final_waits)
    return nc


def _split_hi_lo(a64):
    hi = a64.astype(np.float32)
    lo = (a64 - hi.astype(np.float64)).astype(np.float32)
    return hi, lo


def _rope_tables_f64(qoff):
    pos = np.arange(S, dtype=np.float64)
    inv = np.power(10000.0, -2.0 * np.arange(HD // 2, dtype=np.float64) / HD)
    ang = pos[None, :] * inv[:, None]
    cos64 = np.empty((HD, S), np.float64)
    sin64 = np.empty((HD, S), np.float64)
    cos64[0::2] = np.cos(ang)
    cos64[1::2] = np.cos(ang)
    sin64[0::2] = -np.sin(ang)
    sin64[1::2] = np.sin(ang)
    cos64 = np.roll(cos64, -qoff, axis=1)
    sin64 = np.roll(sin64, -qoff, axis=1)
    cos64 = np.concatenate([cos64, cos64], axis=0)
    sin64 = np.concatenate([sin64, sin64], axis=0)
    return cos64, sin64


def _plan_from_top_idx(TI):
    tok = TI.T
    plans = []
    maxd = 1
    for c in range(NCORES):
        bA = c // 4
        qoff = (c % 4) * OWN
        lo, hi_ = bA * S + qoff, bA * S + qoff + OWN
        per_e = []
        for ee in range(E):
            te = tok[ee]
            sel = te[(te >= lo) & (te < hi_)] - lo
            rows, cnts = np.unique(sel, return_counts=True)
            per_e.append((rows.astype(np.int64), cnts.astype(np.int64)))
            maxd = max(maxd, len(rows))
        plans.append(per_e)
    G = (maxd + 127) // 128
    return plans, G


def kernel(L0_attn_hidden_states, L0_attn_Wq, L0_attn_bq, L0_attn_Wk, L0_attn_bk,
           L0_attn_Wv, L0_attn_bv, L0_ffn_combine_weights, L0_ffn_top_idx,
           L0_ffn_W1, L0_ffn_b1, L0_ffn_W2, L0_ffn_b2):
    X = np.ascontiguousarray(np.asarray(L0_attn_hidden_states, dtype=np.float32))
    Wq = np.asarray(L0_attn_Wq, np.float32); Wk = np.asarray(L0_attn_Wk, np.float32)
    Wv = np.asarray(L0_attn_Wv, np.float32)
    bq = np.asarray(L0_attn_bq, np.float32); bk = np.asarray(L0_attn_bk, np.float32)
    bv = np.asarray(L0_attn_bv, np.float32)
    CWf = np.ascontiguousarray(np.asarray(L0_ffn_combine_weights, np.float32))
    TI = np.asarray(L0_ffn_top_idx).astype(np.int64)
    W1 = np.asarray(L0_ffn_W1, np.float32); b1 = np.asarray(L0_ffn_b1, np.float32)
    W2 = np.asarray(L0_ffn_W2, np.float32); b2 = np.asarray(L0_ffn_b2, np.float32)

    plans, G = _plan_from_top_idx(TI)

    wqt = Wq.reshape(8, 128, HP, 128).transpose(2, 1, 0, 3).reshape(HP, 128, D).copy()
    wkt = Wk.reshape(8, 128, HP, 128).transpose(2, 1, 0, 3).reshape(HP, 128, D).copy()
    wvt = Wv.reshape(8, 128, D).transpose(1, 0, 2).copy()
    bqt = bq.reshape(HP, 128).T.copy(); bkt = bk.reshape(HP, 128).T.copy()
    bvtm = np.zeros((H, 128), np.float32); bvtm[:, 0:64] = bv.reshape(H, 64)
    identm = np.eye(128, dtype=np.float32)
    w1t = np.ascontiguousarray(
        W1.reshape(E, 8, 128, 8, 512).transpose(0, 3, 2, 1, 4)).reshape(E, 8, 128, FF)
    w2t = np.ascontiguousarray(W2.reshape(E, 32, 128, 2, 512).transpose(0, 3, 1, 2, 4))

    in_maps = []
    for c in range(NCORES):
        bA = c // 4
        bB = 2 + c // 4
        qoff = (c % 4) * OWN
        xa = np.ascontiguousarray(X[bA, qoff:qoff + OWN])
        xb = np.ascontiguousarray(X[bB, qoff:qoff + OWN])
        cos64, sin64 = _rope_tables_f64(qoff)
        rc_hi, rc_lo = _split_hi_lo(cos64)
        rs_hi, rs_lo = _split_hi_lo(sin64)
        cwc = np.ascontiguousarray(CWf[bA * S + qoff: bA * S + qoff + OWN, :])
        gidxm = np.zeros((E * G, 128), np.uint32)
        cnttm = np.zeros((E * G, 128), np.float32)
        for ee in range(E):
            rows, cnts = plans[c][ee]
            n = len(rows)
            used = np.zeros(OWN, bool)
            used[rows] = True
            padrows = np.nonzero(~used)[0][: G * 128 - n]
            allrows = np.concatenate([rows, padrows])
            allcnt = np.concatenate([cnts, np.zeros(len(padrows), np.int64)])
            gidxm[ee * G:(ee + 1) * G, :] = allrows.reshape(G, 128)
            cnttm[ee * G:(ee + 1) * G, :] = allcnt.reshape(G, 128)
        in_maps.append(dict(
            xa=xa, xb=xb, wqt=wqt, wkt=wkt, wvt=wvt, bqt=bqt, bkt=bkt, bvt=bvtm,
            rc_hi=rc_hi, rc_lo=rc_lo, rs_hi=rs_hi, rs_lo=rs_lo, ident=identm,
            cw=cwc, cntt=np.ascontiguousarray(cnttm.T), gidx=np.ascontiguousarray(gidxm.T),
            b1f=b1, b2f=b2, w1t=w1t, w2t=w2t))

    nc = build_program(G)
    res = run_bass_kernel_spmd(nc, in_maps, core_ids=list(range(NCORES)))

    result = np.empty((B, S, D), np.float32)
    for c in range(NCORES):
        o = res.results[c]["out"]
        bA = c // 4
        bB = 2 + c // 4
        qoff = (c % 4) * OWN
        result[bA, qoff:qoff + OWN] = o[0:OWN]
        result[bB, qoff:qoff + OWN] = o[OWN:2 * OWN]
    return result


# version 15
# speedup vs baseline: 1.2654x; 1.0549x over previous
import os
import numpy as np
import concourse.bass as bass
import concourse.mybir as mybir
from concourse.bass import IndirectOffsetOnAxis
from concourse.bass_utils import run_bass_kernel_spmd
from contextlib import ExitStack

F32 = mybir.dt.float32
BF16 = mybir.dt.bfloat16
U32 = mybir.dt.uint32
AF = mybir.ActivationFunctionType
ALU = mybir.AluOpType

B, S, D = 4, 4096, 1024
H, HD = 16, 64
HP = H // 2
E, FF = 8, 4096
OWN = 1024
NCORES = 8
EPS = 1e-5
CSHIFT = 12.0
SM_SCALE = 0.125
PAIR_MASK = [i ^ 1 for i in range(32)]
SAFE_SYNC = os.environ.get("KERNEL_SAFE_SYNC") == "1"
DEBUG_DUMP = os.environ.get("KERNEL_DEBUG_DUMP") == "1"


class Sched:

    ENG_SEMS = {"tensor": "tT", "vector": "tV", "scalar": "tA", "gpsimd": "tG"}

    def __init__(self, nc, stack):
        self.nc = nc
        self.stack = stack
        self.ops = {k: [] for k in ("sync", "scalar", "vector", "tensor", "gpsimd")}
        self.sems = {}
        self.count = {}
        for s in ("tT", "tV", "tA", "tG"):
            self.sems[s] = stack.enter_context(nc.semaphore(s))
            self.count[s] = 0
        self.hwm = {k: {} for k in self.ops}

    def op(self, eng, emit, wait=(), dma=False, slot=None):
        ownsem = self.ENG_SEMS.get(eng)
        guard = SAFE_SYNC or eng in ("vector", "gpsimd")
        selfwait = ((ownsem, self.count[ownsem])
                    if (guard and ownsem and self.count[ownsem] > 0) else None)
        if dma:
            assert slot is not None, "DMA ops need a slot name"
            sem = "D_" + slot
            if sem not in self.sems:
                self.sems[sem] = self.stack.enter_context(self.nc.semaphore(sem))
                self.count[sem] = 0
        else:
            sem = self.ENG_SEMS[eng]
        self.count[sem] += 16 if dma else 1
        tok = (sem, self.count[sem])

        def _flat(ws):
            for w in ws:
                if w is None:
                    continue
                if isinstance(w, list):
                    yield from _flat(w)
                else:
                    yield w
        waits = []
        for (s, v) in _flat([selfwait] + list(wait)):
            if self.hwm[eng].get(s, 0) >= v:
                continue
            self.hwm[eng][s] = v
            waits.append((s, v))
        self.ops[eng].append((waits, emit, tok, 16 if dma else 1))
        return tok

    def barrier(self):
        return [(s, c) for s, c in self.count.items() if c > 0]

    def emit_all(self, block, final_eng, final_waits):
        sems = self.sems

        def mk(engname):
            oplist = self.ops[engname]
            extra = final_waits if engname == final_eng else ()

            def emitfn(g):
                for waits, emit, tok, amt in oplist:
                    for (s, v) in waits:
                        g.wait_ge(sems[s], v)
                    emit(g).then_inc(sems[tok[0]], amt)
                for (s, v) in extra:
                    g.wait_ge(sems[s], v)
            return emitfn

        block.sync(mk("sync"))
        block.scalar(mk("scalar"))
        block.vector(mk("vector"))
        block.tensor(mk("tensor"))
        block.gpsimd(mk("gpsimd"))


def build_program(G):
    assert 1 <= G <= 4
    nc = bass.Bass("TRN2", target_bir_lowering=False, debug=False,
                   detect_race_conditions=SAFE_SYNC)

    def I(name, shape, dt=F32):
        return nc.dram_tensor(name, list(shape), dt, kind="ExternalInput").ap()

    xin = [I("xa", (OWN, D)), I("xb", (OWN, D))]
    wqt = I("wqt", (HP, 128, D)); wkt = I("wkt", (HP, 128, D))
    wvt = I("wvt", (128, 8, D))
    bqt = I("bqt", (128, HP)); bkt = I("bkt", (128, HP))
    bvt = I("bvt", (H, 128))
    rc_hi = I("rc_hi", (128, S)); rc_lo = I("rc_lo", (128, S))
    rs_hi = I("rs_hi", (128, S)); rs_lo = I("rs_lo", (128, S))
    ident = I("ident", (128, 128))
    cw = I("cw", (OWN, E))
    cntt = I("cntt", (128, E * G))
    gidx = I("gidx", (128, E * G), U32)
    b1t = I("b1t", (E, 128, 32)); b2f = I("b2f", (E, D))
    w1t = I("w1t", (E, 8, 128, FF))
    w2t = I("w2t", (E, 2, 32, 128, 512))

    out = nc.dram_tensor("out", [2 * OWN, D], F32, kind="ExternalOutput").ap()
    dbg = {}
    if DEBUG_DUMP:
        dbg["ctx"] = nc.dram_tensor("dbg_ctx", [128, 8 * D], F32, kind="ExternalOutput").ap()
        dbg["p"] = nc.dram_tensor("dbg_p", [128, 512], F32, kind="ExternalOutput").ap()
        dbg["ctxT"] = nc.dram_tensor("dbg_ctxT", [65, 512], F32, kind="ExternalOutput").ap()
        dbg["dn"] = nc.dram_tensor("dbg_dn", [32, 512], F32, kind="ExternalOutput").ap()
        dbg["kh"] = nc.dram_tensor("dbg_kh", [128, S], F32, kind="ExternalOutput").ap()
        dbg["qh"] = nc.dram_tensor("dbg_qh", [128, OWN], F32, kind="ExternalOutput").ap()
        dbg["xln"] = nc.dram_tensor("dbg_xln", [128, 4 * D], F32, kind="ExternalOutput").ap()
        dbg["xlnT"] = nc.dram_tensor("dbg_xlnT", [128, 8 * 512], F32, kind="ExternalOutput").ap()
        dbg["kev"] = nc.dram_tensor("dbg_kev", [128, 512], F32, kind="ExternalOutput").ap()
        dbg["krp"] = nc.dram_tensor("dbg_krp", [128, 512], F32, kind="ExternalOutput").ap()

    kcon = nc.dram_tensor("kcon", [2, H * 128, OWN], BF16, kind="Internal").ap()
    qspill = nc.dram_tensor("qspill", [2, H * 128, OWN], BF16, kind="Internal").ap()
    vcon = nc.dram_tensor("vcon", [2, OWN, D], F32, kind="Internal").ap()
    kall = [nc.dram_tensor(f"kall{b}", [4 * H * 128, OWN], BF16, kind="Internal").ap()
            for b in range(2)]
    vall = [nc.dram_tensor(f"vall{b}", [4 * OWN, D], F32, kind="Internal").ap()
            for b in range(2)]
    yspill = nc.dram_tensor("yspill", [OWN, D], F32, kind="Internal").ap()

    st_main = ExitStack()
    e = st_main.enter_context
    sch = Sched(nc, st_main)

    ident_sb = e(nc.sbuf_tensor("ident_sb", [128, 128], F32))
    bq_sb = e(nc.sbuf_tensor("bq_sb", [128, HP], F32))
    bk_sb = e(nc.sbuf_tensor("bk_sb", [128, HP], F32))
    ones_sb = e(nc.sbuf_tensor("ones_sb", [1, 128], F32))
    eps_sb = e(nc.sbuf_tensor("eps_sb", [128, 1], F32))
    negc_sb = e(nc.sbuf_tensor("negc_sb", [128, 1], F32))
    stats_sb = e(nc.sbuf_tensor("stats_sb", [128, 12], F32))
    mv_sb = e(nc.sbuf_tensor("mv_sb", [128, 2], F32))
    sd_sb = e(nc.sbuf_tensor("sd_sb", [128, 1], F32))
    rstd_sb = e(nc.sbuf_tensor("rstd_sb", [128, 2], F32))
    nb_sb = e(nc.sbuf_tensor("nb_sb", [128, 2], F32))

    psS = [e(nc.psum_tensor(f"psS{i}", [128, 1024], F32)) for i in range(2)]
    psC = [e(nc.psum_tensor(f"psC{i}", [128, 512], F32)) for i in range(4)]
    ps = [psS[0][:, 0:512], psS[0][:, 512:1024],
          psS[1][:, 0:512], psS[1][:, 512:1024],
          psC[0][:], psC[1][:], psC[2][:], psC[3][:]]
    bank_cons = [None] * 8

    sch.op("vector", lambda g: g.memset(eps_sb[:], EPS))
    sch.op("vector", lambda g: g.memset(negc_sb[:], -CSHIFT))
    t_ones = sch.op("vector", lambda g: g.memset(ones_sb[:], 1.0))
    t_id = sch.op("sync", lambda g: g.dma_start(out=ident_sb[:], in_=ident), dma=True, slot="ident")
    t_bq = sch.op("sync", lambda g: g.dma_start(out=bq_sb[:], in_=bqt), dma=True, slot="bq")
    t_bk = sch.op("sync", lambda g: g.dma_start(out=bk_sb[:], in_=bkt), dma=True, slot="bk")

    ln_free = [None, None]

    def ln_normalize(src_ap, dst_ap, t_src, par, extra=()):
        sch.op("vector", lambda g: g.bn_stats(out=stats_sb[:, 0:6], in_=src_ap[:, 0:512]),
               wait=[t_src])
        sch.op("vector", lambda g: g.bn_stats(out=stats_sb[:, 6:12], in_=src_ap[:, 512:1024]))
        t_ag = sch.op("vector", lambda g: g.bn_aggr(out=mv_sb[:], in_=stats_sb[:]))
        t_sd = sch.op("scalar", lambda g: g.activation(
            out=sd_sb[:], in_=mv_sb[:, 1:2], func=AF.Sqrt, bias=eps_sb[:], scale=1.0),
            wait=[t_ag])
        sch.op("vector", (lambda p: lambda g: g.reciprocal(
            out=rstd_sb[:, p:p + 1], in_=sd_sb[:]))(par), wait=[t_sd, ln_free[par]])
        t_nb = sch.op("vector", (lambda p: lambda g: g.scalar_tensor_tensor(
            out=nb_sb[:, p:p + 1], in0=mv_sb[:, 0:1], scalar=-1.0,
            in1=rstd_sb[:, p:p + 1], op0=ALU.mult, op1=ALU.mult))(par))
        t_act = sch.op("scalar", (lambda p: lambda g: g.activation(
            out=dst_ap, in_=src_ap, func=AF.Identity,
            bias=nb_sb[:, p:p + 1], scale=rstd_sb[:, p:p + 1]))(par),
            wait=[t_nb] + list(extra))
        ln_free[par] = t_act
        return t_act

    ph1 = ExitStack()
    e1 = ph1.enter_context
    wq_sb = [e1(nc.sbuf_tensor("wq_sb0", [128, 8, 128], F32))] * 2
    wk_sb = [e1(nc.sbuf_tensor("wk_sb0", [128, 8, 128], F32))] * 2
    wqh_sb = [e1(nc.sbuf_tensor(f"wqh_sb{i}", [128, 2, 8, 128], BF16)) for i in range(2)]
    wkh_sb = [e1(nc.sbuf_tensor(f"wkh_sb{i}", [128, 2, 8, 128], BF16)) for i in range(2)]
    wvh_sb = e1(nc.sbuf_tensor("wvh_sb", [128, 2, 8, D], BF16))
    ropec_sb = e1(nc.sbuf_tensor("ropec_sb", [128, S], F32))
    ropes_sb = e1(nc.sbuf_tensor("ropes_sb", [128, S], F32))
    xt_sb = [e1(nc.sbuf_tensor(f"xt_sb{i}", [128, D], F32)) for i in range(2)]
    xln_sb = e1(nc.sbuf_tensor("xln_sb", [128, 4, D], F32))
    rtmp_sb = xln_sb[:].rearrange("p a b -> p (a b)")
    xlnTh_sb2 = [e1(nc.sbuf_tensor(f"xlnTh_sb{i}", [128, 2, 8, 512], BF16)) for i in range(2)]
    kev_sb = e1(nc.sbuf_tensor("kev_sb", [128, 512], F32))
    ksh_sb = e1(nc.sbuf_tensor("ksh_sb", [128, 512], F32))
    kt1_sb = e1(nc.sbuf_tensor("kt1_sb", [128, 512], F32))
    kt2_sb = e1(nc.sbuf_tensor("kt2_sb", [128, 512], F32))
    krp_sb = [e1(nc.sbuf_tensor(f"krp_sb{i}", [128, 512], F32)) for i in range(2)]
    hilo_sb = [e1(nc.sbuf_tensor(f"hilo_sb{i}", [128, 2, 512], BF16)) for i in range(2)]
    vev_sb = [e1(nc.sbuf_tensor(f"vev_sb{i}", [128, 512], F32)) for i in range(2)]

    t_wv = None
    for ch in range(2):
        t_l = sch.op("sync", (lambda c_: lambda g: g.dma_start(
            out=xln_sb[:], in_=wvt[:, c_ * 4:(c_ + 1) * 4, :]))(ch),
            wait=[t_wv], dma=True, slot="wv")
        sch.op("vector", (lambda c_: lambda g: g.tensor_copy(
            out=wvh_sb[:, 0, c_ * 4:(c_ + 1) * 4, :], in_=xln_sb[:]))(ch), wait=[t_l])
        t_wv = sch.op("vector", (lambda c_: lambda g: g.tensor_tensor(
            out=wvh_sb[:, 1, c_ * 4:(c_ + 1) * 4, :], in0=xln_sb[:],
            in1=wvh_sb[:, 0, c_ * 4:(c_ + 1) * 4, :], op=ALU.subtract))(ch))
    ta = sch.op("sync", lambda g: g.dma_start(out=ropec_sb[:], in_=rc_hi), dma=True, slot="rc")
    tb = sch.op("sync", lambda g: g.dma_start(out=rtmp_sb, in_=rc_lo), dma=True, slot="rtmp",
                wait=[t_wv])
    t_rc = sch.op("vector", lambda g: g.tensor_tensor(
        out=ropec_sb[:], in0=ropec_sb[:], in1=rtmp_sb, op=ALU.add), wait=[ta, tb])
    ta = sch.op("sync", lambda g: g.dma_start(out=ropes_sb[:], in_=rs_hi), dma=True, slot="rs")
    tb = sch.op("sync", lambda g: g.dma_start(out=rtmp_sb, in_=rs_lo), dma=True,
                wait=[t_rc], slot="rtmp")
    t_rs = sch.op("vector", lambda g: g.tensor_tensor(
        out=ropes_sb[:], in0=ropes_sb[:], in1=rtmp_sb, op=ALU.add), wait=[ta, tb])

    fr = {"xt": [None, None], "krp": [None, None], "vev": [None, None],
          "wq": [None, None], "wk": [None, None], "xlnT": [None, None],
          "xln": [None] * 4}

    spill_toks = [[], []]

    def ph1_batch(bi):
        xsrc = xin[bi]
        for tt in range(2):
            xlnTh_sb = xlnTh_sb2[tt % 2]
            last_ev = None
            for stl in range(4):
                sp = stl % 2
                rows = tt * 512 + stl * 128
                t_ld = sch.op("sync", (lambda sp_, r_, x_: lambda g: g.dma_start(
                    out=xt_sb[sp_][:], in_=x_[r_:r_ + 128, :]))(sp, rows, xsrc),
                    wait=[fr["xt"][sp]], dma=True, slot=f"xt{sp}")
                t_act = ln_normalize(xt_sb[sp][:], xln_sb[:, stl, :], t_ld, sp,
                                     extra=[fr["xln"][stl], t_wv, t_rs])
                fr["xt"][sp] = t_act
                for half in range(2):
                    tbk = 0 + half
                    tr_tok = None
                    for j in range(4):
                        dc = half * 4 + j
                        w = [t_act, t_id]
                        if j == 0:
                            w.append(bank_cons[tbk])
                        tr_tok = sch.op("tensor", (lambda s_, d_, b_, j_: lambda g: g.transpose(
                            ps[b_][:, j_ * 128:(j_ + 1) * 128],
                            xln_sb[:, s_, d_ * 128:(d_ + 1) * 128], ident_sb[:]))(stl, dc, tbk, j),
                            wait=w)
                    fr["xln"][stl] = tr_tok
                    t_hi = sch.op("scalar", (lambda s_, h_, b_: lambda g: g.activation(
                        out=xlnTh_sb[:, 0, h_ * 4:(h_ + 1) * 4, s_ * 128:(s_ + 1) * 128],
                        in_=ps[b_][:].rearrange("p (c t) -> p c t", c=4),
                        func=AF.Identity, scale=1.0))(stl, half, tbk),
                        wait=[tr_tok, fr["xlnT"][tt % 2]])
                    last_ev = sch.op("vector", (lambda s_, h_, b_: lambda g: g.tensor_tensor(
                        out=xlnTh_sb[:, 1, h_ * 4:(h_ + 1) * 4, s_ * 128:(s_ + 1) * 128],
                        in0=ps[b_][:].rearrange("p (c t) -> p c t", c=4),
                        in1=xlnTh_sb[:, 0, h_ * 4:(h_ + 1) * 4, s_ * 128:(s_ + 1) * 128],
                        op=ALU.subtract))(stl, half, tbk), wait=[t_hi])
                    bank_cons[tbk] = last_ev
            t_xlnT = last_ev
            if DEBUG_DUMP and bi == 1 and tt == 0:
                td = sch.op("gpsimd", lambda g: g.dma_start(
                    out=dbg["xln"], in_=xln_sb[:].rearrange("p a b -> p (a b)")),
                    wait=[t_xlnT], dma=True, slot="dbgxln")
                td2 = sch.op("gpsimd", lambda g: g.dma_start(
                    out=dbg["xlnT"], in_=xlnT_sb[:].rearrange("p a b -> p (a b)")),
                    wait=[t_xlnT], dma=True, slot="dbgxlnT")
                t_xlnT = [t_xlnT, td, td2]
            last_proj_mm = None

            for which in ("k", "q"):
                ws_ = wk_sb if which == "k" else wq_sb
                wsrc = wkt if which == "k" else wqt
                bsb = bk_sb if which == "k" else bq_sb
                tbias = t_bk if which == "k" else t_bq
                spill = kcon if which == "k" else qspill
                for hp_i in range(HP):
                    par = hp_i % 2
                    t_w = sch.op("sync", (lambda p_, h_, s_, d_: lambda g: g.dma_start(
                        out=d_[p_][:], in_=s_[h_]))(par, hp_i, wsrc, ws_),
                        wait=[fr["w" + which][par], fr.get("wslab" + which)],
                        dma=True, slot=f"w{which}{par}")
                    wh_ = wkh_sb if which == "k" else wqh_sb
                    sch.op("gpsimd", (lambda p_, wl_, wh2_: lambda g: g.tensor_copy(
                        out=wh2_[p_][:, 0], in_=wl_[p_][:]))(par, ws_, wh_), wait=[t_w])
                    t_w = sch.op("gpsimd", (lambda p_, wl_, wh2_: lambda g: g.tensor_tensor(
                        out=wh2_[p_][:, 1], in0=wl_[p_][:], in1=wh2_[p_][:, 0],
                        op=ALU.subtract))(par, ws_, wh_))
                    fr["wslab" + which] = t_w
                    pb = 2 + par
                    mm_tok = None
                    for dc in range(8):
                        w = [t_w, t_xlnT, bank_cons[pb]] if dc == 0 else []
                        for wi, xi in ((0, 0), (1, 0), (0, 1)):
                            mm_tok = sch.op("tensor", (lambda p_, d_, b_, wh2_, wi_, xi_, f_: lambda g: g.matmul(
                                ps[b_][:], wh2_[p_][:, wi_, d_, :],
                                xlnTh_sb[:, xi_, d_, :],
                                start=(d_ == 0 and f_), stop=(d_ == 7 and wi_ == 0 and xi_ == 1)))(
                                    par, dc, pb, wh_, wi, xi, (wi == 0 and xi == 0)),
                                wait=w if (wi == 0 and xi == 0) else ())
                    fr["w" + which][par] = mm_tok
                    last_proj_mm = mm_tok
                    t_ev = sch.op("scalar", (lambda h_, b_, bs_: lambda g: g.activation(
                        out=kev_sb[:], in_=ps[b_][:], func=AF.Identity,
                        bias=bs_[:, h_:h_ + 1], scale=1.0))(hp_i, pb, bsb),
                        wait=[mm_tok, tbias, fr.get("kev")])
                    bank_cons[pb] = t_ev
                    if DEBUG_DUMP and bi == 1 and tt == 0 and which == "k" and hp_i == 0:
                        tdk = sch.op("gpsimd", lambda g: g.dma_start(
                            out=dbg["kev"], in_=kev_sb[:]), wait=[t_ev], dma=True, slot="dbgkev")
                        bank_cons[pb] = [t_ev, tdk]
                    sch.op("vector", lambda g: g.stream_shuffle(
                        out=ksh_sb[:], in_=kev_sb[:], mask=PAIR_MASK), wait=[t_ev])
                    sch.op("vector", (lambda tt_: lambda g: g.tensor_tensor(
                        out=kt1_sb[:], in0=kev_sb[:],
                        in1=ropec_sb[:, tt_ * 512:(tt_ + 1) * 512], op=ALU.mult))(tt),
                        wait=[t_rc])
                    fr["kev"] = sch.op("vector", (lambda tt_: lambda g: g.tensor_tensor(
                        out=kt2_sb[:], in0=ksh_sb[:],
                        in1=ropes_sb[:, tt_ * 512:(tt_ + 1) * 512], op=ALU.mult))(tt),
                        wait=[t_rs])
                    kp = hp_i % 2
                    t_rp = sch.op("vector", (lambda kp_: lambda g: g.tensor_tensor(
                        out=krp_sb[kp_][:], in0=kt1_sb[:], in1=kt2_sb[:], op=ALU.add))(kp))
                    sch.op("vector", (lambda kp_: lambda g: g.tensor_copy(
                        out=hilo_sb[kp_][:, 0, :], in_=krp_sb[kp_][:]))(kp),
                        wait=[fr["krp"][kp]])
                    t_rp = sch.op("vector", (lambda kp_: lambda g: g.tensor_tensor(
                        out=hilo_sb[kp_][:, 1, :], in0=krp_sb[kp_][:],
                        in1=hilo_sb[kp_][:, 0, :], op=ALU.subtract))(kp))
                    sps = []
                    for hh in range(2):
                        sps.append(sch.op("scalar", (lambda kp_, h_, tt_, b_, sp_, hh_: lambda g: g.dma_start(
                            out=sp_[b_, (2 * h_ + hh_) * 128:(2 * h_ + hh_ + 1) * 128,
                                    tt_ * 512:(tt_ + 1) * 512]
                                .rearrange("(h p) c -> p h c", h=2),
                            in_=hilo_sb[kp_][hh_ * 64:(hh_ + 1) * 64, :, :]))(kp, hp_i, tt, bi, spill, hh),
                            wait=[t_rp], dma=True, slot=f"krp{kp}{which}{hh}"))
                    if which == "k":
                        spill_toks[bi].extend(sps)
                    fr["krp"][kp] = sps

            for vc in range(2):
                for tc in range(4):
                    pb = 4 + tc % 2
                    mm_tok = None
                    for dc in range(8):
                        w = [t_wv, t_xlnT, bank_cons[pb]] if dc == 0 else []
                        for xi, wi in ((0, 0), (1, 0), (0, 1)):
                            mm_tok = sch.op("tensor", (lambda v_, t_, d_, b_, xi_, wi_, f_: lambda g: g.matmul(
                                ps[b_][:], xlnTh_sb[:, xi_, d_, t_ * 128:(t_ + 1) * 128],
                                wvh_sb[:, wi_, d_, v_ * 512:(v_ + 1) * 512],
                                start=(d_ == 0 and f_), stop=(d_ == 7 and xi_ == 0 and wi_ == 1)))(
                                    vc, tc, dc, pb, xi, wi, (xi == 0 and wi == 0)),
                                wait=w if (xi == 0 and wi == 0) else ())
                    last_proj_mm = mm_tok
                    vp = tc % 2
                    t_ev = sch.op("scalar", (lambda vp_, b_: lambda g: g.activation(
                        out=vev_sb[vp_][:], in_=ps[b_][:], func=AF.Identity,
                        scale=1.0))(vp, pb),
                        wait=[mm_tok, fr["vev"][vp]])
                    bank_cons[pb] = t_ev
                    t_sp = sch.op("gpsimd", (lambda vp_, v_, t_, tt_, b_: lambda g: g.dma_start(
                        out=vcon[b_, tt_ * 512 + t_ * 128: tt_ * 512 + (t_ + 1) * 128,
                                 v_ * 512:(v_ + 1) * 512],
                        in_=vev_sb[vp_][:]))(vp, vc, tc, tt, bi),
                        wait=[t_ev], dma=True, slot=f"vev{vp}")
                    spill_toks[bi].append(t_sp)
                    fr["vev"][vp] = t_sp
            fr["xlnT"][tt % 2] = last_proj_mm

    RG = [[0, 1, 2, 3], [4, 5, 6, 7]]
    t_agk = [None, None]
    t_agv = [None, None]

    def emit_ag(bi):
        t_agk[bi] = sch.op("gpsimd", (lambda b_: lambda g: g.collective_compute(
            "AllGather", ALU.bypass, replica_groups=RG,
            ins=[kcon[b_].opt()], outs=[kall[b_].opt()]))(bi), wait=spill_toks[bi])
        t_agv[bi] = sch.op("gpsimd", (lambda b_: lambda g: g.collective_compute(
            "AllGather", ALU.bypass, replica_groups=RG,
            ins=[vcon[b_].opt()], outs=[vall[b_].opt()]))(bi))

    ph1_batch(0)
    emit_ag(0)
    ph1_batch(1)
    bar1 = sch.barrier()
    emit_ag(1)
    ph1.close()

    ph2 = ExitStack()
    e2 = ph2.enter_context
    ksb = [e2(nc.sbuf_tensor(f"ksb{i}", [128, S], BF16)) for i in range(2)]
    bvt_sb = e2(nc.sbuf_tensor("bvt_sb", [1, H * 128], F32))
    ctx_sb = e2(nc.sbuf_tensor("ctx_sb", [128, 8, D], F32))
    vh_sb = [e2(nc.sbuf_tensor(f"vh_sb{i}", [128, 32, 128], F32)) for i in range(2)]
    qdh_sb = [e2(nc.sbuf_tensor(f"qdh_sb{i}", [128, OWN], BF16)) for i in range(2)]
    qdl_sb = [e2(nc.sbuf_tensor(f"qdl_sb{i}", [128, OWN], BF16)) for i in range(2)]
    pT_sb = [e2(nc.sbuf_tensor(f"pT_sb{i}", [128, 1024], F32)) for i in range(3)]
    ctxT_sb = e2(nc.sbuf_tensor("ctxT_sb", [65, 512], F32))
    bvo_sb = e2(nc.sbuf_tensor("bvo_sb", [65, 512], F32))
    dn_sb = e2(nc.sbuf_tensor("dn_sb", [32, 512], F32))
    rdn_sb = e2(nc.sbuf_tensor("rdn_sb", [128, 1], F32))
    xatt_sb = [e2(nc.sbuf_tensor(f"xatt_sb{i}", [128, D], F32)) for i in range(2)]
    y_sb = [e2(nc.sbuf_tensor(f"y_sb{i}", [128, D], F32)) for i in range(2)]

    t_bv = sch.op("sync", lambda g: g.dma_start(
        out=bvt_sb[:].rearrange("o (h c) -> o h c", h=H), in_=bvt),
        wait=bar1, dma=True, slot="bv")
    tvz = []
    for i in range(2):
        sch.op("vector", (lambda i_: lambda g: g.memset(vh_sb[i_][:], 0.0))(i),
               wait=bar1 if i == 0 else ())
        tvz.append(sch.op("vector", (lambda i_: lambda g: g.memset(
            vh_sb[i_][:, :, 64:65], 1.0))(i)))
    frq = {"k": [None, None], "v": [tvz[0], tvz[1]], "q": [None, None],
           "pT": [None, None, None], "xatt": [None, None], "y": [None, None],
           "ctxT": None, "dn": None}
    t_out_tokens = []

    def ph2_batch(bi):
        for h in range(H):
            par = h % 2
            t_kh = []
            t_vh = []
            for r in range(4):
                t_kh.append(sch.op("sync", (lambda b_, h_, p_, r_: lambda g: g.dma_start(
                    out=ksb[p_][:, r_ * 1024:(r_ + 1) * 1024],
                    in_=kall[b_][r_ * H * 128 + h_ * 128:
                                 r_ * H * 128 + (h_ + 1) * 128, :]))(bi, h, par, r),
                    wait=[frq["k"][par], t_agk[bi]] + bar1, dma=True, slot=f"kh{par}{r}"))
                t_vh.append(sch.op("sync", (lambda b_, h_, p_, r_: lambda g: g.dma_start(
                    out=vh_sb[p_][:, r_ * 8:(r_ + 1) * 8, 0:64],
                    in_=vall[b_][r_ * OWN:(r_ + 1) * OWN, h_ * 64:(h_ + 1) * 64]
                        .rearrange("(c p) d -> p c d", p=128)))(bi, h, par, r),
                    wait=[frq["v"][par], t_agv[bi]], dma=True, slot=f"vh{par}{r}"))
            t_qh = []
            for hh in range(2):
                dstq = (qdh_sb, qdl_sb)[hh]
                for dup in range(2):
                    t_qh.append(sch.op("sync", (lambda b_, h_, p_, hh_, d_, dq_: lambda g: g.dma_start(
                        out=dq_[p_][d_ * 64:(d_ + 1) * 64, :],
                        in_=qspill[b_, h_ * 128 + hh_ * 64: h_ * 128 + (hh_ + 1) * 64, :]))(
                            bi, h, par, hh, dup, dstq),
                        wait=[frq["q"][par]] if (hh == 0 and dup == 0) else (),
                        dma=True, slot=f"qh{par}{hh}{dup}"))
            last_s = None
            last_pv = None
            exp_toks = [None] * 32
            s_done = [None] * 32

            def emit_s(kc):
                nonlocal last_s
                sp = kc % 2
                w = t_kh + t_qh
                if kc >= 2:
                    w.append(exp_toks[kc - 2])
                else:
                    w += [bank_cons[2 * sp], bank_cons[2 * sp + 1]]
                for qt in range(2):
                    sch.op("tensor", (lambda p_, k_, s_, q_: lambda g: g.matmul(
                        psS[s_][:, q_ * 512:(q_ + 1) * 512],
                        ksb[p_][:, k_ * 128:(k_ + 1) * 128],
                        qdh_sb[p_][:, q_ * 512:(q_ + 1) * 512],
                        start=True, stop=False))(par, kc, sp, qt),
                        wait=w if qt == 0 else ())
                    last_s = sch.op("tensor", (lambda p_, k_, s_, q_: lambda g: g.matmul(
                        psS[s_][:, q_ * 512:(q_ + 1) * 512],
                        ksb[p_][:, k_ * 128:(k_ + 1) * 128],
                        qdl_sb[p_][:, q_ * 512:(q_ + 1) * 512],
                        start=False, stop=True))(par, kc, sp, qt))
                s_done[kc] = last_s

            def emit_exp(kc):
                sp = kc % 2
                pp = kc % 3
                exp_toks[kc] = sch.op("scalar", (lambda s_, p_: lambda g: g.activation(
                    out=pT_sb[p_][:], in_=psS[s_][:], func=AF.Exp,
                    bias=negc_sb[:], scale=SM_SCALE))(sp, pp),
                    wait=[s_done[kc], frq["pT"][pp]])

            def emit_pv(kc):
                nonlocal last_pv
                pp = kc % 3
                pvs = []
                for qt in range(2):
                    wp = [exp_toks[kc]] + t_vh
                    if kc == 0:
                        wp.append(bank_cons[4 + qt])
                    pvs.append(sch.op("tensor", (lambda p_, k_, q_, pp_: lambda g: g.matmul(
                        psC[q_][:], vh_sb[p_][:, k_, :], pT_sb[pp_][:, q_ * 512:(q_ + 1) * 512],
                        start=(k_ == 0), stop=(k_ == 31)))(par, kc, qt, pp),
                        wait=wp))
                frq["pT"][pp] = pvs
                last_pv = pvs

            emit_s(0)
            emit_exp(0)
            for kc in range(1, 32):
                emit_s(kc)
                emit_exp(kc)
                emit_pv(kc - 1)
            emit_pv(31)
            bank_cons[0] = bank_cons[1] = exp_toks[30]
            bank_cons[2] = bank_cons[3] = exp_toks[31]
            for qt in range(2):
                t_dn = sch.op("vector", (lambda q_: lambda g: g.stream_shuffle(
                    out=dn_sb[:], in_=psC[q_][64:96, :], mask=[0] * 32))(qt),
                    wait=last_pv + [frq["dn"]])
                t_bvm = sch.op("tensor", (lambda h_: lambda g: g.matmul(
                    psC[2][:], bvt_sb[:, h_ * 128:(h_ + 1) * 128], dn_sb[0:1, :],
                    start=True, stop=True))(h), wait=[t_dn, t_bv, bank_cons[6]])
                frq["dn"] = t_bvm
                t_bve = sch.op("vector", lambda g: g.tensor_copy(
                    out=bvo_sb[:], in_=psC[2][0:65, :]), wait=[t_bvm, frq.get("bvo")])
                bank_cons[6] = t_bve
                t_cev = sch.op("vector", (lambda q_: lambda g: g.tensor_tensor(
                    out=ctxT_sb[:], in0=psC[q_][0:65, :], in1=bvo_sb[:],
                    op=ALU.add))(qt),
                    wait=[t_bve, frq["ctxT"]])
                frq["bvo"] = t_cev
                bank_cons[4 + qt] = t_cev
                if DEBUG_DUMP and bi == 1 and h == 0 and qt == 0:
                    tdmp = sch.op("gpsimd", lambda g: g.dma_start(
                        out=dbg["ctxT"], in_=ctxT_sb[:]), wait=[t_cev], dma=True, slot="dbgctxT")
                    frq["ctxT"] = [frq["ctxT"], tdmp]
                for qc in range(4):
                    w = [t_cev, bank_cons[7]]
                    t_tr = sch.op("tensor", (lambda q_: lambda g: g.transpose(
                        psC[3][:, q_ * 72:q_ * 72 + 65],
                        ctxT_sb[:, q_ * 128:(q_ + 1) * 128], ident_sb[0:65, 0:65]))(qc),
                        wait=w)
                    frq["ctxT"] = t_tr
                    sch.op("vector", (lambda q_: lambda g: g.reciprocal(
                        out=rdn_sb[:], in_=psC[3][:, q_ * 72 + 64:q_ * 72 + 65]))(qc),
                        wait=[t_tr])
                    t_cx = sch.op("vector", (lambda q_, qt_, h_: lambda g: g.tensor_scalar(
                        out=ctx_sb[:, qt_ * 4 + q_, h_ * 64:(h_ + 1) * 64],
                        in0=psC[3][:, q_ * 72:q_ * 72 + 64],
                        scalar1=rdn_sb[:], scalar2=None, op0=ALU.mult))(qc, qt, h))
                    bank_cons[7] = t_cx
            frq["k"][par] = last_s
            frq["v"][par] = last_pv
            frq["q"][par] = last_s
        return ("tV", sch.count["tV"])

    def ph3_batch(bi, t_ctx_ready, do_ln2):
        last_tok = None
        for stl in range(8):
            sp = stl % 2
            t_ld = sch.op("sync", (lambda s_, b_, p_: lambda g: g.dma_start(
                out=xatt_sb[p_][:], in_=xin[b_][s_ * 128:(s_ + 1) * 128, :]))(stl, bi, sp),
                wait=[frq["xatt"][sp]], dma=True, slot=f"xatt{sp}")
            t_add = sch.op("vector", (lambda s_, p_: lambda g: g.tensor_tensor(
                out=xatt_sb[p_][:], in0=xatt_sb[p_][:], in1=ctx_sb[:, s_, :],
                op=ALU.add))(stl, sp), wait=[t_ld, t_ctx_ready])
            t_st = sch.op("gpsimd", (lambda s_, b_, p_: lambda g: g.dma_start(
                out=out[b_ * OWN + s_ * 128: b_ * OWN + (s_ + 1) * 128, :],
                in_=xatt_sb[p_][:]))(stl, bi, sp), wait=[t_add], dma=True, slot=f"ost{sp}")
            t_out_tokens.append(t_st)
            if do_ln2:
                t_y = ln_normalize(xatt_sb[sp][:], y_sb[sp][:], t_add, sp,
                                   extra=[frq["y"][sp]])
                t_ys = sch.op("gpsimd", (lambda s_, p_: lambda g: g.dma_start(
                    out=yspill[s_ * 128:(s_ + 1) * 128, :], in_=y_sb[p_][:]))(stl, sp),
                    wait=[t_y, frq["y"][sp]], dma=True, slot=f"ys{sp}")
                frq["y"][sp] = t_ys
                frq["xatt"][sp] = [t_st, t_y]
                last_tok = t_ys
            else:
                frq["xatt"][sp] = t_st
                last_tok = t_st
        return last_tok, list(t_out_tokens)

    tA = ph2_batch(0)
    t_y_last, t_ph3a_dG = ph3_batch(0, tA, True)
    tB = ph2_batch(1)
    if DEBUG_DUMP:
        sch.op("gpsimd", lambda g: g.dma_start(
            out=dbg["ctx"], in_=ctx_sb[:].rearrange("p t d -> p (t d)")),
            wait=[tB], dma=True, slot="dbgctx")
    _, t_ph3b_dG = ph3_batch(1, tB, False)
    t_out_base = t_ph3a_dG + t_ph3b_dG
    ph2.close()
    bar2 = sch.barrier()

    ph4 = ExitStack()
    e4 = ph4.enter_context
    gidx_sb = e4(nc.sbuf_tensor("gidx_sb", [128, E * G], U32))
    cntt_sb = e4(nc.sbuf_tensor("cntt_sb", [128, E * G], F32))
    cwg_sb = e4(nc.sbuf_tensor("cwg_sb", [128, E], F32))
    m_sb = e4(nc.sbuf_tensor("m_sb", [128, 2, G], F32))
    gy_sb = [e4(nc.sbuf_tensor(f"gy_sb{i}", [128, G, D], F32)) for i in range(2)]
    gyT_sb = [e4(nc.sbuf_tensor(f"gyT_sb{i}", [128, 2, 8, G * 128], BF16)) for i in range(2)]
    gT_sb = e4(nc.sbuf_tensor("gT_sb", [128, 2, 32, G * 128], BF16))
    w1s_sb = e4(nc.sbuf_tensor("w1s_sb", [128, 8, 512], F32))
    w1_sb = [e4(nc.sbuf_tensor(f"w1_sb{i}", [128, 2, 8, 512], BF16)) for i in range(2)]
    w2s_sb = [e4(nc.sbuf_tensor(f"w2s_sb{i}", [128, 2, 512], F32)) for i in range(2)]
    w2_sb = [e4(nc.sbuf_tensor(f"w2_sb{i}", [128, 2, 2, 512], BF16)) for i in range(2)]
    b1_sb = [e4(nc.sbuf_tensor("b1_sb0", [128, 32], F32))] * 2
    b2_sb = [e4(nc.sbuf_tensor("b2_sb0", [1, D], F32))] * 2
    gt_sb = e4(nc.sbuf_tensor("gt_sb", [128, G * 128], F32))
    sc_sb = [e4(nc.sbuf_tensor(f"sc_sb{i}", [128, D], F32)) for i in range(G)]

    t_gi = sch.op("sync", lambda g: g.dma_start(out=gidx_sb[:], in_=gidx),
                  wait=bar2, dma=True, slot="gidx")
    t_cn = sch.op("sync", lambda g: g.dma_start(out=cntt_sb[:], in_=cntt), dma=True, slot="cntt")

    frf = {"w1": [None, None], "w2": [None, None], "gy": [[None], [None]],
           "gyT": [None, None], "gT": None, "b": [None, None],
           "sc": [None] * G, "cwg": None, "m": [[None] * G, [None] * G]}
    t_scatters = []
    t_gath_e = [None] * (E + 1)
    t_m_e = [None] * (E + 1)

    def emit_gathers(ee):
        par = ee % 2
        t_gath = []
        t_m = []
        for gg in range(G):
            t_g = sch.op("gpsimd", (lambda e_, g_, p_: lambda g: g.indirect_dma_start(
                out=gy_sb[p_][:, g_, :], out_offset=None, in_=yspill,
                in_offset=IndirectOffsetOnAxis(gidx_sb[:, e_ * G + g_:e_ * G + g_ + 1], 0)))(ee, gg, par),
                wait=[t_gi, t_y_last] + frf["gy"][par] + bar2, dma=True, slot=f"gy{par}{gg}")
            t_gath.append(t_g)
            t_w = sch.op("gpsimd", (lambda e_, g_: lambda g: g.indirect_dma_start(
                out=cwg_sb[:], out_offset=None, in_=cw,
                in_offset=IndirectOffsetOnAxis(gidx_sb[:, e_ * G + g_:e_ * G + g_ + 1], 0)))(ee, gg),
                wait=[frf["cwg"]], dma=True, slot="cwg")
            t_m.append(sch.op("vector", (lambda e_, g_, p_: lambda g: g.tensor_tensor(
                out=m_sb[:, p_, g_:g_ + 1], in0=cntt_sb[:, e_ * G + g_:e_ * G + g_ + 1],
                in1=cwg_sb[:, e_:e_ + 1], op=ALU.mult))(ee, gg, par),
                wait=[t_w, t_cn, frf["m"][par][gg]]))
            frf["cwg"] = t_m[-1]
        t_gath_e[ee] = t_gath
        t_m_e[ee] = t_m

    def emit_gyT(ee):
        par = ee % 2
        t_gath = t_gath_e[ee]
        t_gyT = None
        gy_cons = []
        for gg in range(G):
            for half in range(2):
                tbk = 6 + half
                tr_tok = None
                for j in range(4):
                    dc = half * 4 + j
                    w = [t_gath[gg], bank_cons[tbk]] if j == 0 else [t_gath[gg]]
                    tr_tok = sch.op("tensor", (lambda g_, d_, b_, j_, p_: lambda g: g.transpose(
                        ps[b_][:, j_ * 128:(j_ + 1) * 128],
                        gy_sb[p_][:, g_, d_ * 128:(d_ + 1) * 128], ident_sb[:]))(gg, dc, tbk, j, par),
                        wait=w)
                gy_cons.append(tr_tok)
                t_hi = sch.op("scalar", (lambda g_, h_, b_, p_: lambda g: g.activation(
                    out=gyT_sb[p_][:, 0, h_ * 4:(h_ + 1) * 4, g_ * 128:(g_ + 1) * 128],
                    in_=ps[b_][:].rearrange("p (c t) -> p c t", c=4),
                    func=AF.Identity, scale=1.0))(gg, half, tbk, par),
                    wait=[tr_tok, frf["gyT"][par]])
                t_gyT = sch.op("vector", (lambda g_, h_, b_, p_: lambda g: g.tensor_tensor(
                    out=gyT_sb[p_][:, 1, h_ * 4:(h_ + 1) * 4, g_ * 128:(g_ + 1) * 128],
                    in0=ps[b_][:].rearrange("p (c t) -> p c t", c=4),
                    in1=gyT_sb[p_][:, 0, h_ * 4:(h_ + 1) * 4, g_ * 128:(g_ + 1) * 128],
                    op=ALU.subtract))(gg, half, tbk, par),
                    wait=[t_hi])
                bank_cons[tbk] = t_gyT
        frf["gy"][par] = gy_cons
        return t_gyT

    emit_gathers(0)
    t_gyT_par = [None, None]
    t_gyT_par[0] = emit_gyT(0)

    for ee in range(E):
        par = ee % 2
        t_gyT = t_gyT_par[par]
        t_m = t_m_e[ee]
        t_b1 = sch.op("sync", (lambda e_: lambda g: g.dma_start(
            out=b1_sb[0][:], in_=b1t[e_]))(ee),
            wait=[frf.get("b1last")] + bar2, dma=True, slot="b1")
        t_b2 = sch.op("sync", (lambda e_: lambda g: g.dma_start(
            out=b2_sb[0][:], in_=b2f[e_:e_ + 1, :]))(ee),
            wait=[frf.get("b2last")], dma=True, slot="b2")
        if ee + 1 < E:
            emit_gathers(ee + 1)

        t_gT = None
        for fc in range(8):
            wpar = fc % 2
            t_ws = sch.op("sync", (lambda e_, f_: lambda g: g.dma_start(
                out=w1s_sb[:],
                in_=w1t[e_, f_].rearrange("p (d c) -> p d c", d=8)))(ee, fc),
                wait=[frf.get("w1s")], dma=True, slot="w1s")
            t_wh = sch.op("scalar", (lambda p_: lambda g: g.activation(
                out=w1_sb[p_][:, 0], in_=w1s_sb[:], func=AF.Identity, scale=1.0))(wpar),
                wait=[t_ws, frf["w1"][wpar]])
            t_w1 = sch.op("vector", (lambda p_: lambda g: g.tensor_tensor(
                out=w1_sb[p_][:, 1], in0=w1s_sb[:], in1=w1_sb[p_][:, 0],
                op=ALU.subtract))(wpar), wait=[t_wh])
            frf["w1s"] = t_w1
            for sub in range(4):
                ffb = fc * 4 + sub
                pb = ffb % 3
                last_mm = None
                for dc in range(8):
                    w = [t_w1, t_gyT, bank_cons[pb]] if dc == 0 else []
                    for wi, gi in ((0, 0), (1, 0), (0, 1)):
                        last_mm = sch.op("tensor", (lambda d_, b_, p_, ep_, gi_, wi_, s_: lambda g: g.matmul(
                            ps[b_][:, 0:G * 128],
                            w1_sb[p_][:, wi_, d_, s_ * 128:(s_ + 1) * 128],
                            gyT_sb[ep_][:, gi_, d_, :],
                            start=(d_ == 0 and wi_ == 0 and gi_ == 0),
                            stop=(d_ == 7 and wi_ == 0 and gi_ == 1)))(
                                dc, pb, wpar, par, gi, wi, sub),
                            wait=w if (wi == 0 and gi == 0) else ())
                frf["w1"][wpar] = last_mm
                t_g32 = sch.op("scalar", (lambda b_, f2_: lambda g: g.activation(
                    out=gt_sb[:, 0:G * 128], in_=ps[b_][:, 0:G * 128],
                    func=AF.Gelu_apprx_tanh,
                    bias=b1_sb[0][:, f2_:f2_ + 1], scale=1.0))(pb, ffb),
                    wait=[last_mm, t_b1, frf.get("gt")])
                t_hi2 = sch.op("vector", (lambda f2_: lambda g: g.tensor_copy(
                    out=gT_sb[:, 0, f2_, :], in_=gt_sb[:, 0:G * 128]))(ffb),
                    wait=[t_g32, frf["gT"]])
                t_gT = sch.op("vector", (lambda f2_: lambda g: g.tensor_tensor(
                    out=gT_sb[:, 1, f2_, :], in0=gt_sb[:, 0:G * 128],
                    in1=gT_sb[:, 0, f2_, :], op=ALU.subtract))(ffb))
                frf["gt"] = t_gT
                bank_cons[pb] = t_gT
        frf["gyT"][par] = t_gT

        if ee + 1 < E:
            t_gyT_par[(ee + 1) % 2] = emit_gyT(ee + 1)

        t_ev = None
        for oc in range(2):
            mm_done = [None] * G
            for fp_ in range(16):
                wpar = fp_ % 2
                t_ws2 = sch.op("sync", (lambda e_, o_, f_, p_: lambda g: g.dma_start(
                    out=w2s_sb[p_][:],
                    in_=w2t[e_, o_, 2 * f_:2 * f_ + 2].rearrange("a p c -> p a c")))(ee, oc, fp_, wpar),
                    wait=[frf.get(f"w2s{wpar}")], dma=True, slot=f"w2s{wpar}")
                t_wh2 = sch.op("scalar", (lambda p_: lambda g: g.activation(
                    out=w2_sb[p_][:, 0], in_=w2s_sb[p_][:], func=AF.Identity, scale=1.0))(wpar),
                    wait=[t_ws2, frf["w2"][wpar]])
                t_w2 = sch.op("vector", (lambda p_: lambda g: g.tensor_tensor(
                    out=w2_sb[p_][:, 1], in0=w2s_sb[p_][:], in1=w2_sb[p_][:, 0],
                    op=ALU.subtract))(wpar), wait=[t_wh2])
                frf[f"w2s{wpar}"] = t_w2
                for sub in range(2):
                    fc2 = 2 * fp_ + sub
                    for tg in range(G):
                        pb = tg
                        if fc2 == 0:
                            frf["b2last"] = sch.op("tensor", (lambda o_, b_, p_: lambda g: g.matmul(
                                ps[b_][:], ones_sb[:], b2_sb[p_][:, o_ * 512:(o_ + 1) * 512],
                                start=True, stop=False))(oc, pb, par),
                                wait=[t_b2, t_ones, bank_cons[pb]])
                        for gi, wi in ((0, 0), (1, 0), (0, 1)):
                            mm_done[tg] = sch.op("tensor", (lambda f_, t_, b_, p_, gi_, wi_, s_: lambda g: g.matmul(
                                ps[b_][:], gT_sb[:, gi_, f_, t_ * 128:(t_ + 1) * 128],
                                w2_sb[p_][:, wi_, s_, :],
                                start=False, stop=(f_ == 31 and gi_ == 0 and wi_ == 1)))(
                                    fc2, tg, pb, wpar, gi, wi, sub),
                                wait=([t_w2, t_gT] if fc2 == 0 else [t_w2]) if (gi == 0 and wi == 0 and sub == 0) else ())
                    frf["w2"][wpar] = mm_done[G - 1]
            for tg in range(G):
                pb = tg
                t_ev = sch.op("vector", (lambda t_, o_, b_, p_: lambda g: g.tensor_scalar(
                    out=sc_sb[t_][:, o_ * 512:(o_ + 1) * 512], in0=ps[b_][:],
                    scalar1=m_sb[:, p_, t_:t_ + 1], scalar2=None, op0=ALU.mult))(tg, oc, pb, par),
                    wait=[mm_done[tg], t_m[tg], frf["sc"][tg]])
                bank_cons[pb] = t_ev
                frf["m"][par][tg] = t_ev
                if oc == 1:
                    t_sc = sch.op("gpsimd", (lambda e_, t_: lambda g: g.indirect_dma_start(
                        out=out, out_offset=IndirectOffsetOnAxis(
                            gidx_sb[:, e_ * G + t_:e_ * G + t_ + 1], 0),
                        in_=sc_sb[t_][:], in_offset=None,
                        compute_op=ALU.add))(ee, tg),
                        wait=[t_ev] + t_out_base, dma=True, slot=f"sc{tg}")
                    frf["sc"][tg] = t_sc
                    t_scatters.append(t_sc)
        frf["gT"] = t_ev

    ph4.close()

    final_waits = sch.barrier()
    with st_main:
        block = st_main.enter_context(nc.Block())
        sch.emit_all(block, "gpsimd", final_waits)
    return nc


def _split_hi_lo(a64):
    hi = a64.astype(np.float32)
    lo = (a64 - hi.astype(np.float64)).astype(np.float32)
    return hi, lo


def _rope_tables_f64(qoff):
    pos = np.arange(S, dtype=np.float64)
    inv = np.power(10000.0, -2.0 * np.arange(HD // 2, dtype=np.float64) / HD)
    ang = pos[None, :] * inv[:, None]
    cos64 = np.empty((HD, S), np.float64)
    sin64 = np.empty((HD, S), np.float64)
    cos64[0::2] = np.cos(ang)
    cos64[1::2] = np.cos(ang)
    sin64[0::2] = -np.sin(ang)
    sin64[1::2] = np.sin(ang)
    cos64 = np.roll(cos64, -qoff, axis=1)
    sin64 = np.roll(sin64, -qoff, axis=1)
    cos64 = np.concatenate([cos64, cos64], axis=0)
    sin64 = np.concatenate([sin64, sin64], axis=0)
    return cos64, sin64


def _plan_from_top_idx(TI):
    tok = TI.T
    plans = []
    maxd = 1
    for c in range(NCORES):
        bA = c // 4
        qoff = (c % 4) * OWN
        lo, hi_ = bA * S + qoff, bA * S + qoff + OWN
        per_e = []
        for ee in range(E):
            te = tok[ee]
            sel = te[(te >= lo) & (te < hi_)] - lo
            rows, cnts = np.unique(sel, return_counts=True)
            per_e.append((rows.astype(np.int64), cnts.astype(np.int64)))
            maxd = max(maxd, len(rows))
        plans.append(per_e)
    G = (maxd + 127) // 128
    return plans, G


def kernel(L0_attn_hidden_states, L0_attn_Wq, L0_attn_bq, L0_attn_Wk, L0_attn_bk,
           L0_attn_Wv, L0_attn_bv, L0_ffn_combine_weights, L0_ffn_top_idx,
           L0_ffn_W1, L0_ffn_b1, L0_ffn_W2, L0_ffn_b2):
    X = np.ascontiguousarray(np.asarray(L0_attn_hidden_states, dtype=np.float32))
    Wq = np.asarray(L0_attn_Wq, np.float32); Wk = np.asarray(L0_attn_Wk, np.float32)
    Wv = np.asarray(L0_attn_Wv, np.float32)
    bq = np.asarray(L0_attn_bq, np.float32); bk = np.asarray(L0_attn_bk, np.float32)
    bv = np.asarray(L0_attn_bv, np.float32)
    CWf = np.ascontiguousarray(np.asarray(L0_ffn_combine_weights, np.float32))
    TI = np.asarray(L0_ffn_top_idx).astype(np.int64)
    W1 = np.asarray(L0_ffn_W1, np.float32); b1 = np.asarray(L0_ffn_b1, np.float32)
    W2 = np.asarray(L0_ffn_W2, np.float32); b2 = np.asarray(L0_ffn_b2, np.float32)

    plans, G = _plan_from_top_idx(TI)

    wqt = Wq.reshape(8, 128, HP, 128).transpose(2, 1, 0, 3).reshape(HP, 128, D).copy()
    wkt = Wk.reshape(8, 128, HP, 128).transpose(2, 1, 0, 3).reshape(HP, 128, D).copy()
    wvt = Wv.reshape(8, 128, D).transpose(1, 0, 2).copy()
    bqt = bq.reshape(HP, 128).T.copy(); bkt = bk.reshape(HP, 128).T.copy()
    bvtm = np.zeros((H, 128), np.float32); bvtm[:, 0:64] = bv.reshape(H, 64)
    identm = np.eye(128, dtype=np.float32)
    b1tm = np.ascontiguousarray(b1.reshape(E, 32, 128).transpose(0, 2, 1))
    w1t = np.ascontiguousarray(
        W1.reshape(E, 8, 128, 8, 512).transpose(0, 3, 2, 1, 4)).reshape(E, 8, 128, FF)
    w2t = np.ascontiguousarray(W2.reshape(E, 32, 128, 2, 512).transpose(0, 3, 1, 2, 4))

    in_maps = []
    for c in range(NCORES):
        bA = c // 4
        bB = 2 + c // 4
        qoff = (c % 4) * OWN
        xa = np.ascontiguousarray(X[bA, qoff:qoff + OWN])
        xb = np.ascontiguousarray(X[bB, qoff:qoff + OWN])
        cos64, sin64 = _rope_tables_f64(qoff)
        rc_hi, rc_lo = _split_hi_lo(cos64)
        rs_hi, rs_lo = _split_hi_lo(sin64)
        cwc = np.ascontiguousarray(CWf[bA * S + qoff: bA * S + qoff + OWN, :])
        gidxm = np.zeros((E * G, 128), np.uint32)
        cnttm = np.zeros((E * G, 128), np.float32)
        for ee in range(E):
            rows, cnts = plans[c][ee]
            n = len(rows)
            used = np.zeros(OWN, bool)
            used[rows] = True
            padrows = np.nonzero(~used)[0][: G * 128 - n]
            allrows = np.concatenate([rows, padrows])
            allcnt = np.concatenate([cnts, np.zeros(len(padrows), np.int64)])
            gidxm[ee * G:(ee + 1) * G, :] = allrows.reshape(G, 128)
            cnttm[ee * G:(ee + 1) * G, :] = allcnt.reshape(G, 128)
        in_maps.append(dict(
            xa=xa, xb=xb, wqt=wqt, wkt=wkt, wvt=wvt, bqt=bqt, bkt=bkt, bvt=bvtm,
            rc_hi=rc_hi, rc_lo=rc_lo, rs_hi=rs_hi, rs_lo=rs_lo, ident=identm,
            cw=cwc, cntt=np.ascontiguousarray(cnttm.T), gidx=np.ascontiguousarray(gidxm.T),
            b1t=b1tm, b2f=b2, w1t=w1t, w2t=w2t))

    nc = build_program(G)
    res = run_bass_kernel_spmd(nc, in_maps, core_ids=list(range(NCORES)))

    result = np.empty((B, S, D), np.float32)
    for c in range(NCORES):
        o = res.results[c]["out"]
        bA = c // 4
        bB = 2 + c // 4
        qoff = (c % 4) * OWN
        result[bA, qoff:qoff + OWN] = o[0:OWN]
        result[bB, qoff:qoff + OWN] = o[OWN:2 * OWN]
    return result


# version 16
# speedup vs baseline: 1.2916x; 1.0207x over previous
import os
import numpy as np
import concourse.bass as bass
import concourse.mybir as mybir
from concourse.bass import IndirectOffsetOnAxis
from concourse.bass_utils import run_bass_kernel_spmd
from contextlib import ExitStack

F32 = mybir.dt.float32
BF16 = mybir.dt.bfloat16
U32 = mybir.dt.uint32
AF = mybir.ActivationFunctionType
ALU = mybir.AluOpType

B, S, D = 4, 4096, 1024
H, HD = 16, 64
HP = H // 2
E, FF = 8, 4096
OWN = 1024
NCORES = 8
EPS = 1e-5
CSHIFT = 12.0
SM_SCALE = 0.125
PAIR_MASK = [i ^ 1 for i in range(32)]
SAFE_SYNC = os.environ.get("KERNEL_SAFE_SYNC") == "1"
DEBUG_DUMP = os.environ.get("KERNEL_DEBUG_DUMP") == "1"


class Sched:

    ENG_SEMS = {"tensor": "tT", "vector": "tV", "scalar": "tA", "gpsimd": "tG"}

    def __init__(self, nc, stack):
        self.nc = nc
        self.stack = stack
        self.ops = {k: [] for k in ("sync", "scalar", "vector", "tensor", "gpsimd")}
        self.sems = {}
        self.count = {}
        for s in ("tT", "tV", "tA", "tG"):
            self.sems[s] = stack.enter_context(nc.semaphore(s))
            self.count[s] = 0
        self.hwm = {k: {} for k in self.ops}

    def op(self, eng, emit, wait=(), dma=False, slot=None):
        ownsem = self.ENG_SEMS.get(eng)
        guard = SAFE_SYNC or eng in ("vector", "gpsimd")
        selfwait = ((ownsem, self.count[ownsem])
                    if (guard and ownsem and self.count[ownsem] > 0) else None)
        if dma:
            assert slot is not None, "DMA ops need a slot name"
            sem = "D_" + slot
            if sem not in self.sems:
                self.sems[sem] = self.stack.enter_context(self.nc.semaphore(sem))
                self.count[sem] = 0
        else:
            sem = self.ENG_SEMS[eng]
        self.count[sem] += 16 if dma else 1
        tok = (sem, self.count[sem])

        def _flat(ws):
            for w in ws:
                if w is None:
                    continue
                if isinstance(w, list):
                    yield from _flat(w)
                else:
                    yield w
        waits = []
        for (s, v) in _flat([selfwait] + list(wait)):
            if self.hwm[eng].get(s, 0) >= v:
                continue
            self.hwm[eng][s] = v
            waits.append((s, v))
        self.ops[eng].append((waits, emit, tok, 16 if dma else 1))
        return tok

    def barrier(self):
        return [(s, c) for s, c in self.count.items() if c > 0]

    def emit_all(self, block, final_eng, final_waits):
        sems = self.sems

        def mk(engname):
            oplist = self.ops[engname]
            extra = final_waits if engname == final_eng else ()

            def emitfn(g):
                for waits, emit, tok, amt in oplist:
                    for (s, v) in waits:
                        g.wait_ge(sems[s], v)
                    emit(g).then_inc(sems[tok[0]], amt)
                for (s, v) in extra:
                    g.wait_ge(sems[s], v)
            return emitfn

        block.sync(mk("sync"))
        block.scalar(mk("scalar"))
        block.vector(mk("vector"))
        block.tensor(mk("tensor"))
        block.gpsimd(mk("gpsimd"))


def build_program(G):
    assert 1 <= G <= 4
    nc = bass.Bass("TRN2", target_bir_lowering=False, debug=False,
                   detect_race_conditions=SAFE_SYNC)

    def I(name, shape, dt=F32):
        return nc.dram_tensor(name, list(shape), dt, kind="ExternalInput").ap()

    xin = [I("xa", (OWN, D)), I("xb", (OWN, D))]
    wqt = I("wqt", (HP, 128, D)); wkt = I("wkt", (HP, 128, D))
    wvt = I("wvt", (128, 8, D))
    bqt = I("bqt", (128, HP)); bkt = I("bkt", (128, HP))
    bvc = I("bvc", (65, H))
    rc_hi = I("rc_hi", (128, S)); rc_lo = I("rc_lo", (128, S))
    rs_hi = I("rs_hi", (128, S)); rs_lo = I("rs_lo", (128, S))
    ident = I("ident", (128, 128))
    cw = I("cw", (OWN, E))
    cntt = I("cntt", (128, E * G))
    gidx = I("gidx", (128, E * G), U32)
    b1t = I("b1t", (E, 128, 32)); b2f = I("b2f", (E, D))
    w1t = I("w1t", (E, 8, 128, FF))
    w2t = I("w2t", (E, 2, 32, 128, 512))

    out = nc.dram_tensor("out", [2 * OWN, D], F32, kind="ExternalOutput").ap()
    dbg = {}
    if DEBUG_DUMP:
        dbg["ctx"] = nc.dram_tensor("dbg_ctx", [128, 8 * D], F32, kind="ExternalOutput").ap()
        dbg["p"] = nc.dram_tensor("dbg_p", [128, 512], F32, kind="ExternalOutput").ap()
        dbg["ctxT"] = nc.dram_tensor("dbg_ctxT", [65, 512], F32, kind="ExternalOutput").ap()
        dbg["dn"] = nc.dram_tensor("dbg_dn", [32, 512], F32, kind="ExternalOutput").ap()
        dbg["kh"] = nc.dram_tensor("dbg_kh", [128, S], F32, kind="ExternalOutput").ap()
        dbg["qh"] = nc.dram_tensor("dbg_qh", [128, OWN], F32, kind="ExternalOutput").ap()
        dbg["xln"] = nc.dram_tensor("dbg_xln", [128, 4 * D], F32, kind="ExternalOutput").ap()
        dbg["xlnT"] = nc.dram_tensor("dbg_xlnT", [128, 8 * 512], F32, kind="ExternalOutput").ap()
        dbg["kev"] = nc.dram_tensor("dbg_kev", [128, 512], F32, kind="ExternalOutput").ap()
        dbg["krp"] = nc.dram_tensor("dbg_krp", [128, 512], F32, kind="ExternalOutput").ap()

    kcon = nc.dram_tensor("kcon", [2, H * 128, OWN], BF16, kind="Internal").ap()
    qspill = nc.dram_tensor("qspill", [2, H * 128, OWN], BF16, kind="Internal").ap()
    vcon = nc.dram_tensor("vcon", [2, OWN, D], F32, kind="Internal").ap()
    kall = [nc.dram_tensor(f"kall{b}", [4 * H * 128, OWN], BF16, kind="Internal").ap()
            for b in range(2)]
    vall = [nc.dram_tensor(f"vall{b}", [4 * OWN, D], F32, kind="Internal").ap()
            for b in range(2)]
    yspill = nc.dram_tensor("yspill", [OWN, D], F32, kind="Internal").ap()

    st_main = ExitStack()
    e = st_main.enter_context
    sch = Sched(nc, st_main)

    ident_sb = e(nc.sbuf_tensor("ident_sb", [128, 128], F32))
    bq_sb = e(nc.sbuf_tensor("bq_sb", [128, HP], F32))
    bk_sb = e(nc.sbuf_tensor("bk_sb", [128, HP], F32))
    ones_sb = e(nc.sbuf_tensor("ones_sb", [1, 128], F32))
    eps_sb = e(nc.sbuf_tensor("eps_sb", [128, 1], F32))
    negc_sb = e(nc.sbuf_tensor("negc_sb", [128, 1], F32))
    stats_sb = e(nc.sbuf_tensor("stats_sb", [128, 12], F32))
    mv_sb = e(nc.sbuf_tensor("mv_sb", [128, 2], F32))
    sd_sb = e(nc.sbuf_tensor("sd_sb", [128, 1], F32))
    rstd_sb = e(nc.sbuf_tensor("rstd_sb", [128, 2], F32))
    nb_sb = e(nc.sbuf_tensor("nb_sb", [128, 2], F32))

    psS = [e(nc.psum_tensor(f"psS{i}", [128, 1024], F32)) for i in range(2)]
    psC = [e(nc.psum_tensor(f"psC{i}", [128, 512], F32)) for i in range(4)]
    ps = [psS[0][:, 0:512], psS[0][:, 512:1024],
          psS[1][:, 0:512], psS[1][:, 512:1024],
          psC[0][:], psC[1][:], psC[2][:], psC[3][:]]
    bank_cons = [None] * 8

    sch.op("vector", lambda g: g.memset(eps_sb[:], EPS))
    sch.op("vector", lambda g: g.memset(negc_sb[:], -CSHIFT))
    t_ones = sch.op("vector", lambda g: g.memset(ones_sb[:], 1.0))
    t_id = sch.op("sync", lambda g: g.dma_start(out=ident_sb[:], in_=ident), dma=True, slot="ident")
    t_bq = sch.op("sync", lambda g: g.dma_start(out=bq_sb[:], in_=bqt), dma=True, slot="bq")
    t_bk = sch.op("sync", lambda g: g.dma_start(out=bk_sb[:], in_=bkt), dma=True, slot="bk")

    ln_free = [None, None]

    def ln_normalize(src_ap, dst_ap, t_src, par, extra=()):
        sch.op("vector", lambda g: g.bn_stats(out=stats_sb[:, 0:6], in_=src_ap[:, 0:512]),
               wait=[t_src])
        sch.op("vector", lambda g: g.bn_stats(out=stats_sb[:, 6:12], in_=src_ap[:, 512:1024]))
        t_ag = sch.op("vector", lambda g: g.bn_aggr(out=mv_sb[:], in_=stats_sb[:]))
        t_sd = sch.op("scalar", lambda g: g.activation(
            out=sd_sb[:], in_=mv_sb[:, 1:2], func=AF.Sqrt, bias=eps_sb[:], scale=1.0),
            wait=[t_ag])
        sch.op("vector", (lambda p: lambda g: g.reciprocal(
            out=rstd_sb[:, p:p + 1], in_=sd_sb[:]))(par), wait=[t_sd, ln_free[par]])
        t_nb = sch.op("vector", (lambda p: lambda g: g.scalar_tensor_tensor(
            out=nb_sb[:, p:p + 1], in0=mv_sb[:, 0:1], scalar=-1.0,
            in1=rstd_sb[:, p:p + 1], op0=ALU.mult, op1=ALU.mult))(par))
        t_act = sch.op("scalar", (lambda p: lambda g: g.activation(
            out=dst_ap, in_=src_ap, func=AF.Identity,
            bias=nb_sb[:, p:p + 1], scale=rstd_sb[:, p:p + 1]))(par),
            wait=[t_nb] + list(extra))
        ln_free[par] = t_act
        return t_act

    ph1 = ExitStack()
    e1 = ph1.enter_context
    wq_sb = [e1(nc.sbuf_tensor("wq_sb0", [128, 8, 128], F32))] * 2
    wk_sb = [e1(nc.sbuf_tensor("wk_sb0", [128, 8, 128], F32))] * 2
    wqh_sb = [e1(nc.sbuf_tensor(f"wqh_sb{i}", [128, 2, 8, 128], BF16)) for i in range(2)]
    wkh_sb = [e1(nc.sbuf_tensor(f"wkh_sb{i}", [128, 2, 8, 128], BF16)) for i in range(2)]
    wvh_sb = e1(nc.sbuf_tensor("wvh_sb", [128, 2, 8, D], BF16))
    ropec_sb = e1(nc.sbuf_tensor("ropec_sb", [128, S], F32))
    ropes_sb = e1(nc.sbuf_tensor("ropes_sb", [128, S], F32))
    xt_sb = [e1(nc.sbuf_tensor(f"xt_sb{i}", [128, D], F32)) for i in range(2)]
    xln_sb = e1(nc.sbuf_tensor("xln_sb", [128, 4, D], F32))
    rtmp_sb = xln_sb[:].rearrange("p a b -> p (a b)")
    xlnTh_sb2 = [e1(nc.sbuf_tensor(f"xlnTh_sb{i}", [128, 2, 8, 512], BF16)) for i in range(2)]
    kev_sb = e1(nc.sbuf_tensor("kev_sb", [128, 512], F32))
    ksh_sb = e1(nc.sbuf_tensor("ksh_sb", [128, 512], F32))
    kt1_sb = e1(nc.sbuf_tensor("kt1_sb", [128, 512], F32))
    kt2_sb = e1(nc.sbuf_tensor("kt2_sb", [128, 512], F32))
    krp_sb = [e1(nc.sbuf_tensor(f"krp_sb{i}", [128, 512], F32)) for i in range(2)]
    hilo_sb = [e1(nc.sbuf_tensor(f"hilo_sb{i}", [128, 2, 512], BF16)) for i in range(2)]
    vev_sb = [e1(nc.sbuf_tensor(f"vev_sb{i}", [128, 512], F32)) for i in range(2)]

    t_wv = None
    for ch in range(2):
        t_l = sch.op("sync", (lambda c_: lambda g: g.dma_start(
            out=xln_sb[:], in_=wvt[:, c_ * 4:(c_ + 1) * 4, :]))(ch),
            wait=[t_wv], dma=True, slot="wv")
        sch.op("vector", (lambda c_: lambda g: g.tensor_copy(
            out=wvh_sb[:, 0, c_ * 4:(c_ + 1) * 4, :], in_=xln_sb[:]))(ch), wait=[t_l])
        t_wv = sch.op("vector", (lambda c_: lambda g: g.tensor_tensor(
            out=wvh_sb[:, 1, c_ * 4:(c_ + 1) * 4, :], in0=xln_sb[:],
            in1=wvh_sb[:, 0, c_ * 4:(c_ + 1) * 4, :], op=ALU.subtract))(ch))
    ta = sch.op("sync", lambda g: g.dma_start(out=ropec_sb[:], in_=rc_hi), dma=True, slot="rc")
    tb = sch.op("sync", lambda g: g.dma_start(out=rtmp_sb, in_=rc_lo), dma=True, slot="rtmp",
                wait=[t_wv])
    t_rc = sch.op("vector", lambda g: g.tensor_tensor(
        out=ropec_sb[:], in0=ropec_sb[:], in1=rtmp_sb, op=ALU.add), wait=[ta, tb])
    ta = sch.op("sync", lambda g: g.dma_start(out=ropes_sb[:], in_=rs_hi), dma=True, slot="rs")
    tb = sch.op("sync", lambda g: g.dma_start(out=rtmp_sb, in_=rs_lo), dma=True,
                wait=[t_rc], slot="rtmp")
    t_rs = sch.op("vector", lambda g: g.tensor_tensor(
        out=ropes_sb[:], in0=ropes_sb[:], in1=rtmp_sb, op=ALU.add), wait=[ta, tb])

    fr = {"xt": [None, None], "krp": [None, None], "vev": [None, None],
          "wq": [None, None], "wk": [None, None], "xlnT": [None, None],
          "xln": [None] * 4}

    spill_toks = [[], []]

    def ph1_batch(bi):
        xsrc = xin[bi]
        for tt in range(2):
            xlnTh_sb = xlnTh_sb2[tt % 2]
            last_ev = None
            for stl in range(4):
                sp = stl % 2
                rows = tt * 512 + stl * 128
                t_ld = sch.op("sync", (lambda sp_, r_, x_: lambda g: g.dma_start(
                    out=xt_sb[sp_][:], in_=x_[r_:r_ + 128, :]))(sp, rows, xsrc),
                    wait=[fr["xt"][sp]], dma=True, slot=f"xt{sp}")
                t_act = ln_normalize(xt_sb[sp][:], xln_sb[:, stl, :], t_ld, sp,
                                     extra=[fr["xln"][stl], t_wv, t_rs])
                fr["xt"][sp] = t_act
                for half in range(2):
                    tbk = 0 + half
                    tr_tok = None
                    for j in range(4):
                        dc = half * 4 + j
                        w = [t_act, t_id]
                        if j == 0:
                            w.append(bank_cons[tbk])
                        tr_tok = sch.op("tensor", (lambda s_, d_, b_, j_: lambda g: g.transpose(
                            ps[b_][:, j_ * 128:(j_ + 1) * 128],
                            xln_sb[:, s_, d_ * 128:(d_ + 1) * 128], ident_sb[:]))(stl, dc, tbk, j),
                            wait=w)
                    fr["xln"][stl] = tr_tok
                    t_hi = sch.op("scalar", (lambda s_, h_, b_: lambda g: g.activation(
                        out=xlnTh_sb[:, 0, h_ * 4:(h_ + 1) * 4, s_ * 128:(s_ + 1) * 128],
                        in_=ps[b_][:].rearrange("p (c t) -> p c t", c=4),
                        func=AF.Identity, scale=1.0))(stl, half, tbk),
                        wait=[tr_tok, fr["xlnT"][tt % 2]])
                    last_ev = sch.op("vector", (lambda s_, h_, b_: lambda g: g.tensor_tensor(
                        out=xlnTh_sb[:, 1, h_ * 4:(h_ + 1) * 4, s_ * 128:(s_ + 1) * 128],
                        in0=ps[b_][:].rearrange("p (c t) -> p c t", c=4),
                        in1=xlnTh_sb[:, 0, h_ * 4:(h_ + 1) * 4, s_ * 128:(s_ + 1) * 128],
                        op=ALU.subtract))(stl, half, tbk), wait=[t_hi])
                    bank_cons[tbk] = last_ev
            t_xlnT = last_ev
            if DEBUG_DUMP and bi == 1 and tt == 0:
                td = sch.op("gpsimd", lambda g: g.dma_start(
                    out=dbg["xln"], in_=xln_sb[:].rearrange("p a b -> p (a b)")),
                    wait=[t_xlnT], dma=True, slot="dbgxln")
                td2 = sch.op("gpsimd", lambda g: g.dma_start(
                    out=dbg["xlnT"], in_=xlnT_sb[:].rearrange("p a b -> p (a b)")),
                    wait=[t_xlnT], dma=True, slot="dbgxlnT")
                t_xlnT = [t_xlnT, td, td2]
            last_proj_mm = None

            for which in ("k", "q"):
                ws_ = wk_sb if which == "k" else wq_sb
                wsrc = wkt if which == "k" else wqt
                bsb = bk_sb if which == "k" else bq_sb
                tbias = t_bk if which == "k" else t_bq
                spill = kcon if which == "k" else qspill
                for hp_i in range(HP):
                    par = hp_i % 2
                    t_w = sch.op("sync", (lambda p_, h_, s_, d_: lambda g: g.dma_start(
                        out=d_[p_][:], in_=s_[h_]))(par, hp_i, wsrc, ws_),
                        wait=[fr["w" + which][par], fr.get("wslab" + which)],
                        dma=True, slot=f"w{which}{par}")
                    wh_ = wkh_sb if which == "k" else wqh_sb
                    sch.op("gpsimd", (lambda p_, wl_, wh2_: lambda g: g.tensor_copy(
                        out=wh2_[p_][:, 0], in_=wl_[p_][:]))(par, ws_, wh_), wait=[t_w])
                    t_w = sch.op("gpsimd", (lambda p_, wl_, wh2_: lambda g: g.tensor_tensor(
                        out=wh2_[p_][:, 1], in0=wl_[p_][:], in1=wh2_[p_][:, 0],
                        op=ALU.subtract))(par, ws_, wh_))
                    fr["wslab" + which] = t_w
                    pb = 2 + par
                    mm_tok = None
                    for dc in range(8):
                        w = [t_w, t_xlnT, bank_cons[pb]] if dc == 0 else []
                        for wi, xi in ((0, 0), (1, 0), (0, 1)):
                            mm_tok = sch.op("tensor", (lambda p_, d_, b_, wh2_, wi_, xi_, f_: lambda g: g.matmul(
                                ps[b_][:], wh2_[p_][:, wi_, d_, :],
                                xlnTh_sb[:, xi_, d_, :],
                                start=(d_ == 0 and f_), stop=(d_ == 7 and wi_ == 0 and xi_ == 1)))(
                                    par, dc, pb, wh_, wi, xi, (wi == 0 and xi == 0)),
                                wait=w if (wi == 0 and xi == 0) else ())
                    fr["w" + which][par] = mm_tok
                    last_proj_mm = mm_tok
                    t_ev = sch.op("scalar", (lambda h_, b_, bs_: lambda g: g.activation(
                        out=kev_sb[:], in_=ps[b_][:], func=AF.Identity,
                        bias=bs_[:, h_:h_ + 1], scale=1.0))(hp_i, pb, bsb),
                        wait=[mm_tok, tbias, fr.get("kev")])
                    bank_cons[pb] = t_ev
                    if DEBUG_DUMP and bi == 1 and tt == 0 and which == "k" and hp_i == 0:
                        tdk = sch.op("gpsimd", lambda g: g.dma_start(
                            out=dbg["kev"], in_=kev_sb[:]), wait=[t_ev], dma=True, slot="dbgkev")
                        bank_cons[pb] = [t_ev, tdk]
                    sch.op("vector", lambda g: g.stream_shuffle(
                        out=ksh_sb[:], in_=kev_sb[:], mask=PAIR_MASK), wait=[t_ev])
                    sch.op("vector", (lambda tt_: lambda g: g.tensor_tensor(
                        out=kt1_sb[:], in0=kev_sb[:],
                        in1=ropec_sb[:, tt_ * 512:(tt_ + 1) * 512], op=ALU.mult))(tt),
                        wait=[t_rc])
                    fr["kev"] = sch.op("vector", (lambda tt_: lambda g: g.tensor_tensor(
                        out=kt2_sb[:], in0=ksh_sb[:],
                        in1=ropes_sb[:, tt_ * 512:(tt_ + 1) * 512], op=ALU.mult))(tt),
                        wait=[t_rs])
                    kp = hp_i % 2
                    t_rp = sch.op("vector", (lambda kp_: lambda g: g.tensor_tensor(
                        out=krp_sb[kp_][:], in0=kt1_sb[:], in1=kt2_sb[:], op=ALU.add))(kp))
                    sch.op("vector", (lambda kp_: lambda g: g.tensor_copy(
                        out=hilo_sb[kp_][:, 0, :], in_=krp_sb[kp_][:]))(kp),
                        wait=[fr["krp"][kp]])
                    t_rp = sch.op("vector", (lambda kp_: lambda g: g.tensor_tensor(
                        out=hilo_sb[kp_][:, 1, :], in0=krp_sb[kp_][:],
                        in1=hilo_sb[kp_][:, 0, :], op=ALU.subtract))(kp))
                    sps = []
                    for hh in range(2):
                        sps.append(sch.op("scalar", (lambda kp_, h_, tt_, b_, sp_, hh_: lambda g: g.dma_start(
                            out=sp_[b_, (2 * h_ + hh_) * 128:(2 * h_ + hh_ + 1) * 128,
                                    tt_ * 512:(tt_ + 1) * 512]
                                .rearrange("(h p) c -> p h c", h=2),
                            in_=hilo_sb[kp_][hh_ * 64:(hh_ + 1) * 64, :, :]))(kp, hp_i, tt, bi, spill, hh),
                            wait=[t_rp], dma=True, slot=f"krp{kp}{which}{hh}"))
                    if which == "k":
                        spill_toks[bi].extend(sps)
                    fr["krp"][kp] = sps

            for vc in range(2):
                for tc in range(4):
                    pb = 4 + tc % 2
                    mm_tok = None
                    for dc in range(8):
                        w = [t_wv, t_xlnT, bank_cons[pb]] if dc == 0 else []
                        for xi, wi in ((0, 0), (1, 0), (0, 1)):
                            mm_tok = sch.op("tensor", (lambda v_, t_, d_, b_, xi_, wi_, f_: lambda g: g.matmul(
                                ps[b_][:], xlnTh_sb[:, xi_, d_, t_ * 128:(t_ + 1) * 128],
                                wvh_sb[:, wi_, d_, v_ * 512:(v_ + 1) * 512],
                                start=(d_ == 0 and f_), stop=(d_ == 7 and xi_ == 0 and wi_ == 1)))(
                                    vc, tc, dc, pb, xi, wi, (xi == 0 and wi == 0)),
                                wait=w if (xi == 0 and wi == 0) else ())
                    last_proj_mm = mm_tok
                    vp = tc % 2
                    t_ev = sch.op("scalar", (lambda vp_, b_: lambda g: g.activation(
                        out=vev_sb[vp_][:], in_=ps[b_][:], func=AF.Identity,
                        scale=1.0))(vp, pb),
                        wait=[mm_tok, fr["vev"][vp]])
                    bank_cons[pb] = t_ev
                    t_sp = sch.op("gpsimd", (lambda vp_, v_, t_, tt_, b_: lambda g: g.dma_start(
                        out=vcon[b_, tt_ * 512 + t_ * 128: tt_ * 512 + (t_ + 1) * 128,
                                 v_ * 512:(v_ + 1) * 512],
                        in_=vev_sb[vp_][:]))(vp, vc, tc, tt, bi),
                        wait=[t_ev], dma=True, slot=f"vev{vp}")
                    spill_toks[bi].append(t_sp)
                    fr["vev"][vp] = t_sp
            fr["xlnT"][tt % 2] = last_proj_mm

    RG = [[0, 1, 2, 3], [4, 5, 6, 7]]
    t_agk = [None, None]
    t_agv = [None, None]

    def emit_ag(bi):
        t_agk[bi] = sch.op("gpsimd", (lambda b_: lambda g: g.collective_compute(
            "AllGather", ALU.bypass, replica_groups=RG,
            ins=[kcon[b_].opt()], outs=[kall[b_].opt()]))(bi), wait=spill_toks[bi])
        t_agv[bi] = sch.op("gpsimd", (lambda b_: lambda g: g.collective_compute(
            "AllGather", ALU.bypass, replica_groups=RG,
            ins=[vcon[b_].opt()], outs=[vall[b_].opt()]))(bi))

    ph1_batch(0)
    emit_ag(0)
    ph1_batch(1)
    bar1 = sch.barrier()
    emit_ag(1)
    ph1.close()

    ph2 = ExitStack()
    e2 = ph2.enter_context
    ksb = [e2(nc.sbuf_tensor(f"ksb{i}", [128, S], BF16)) for i in range(2)]
    ctx_sb = e2(nc.sbuf_tensor("ctx_sb", [128, 8, D], F32))
    vh_sb = [e2(nc.sbuf_tensor(f"vh_sb{i}", [128, 32, 128], F32)) for i in range(2)]
    qdh_sb = [e2(nc.sbuf_tensor(f"qdh_sb{i}", [128, OWN], BF16)) for i in range(2)]
    qdl_sb = [e2(nc.sbuf_tensor(f"qdl_sb{i}", [128, OWN], BF16)) for i in range(2)]
    pT_sb = [e2(nc.sbuf_tensor(f"pT_sb{i}", [128, 1024], F32)) for i in range(3)]
    ctxT_sb = [e2(nc.sbuf_tensor(f"ctxT_sb{i}", [65, 512], F32)) for i in range(2)]
    dn_sb = e2(nc.sbuf_tensor("dn_sb", [96, 512], F32))
    bvc_sb = e2(nc.sbuf_tensor("bvc_sb", [65, H], F32))
    rdn_sb = e2(nc.sbuf_tensor("rdn_sb", [128, 1], F32))
    xatt_sb = [e2(nc.sbuf_tensor(f"xatt_sb{i}", [128, D], F32)) for i in range(2)]
    y_sb = [e2(nc.sbuf_tensor(f"y_sb{i}", [128, D], F32)) for i in range(2)]

    t_bv = sch.op("sync", lambda g: g.dma_start(
        out=bvc_sb[:], in_=bvc), wait=bar1, dma=True, slot="bv")
    tvz = []
    for i in range(2):
        sch.op("vector", (lambda i_: lambda g: g.memset(vh_sb[i_][:], 0.0))(i),
               wait=bar1 if i == 0 else ())
        tvz.append(sch.op("vector", (lambda i_: lambda g: g.memset(
            vh_sb[i_][:, :, 64:65], 1.0))(i)))
    frq = {"k": [None, None], "v": [tvz[0], tvz[1]], "q": [None, None],
           "pT": [None, None, None], "xatt": [None, None], "y": [None, None],
           "ctxT": [None, None], "dn": None}
    deferred = []

    def emit_ctx_finish(h, qt, cpar, t_cev):
        for qc in range(4):
            t_tr = sch.op("tensor", (lambda q_, c_: lambda g: g.transpose(
                psC[3][:, q_ * 72:q_ * 72 + 65],
                ctxT_sb[c_][:, q_ * 128:(q_ + 1) * 128], ident_sb[0:65, 0:65]))(qc, cpar),
                wait=[t_cev, bank_cons[7]])
            frq["ctxT"][cpar] = t_tr
            sch.op("vector", (lambda q_: lambda g: g.reciprocal(
                out=rdn_sb[:], in_=psC[3][:, q_ * 72 + 64:q_ * 72 + 65]))(qc),
                wait=[t_tr])
            t_cx = sch.op("vector", (lambda q_, qt_, h_: lambda g: g.tensor_scalar(
                out=ctx_sb[:, qt_ * 4 + q_, h_ * 64:(h_ + 1) * 64],
                in0=psC[3][:, q_ * 72:q_ * 72 + 64],
                scalar1=rdn_sb[:], scalar2=None, op0=ALU.mult))(qc, qt, h))
            bank_cons[7] = t_cx
    t_out_tokens = []

    def ph2_batch(bi):
        for h in range(H):
            par = h % 2
            t_kh = []
            t_vh = []
            for r in range(4):
                t_kh.append(sch.op("sync", (lambda b_, h_, p_, r_: lambda g: g.dma_start(
                    out=ksb[p_][:, r_ * 1024:(r_ + 1) * 1024],
                    in_=kall[b_][r_ * H * 128 + h_ * 128:
                                 r_ * H * 128 + (h_ + 1) * 128, :]))(bi, h, par, r),
                    wait=[frq["k"][par], t_agk[bi]] + bar1, dma=True, slot=f"kh{par}{r}"))
                t_vh.append(sch.op("sync", (lambda b_, h_, p_, r_: lambda g: g.dma_start(
                    out=vh_sb[p_][:, r_ * 8:(r_ + 1) * 8, 0:64],
                    in_=vall[b_][r_ * OWN:(r_ + 1) * OWN, h_ * 64:(h_ + 1) * 64]
                        .rearrange("(c p) d -> p c d", p=128)))(bi, h, par, r),
                    wait=[frq["v"][par], t_agv[bi]], dma=True, slot=f"vh{par}{r}"))
            t_qh = []
            for hh in range(2):
                dstq = (qdh_sb, qdl_sb)[hh]
                for dup in range(2):
                    t_qh.append(sch.op("sync", (lambda b_, h_, p_, hh_, d_, dq_: lambda g: g.dma_start(
                        out=dq_[p_][d_ * 64:(d_ + 1) * 64, :],
                        in_=qspill[b_, h_ * 128 + hh_ * 64: h_ * 128 + (hh_ + 1) * 64, :]))(
                            bi, h, par, hh, dup, dstq),
                        wait=[frq["q"][par]] if (hh == 0 and dup == 0) else (),
                        dma=True, slot=f"qh{par}{hh}{dup}"))
            last_s = None
            last_pv = None
            exp_toks = [None] * 32
            s_done = [None] * 32

            def emit_s(kc):
                nonlocal last_s
                sp = kc % 2
                w = t_kh + t_qh
                if kc >= 2:
                    w.append(exp_toks[kc - 2])
                else:
                    w += [bank_cons[2 * sp], bank_cons[2 * sp + 1]]
                for qt in range(2):
                    sch.op("tensor", (lambda p_, k_, s_, q_: lambda g: g.matmul(
                        psS[s_][:, q_ * 512:(q_ + 1) * 512],
                        ksb[p_][:, k_ * 128:(k_ + 1) * 128],
                        qdh_sb[p_][:, q_ * 512:(q_ + 1) * 512],
                        start=True, stop=False))(par, kc, sp, qt),
                        wait=w if qt == 0 else ())
                    last_s = sch.op("tensor", (lambda p_, k_, s_, q_: lambda g: g.matmul(
                        psS[s_][:, q_ * 512:(q_ + 1) * 512],
                        ksb[p_][:, k_ * 128:(k_ + 1) * 128],
                        qdl_sb[p_][:, q_ * 512:(q_ + 1) * 512],
                        start=False, stop=True))(par, kc, sp, qt))
                s_done[kc] = last_s

            def emit_exp(kc):
                sp = kc % 2
                pp = kc % 3
                exp_toks[kc] = sch.op("scalar", (lambda s_, p_: lambda g: g.activation(
                    out=pT_sb[p_][:], in_=psS[s_][:], func=AF.Exp,
                    bias=negc_sb[:], scale=SM_SCALE))(sp, pp),
                    wait=[s_done[kc], frq["pT"][pp]])

            def emit_pv(kc):
                nonlocal last_pv
                pp = kc % 3
                pvs = []
                for qt in range(2):
                    wp = [exp_toks[kc]] + t_vh
                    if kc == 0:
                        wp.append(bank_cons[4 + qt])
                    pvs.append(sch.op("tensor", (lambda p_, k_, q_, pp_: lambda g: g.matmul(
                        psC[q_][:], vh_sb[p_][:, k_, :], pT_sb[pp_][:, q_ * 512:(q_ + 1) * 512],
                        start=(k_ == 0), stop=(k_ == 31)))(par, kc, qt, pp),
                        wait=wp))
                frq["pT"][pp] = pvs
                last_pv = pvs

            emit_s(0)
            emit_exp(0)
            for kc in range(1, 32):
                emit_s(kc)
                emit_exp(kc)
                emit_pv(kc - 1)
                if kc in (4, 8) and deferred:
                    emit_ctx_finish(*deferred.pop(0))
            emit_pv(31)
            bank_cons[0] = bank_cons[1] = exp_toks[30]
            bank_cons[2] = bank_cons[3] = exp_toks[31]
            for qt in range(2):
                cpar = (2 * h + qt) % 2
                t_dn = sch.op("vector", (lambda q_: lambda g: g.stream_shuffle(
                    out=dn_sb[0:32, :], in_=psC[q_][64:96, :], mask=[0] * 32))(qt),
                    wait=last_pv + [frq["dn"]])
                sch.op("vector", lambda g: g.stream_shuffle(
                    out=dn_sb[32:64, :], in_=dn_sb[0:32, :], mask=[0] * 32))
                t_dn = sch.op("vector", lambda g: g.stream_shuffle(
                    out=dn_sb[64:96, :], in_=dn_sb[32:64, :], mask=[0] * 32))
                t_cev = sch.op("vector", (lambda q_, h_, c_: lambda g: g.scalar_tensor_tensor(
                    out=ctxT_sb[c_][:], in0=dn_sb[0:65, :], scalar=bvc_sb[:, h_:h_ + 1],
                    in1=psC[q_][0:65, :], op0=ALU.mult, op1=ALU.add))(qt, h, cpar),
                    wait=[t_bv, frq["ctxT"][cpar]])
                frq["dn"] = t_cev
                bank_cons[4 + qt] = t_cev
                if DEBUG_DUMP and bi == 1 and h == 0 and qt == 0:
                    tdmp = sch.op("gpsimd", (lambda c_: lambda g: g.dma_start(
                        out=dbg["ctxT"], in_=ctxT_sb[c_][:]))(cpar), wait=[t_cev], dma=True, slot="dbgctxT")
                    frq["ctxT"][cpar] = [frq["ctxT"][cpar], tdmp]
                deferred.append((h, qt, cpar, t_cev))
            frq["k"][par] = last_s
            frq["v"][par] = last_pv
            frq["q"][par] = last_s
        while deferred:
            emit_ctx_finish(*deferred.pop(0))
        return ("tV", sch.count["tV"])

    def ph3_batch(bi, t_ctx_ready, do_ln2):
        last_tok = None
        for stl in range(8):
            sp = stl % 2
            t_ld = sch.op("sync", (lambda s_, b_, p_: lambda g: g.dma_start(
                out=xatt_sb[p_][:], in_=xin[b_][s_ * 128:(s_ + 1) * 128, :]))(stl, bi, sp),
                wait=[frq["xatt"][sp]], dma=True, slot=f"xatt{sp}")
            t_add = sch.op("vector", (lambda s_, p_: lambda g: g.tensor_tensor(
                out=xatt_sb[p_][:], in0=xatt_sb[p_][:], in1=ctx_sb[:, s_, :],
                op=ALU.add))(stl, sp), wait=[t_ld, t_ctx_ready])
            t_st = sch.op("gpsimd", (lambda s_, b_, p_: lambda g: g.dma_start(
                out=out[b_ * OWN + s_ * 128: b_ * OWN + (s_ + 1) * 128, :],
                in_=xatt_sb[p_][:]))(stl, bi, sp), wait=[t_add], dma=True, slot=f"ost{sp}")
            t_out_tokens.append(t_st)
            if do_ln2:
                t_y = ln_normalize(xatt_sb[sp][:], y_sb[sp][:], t_add, sp,
                                   extra=[frq["y"][sp]])
                t_ys = sch.op("gpsimd", (lambda s_, p_: lambda g: g.dma_start(
                    out=yspill[s_ * 128:(s_ + 1) * 128, :], in_=y_sb[p_][:]))(stl, sp),
                    wait=[t_y, frq["y"][sp]], dma=True, slot=f"ys{sp}")
                frq["y"][sp] = t_ys
                frq["xatt"][sp] = [t_st, t_y]
                last_tok = t_ys
            else:
                frq["xatt"][sp] = t_st
                last_tok = t_st
        return last_tok, list(t_out_tokens)

    tA = ph2_batch(0)
    t_y_last, t_ph3a_dG = ph3_batch(0, tA, True)
    tB = ph2_batch(1)
    if DEBUG_DUMP:
        sch.op("gpsimd", lambda g: g.dma_start(
            out=dbg["ctx"], in_=ctx_sb[:].rearrange("p t d -> p (t d)")),
            wait=[tB], dma=True, slot="dbgctx")
    _, t_ph3b_dG = ph3_batch(1, tB, False)
    t_out_base = t_ph3a_dG + t_ph3b_dG
    ph2.close()
    bar2 = sch.barrier()

    ph4 = ExitStack()
    e4 = ph4.enter_context
    gidx_sb = e4(nc.sbuf_tensor("gidx_sb", [128, E * G], U32))
    cntt_sb = e4(nc.sbuf_tensor("cntt_sb", [128, E * G], F32))
    cwg_sb = e4(nc.sbuf_tensor("cwg_sb", [128, E], F32))
    m_sb = e4(nc.sbuf_tensor("m_sb", [128, 2, G], F32))
    gy_sb = [e4(nc.sbuf_tensor(f"gy_sb{i}", [128, G, D], F32)) for i in range(2)]
    gyT_sb = [e4(nc.sbuf_tensor(f"gyT_sb{i}", [128, 2, 8, G * 128], BF16)) for i in range(2)]
    gT_sb = e4(nc.sbuf_tensor("gT_sb", [128, 2, 32, G * 128], BF16))
    w1s_sb = e4(nc.sbuf_tensor("w1s_sb", [128, 8, 512], F32))
    w1_sb = [e4(nc.sbuf_tensor(f"w1_sb{i}", [128, 2, 8, 512], BF16)) for i in range(2)]
    w2s_sb = [e4(nc.sbuf_tensor(f"w2s_sb{i}", [128, 2, 512], F32)) for i in range(2)]
    w2_sb = [e4(nc.sbuf_tensor(f"w2_sb{i}", [128, 2, 2, 512], BF16)) for i in range(2)]
    b1_sb = [e4(nc.sbuf_tensor("b1_sb0", [128, 32], F32))] * 2
    b2_sb = [e4(nc.sbuf_tensor("b2_sb0", [1, D], F32))] * 2
    gt_sb = e4(nc.sbuf_tensor("gt_sb", [128, G * 128], F32))
    sc_sb = [e4(nc.sbuf_tensor(f"sc_sb{i}", [128, D], F32)) for i in range(G)]

    t_gi = sch.op("sync", lambda g: g.dma_start(out=gidx_sb[:], in_=gidx),
                  wait=bar2, dma=True, slot="gidx")
    t_cn = sch.op("sync", lambda g: g.dma_start(out=cntt_sb[:], in_=cntt), dma=True, slot="cntt")

    frf = {"w1": [None, None], "w2": [None, None], "gy": [[None], [None]],
           "gyT": [None, None], "gT": None, "b": [None, None],
           "sc": [None] * G, "cwg": None, "m": [[None] * G, [None] * G]}
    t_scatters = []
    t_gath_e = [None] * (E + 1)
    t_m_e = [None] * (E + 1)

    def emit_gathers(ee):
        par = ee % 2
        t_gath = []
        t_m = []
        for gg in range(G):
            t_g = sch.op("gpsimd", (lambda e_, g_, p_: lambda g: g.indirect_dma_start(
                out=gy_sb[p_][:, g_, :], out_offset=None, in_=yspill,
                in_offset=IndirectOffsetOnAxis(gidx_sb[:, e_ * G + g_:e_ * G + g_ + 1], 0)))(ee, gg, par),
                wait=[t_gi, t_y_last] + frf["gy"][par] + bar2, dma=True, slot=f"gy{par}{gg}")
            t_gath.append(t_g)
            t_w = sch.op("gpsimd", (lambda e_, g_: lambda g: g.indirect_dma_start(
                out=cwg_sb[:], out_offset=None, in_=cw,
                in_offset=IndirectOffsetOnAxis(gidx_sb[:, e_ * G + g_:e_ * G + g_ + 1], 0)))(ee, gg),
                wait=[frf["cwg"]], dma=True, slot="cwg")
            t_m.append(sch.op("vector", (lambda e_, g_, p_: lambda g: g.tensor_tensor(
                out=m_sb[:, p_, g_:g_ + 1], in0=cntt_sb[:, e_ * G + g_:e_ * G + g_ + 1],
                in1=cwg_sb[:, e_:e_ + 1], op=ALU.mult))(ee, gg, par),
                wait=[t_w, t_cn, frf["m"][par][gg]]))
            frf["cwg"] = t_m[-1]
        t_gath_e[ee] = t_gath
        t_m_e[ee] = t_m

    def emit_gyT(ee):
        par = ee % 2
        t_gath = t_gath_e[ee]
        t_gyT = None
        gy_cons = []
        for gg in range(G):
            for half in range(2):
                tbk = 6 + half
                tr_tok = None
                for j in range(4):
                    dc = half * 4 + j
                    w = [t_gath[gg], bank_cons[tbk]] if j == 0 else [t_gath[gg]]
                    tr_tok = sch.op("tensor", (lambda g_, d_, b_, j_, p_: lambda g: g.transpose(
                        ps[b_][:, j_ * 128:(j_ + 1) * 128],
                        gy_sb[p_][:, g_, d_ * 128:(d_ + 1) * 128], ident_sb[:]))(gg, dc, tbk, j, par),
                        wait=w)
                gy_cons.append(tr_tok)
                t_hi = sch.op("scalar", (lambda g_, h_, b_, p_: lambda g: g.activation(
                    out=gyT_sb[p_][:, 0, h_ * 4:(h_ + 1) * 4, g_ * 128:(g_ + 1) * 128],
                    in_=ps[b_][:].rearrange("p (c t) -> p c t", c=4),
                    func=AF.Identity, scale=1.0))(gg, half, tbk, par),
                    wait=[tr_tok, frf["gyT"][par]])
                t_gyT = sch.op("vector", (lambda g_, h_, b_, p_: lambda g: g.tensor_tensor(
                    out=gyT_sb[p_][:, 1, h_ * 4:(h_ + 1) * 4, g_ * 128:(g_ + 1) * 128],
                    in0=ps[b_][:].rearrange("p (c t) -> p c t", c=4),
                    in1=gyT_sb[p_][:, 0, h_ * 4:(h_ + 1) * 4, g_ * 128:(g_ + 1) * 128],
                    op=ALU.subtract))(gg, half, tbk, par),
                    wait=[t_hi])
                bank_cons[tbk] = t_gyT
        frf["gy"][par] = gy_cons
        return t_gyT

    emit_gathers(0)
    t_gyT_par = [None, None]
    t_gyT_par[0] = emit_gyT(0)

    for ee in range(E):
        par = ee % 2
        t_gyT = t_gyT_par[par]
        t_m = t_m_e[ee]
        t_b1 = sch.op("sync", (lambda e_: lambda g: g.dma_start(
            out=b1_sb[0][:], in_=b1t[e_]))(ee),
            wait=[frf.get("b1last")] + bar2, dma=True, slot="b1")
        t_b2 = sch.op("sync", (lambda e_: lambda g: g.dma_start(
            out=b2_sb[0][:], in_=b2f[e_:e_ + 1, :]))(ee),
            wait=[frf.get("b2last")], dma=True, slot="b2")
        if ee + 1 < E:
            emit_gathers(ee + 1)

        t_gT = None
        for fc in range(8):
            wpar = fc % 2
            t_ws = sch.op("sync", (lambda e_, f_: lambda g: g.dma_start(
                out=w1s_sb[:],
                in_=w1t[e_, f_].rearrange("p (d c) -> p d c", d=8)))(ee, fc),
                wait=[frf.get("w1s")], dma=True, slot="w1s")
            t_wh = sch.op("scalar", (lambda p_: lambda g: g.activation(
                out=w1_sb[p_][:, 0], in_=w1s_sb[:], func=AF.Identity, scale=1.0))(wpar),
                wait=[t_ws, frf["w1"][wpar]])
            t_w1 = sch.op("vector", (lambda p_: lambda g: g.tensor_tensor(
                out=w1_sb[p_][:, 1], in0=w1s_sb[:], in1=w1_sb[p_][:, 0],
                op=ALU.subtract))(wpar), wait=[t_wh])
            frf["w1s"] = t_w1
            for sub in range(4):
                ffb = fc * 4 + sub
                pb = ffb % 3
                last_mm = None
                for dc in range(8):
                    w = [t_w1, t_gyT, bank_cons[pb]] if dc == 0 else []
                    for wi, gi in ((0, 0), (1, 0), (0, 1)):
                        last_mm = sch.op("tensor", (lambda d_, b_, p_, ep_, gi_, wi_, s_: lambda g: g.matmul(
                            ps[b_][:, 0:G * 128],
                            w1_sb[p_][:, wi_, d_, s_ * 128:(s_ + 1) * 128],
                            gyT_sb[ep_][:, gi_, d_, :],
                            start=(d_ == 0 and wi_ == 0 and gi_ == 0),
                            stop=(d_ == 7 and wi_ == 0 and gi_ == 1)))(
                                dc, pb, wpar, par, gi, wi, sub),
                            wait=w if (wi == 0 and gi == 0) else ())
                frf["w1"][wpar] = last_mm
                t_g32 = sch.op("scalar", (lambda b_, f2_: lambda g: g.activation(
                    out=gt_sb[:, 0:G * 128], in_=ps[b_][:, 0:G * 128],
                    func=AF.Gelu_apprx_tanh,
                    bias=b1_sb[0][:, f2_:f2_ + 1], scale=1.0))(pb, ffb),
                    wait=[last_mm, t_b1, frf.get("gt")])
                t_hi2 = sch.op("vector", (lambda f2_: lambda g: g.tensor_copy(
                    out=gT_sb[:, 0, f2_, :], in_=gt_sb[:, 0:G * 128]))(ffb),
                    wait=[t_g32, frf["gT"]])
                t_gT = sch.op("vector", (lambda f2_: lambda g: g.tensor_tensor(
                    out=gT_sb[:, 1, f2_, :], in0=gt_sb[:, 0:G * 128],
                    in1=gT_sb[:, 0, f2_, :], op=ALU.subtract))(ffb))
                frf["gt"] = t_gT
                bank_cons[pb] = t_gT
        frf["gyT"][par] = t_gT

        if ee + 1 < E:
            t_gyT_par[(ee + 1) % 2] = emit_gyT(ee + 1)

        t_ev = None
        for oc in range(2):
            mm_done = [None] * G
            for fp_ in range(16):
                wpar = fp_ % 2
                t_ws2 = sch.op("sync", (lambda e_, o_, f_, p_: lambda g: g.dma_start(
                    out=w2s_sb[p_][:],
                    in_=w2t[e_, o_, 2 * f_:2 * f_ + 2].rearrange("a p c -> p a c")))(ee, oc, fp_, wpar),
                    wait=[frf.get(f"w2s{wpar}")], dma=True, slot=f"w2s{wpar}")
                t_wh2 = sch.op("scalar", (lambda p_: lambda g: g.activation(
                    out=w2_sb[p_][:, 0], in_=w2s_sb[p_][:], func=AF.Identity, scale=1.0))(wpar),
                    wait=[t_ws2, frf["w2"][wpar]])
                t_w2 = sch.op("vector", (lambda p_: lambda g: g.tensor_tensor(
                    out=w2_sb[p_][:, 1], in0=w2s_sb[p_][:], in1=w2_sb[p_][:, 0],
                    op=ALU.subtract))(wpar), wait=[t_wh2])
                frf[f"w2s{wpar}"] = t_w2
                for sub in range(2):
                    fc2 = 2 * fp_ + sub
                    for tg in range(G):
                        pb = tg
                        if fc2 == 0:
                            frf["b2last"] = sch.op("tensor", (lambda o_, b_, p_: lambda g: g.matmul(
                                ps[b_][:], ones_sb[:], b2_sb[p_][:, o_ * 512:(o_ + 1) * 512],
                                start=True, stop=False))(oc, pb, par),
                                wait=[t_b2, t_ones, bank_cons[pb]])
                        for gi, wi in ((0, 0), (1, 0), (0, 1)):
                            mm_done[tg] = sch.op("tensor", (lambda f_, t_, b_, p_, gi_, wi_, s_: lambda g: g.matmul(
                                ps[b_][:], gT_sb[:, gi_, f_, t_ * 128:(t_ + 1) * 128],
                                w2_sb[p_][:, wi_, s_, :],
                                start=False, stop=(f_ == 31 and gi_ == 0 and wi_ == 1)))(
                                    fc2, tg, pb, wpar, gi, wi, sub),
                                wait=([t_w2, t_gT] if fc2 == 0 else [t_w2]) if (gi == 0 and wi == 0 and sub == 0) else ())
                    frf["w2"][wpar] = mm_done[G - 1]
            for tg in range(G):
                pb = tg
                t_ev = sch.op("vector", (lambda t_, o_, b_, p_: lambda g: g.tensor_scalar(
                    out=sc_sb[t_][:, o_ * 512:(o_ + 1) * 512], in0=ps[b_][:],
                    scalar1=m_sb[:, p_, t_:t_ + 1], scalar2=None, op0=ALU.mult))(tg, oc, pb, par),
                    wait=[mm_done[tg], t_m[tg], frf["sc"][tg]])
                bank_cons[pb] = t_ev
                frf["m"][par][tg] = t_ev
                if oc == 1:
                    t_sc = sch.op("gpsimd", (lambda e_, t_: lambda g: g.indirect_dma_start(
                        out=out, out_offset=IndirectOffsetOnAxis(
                            gidx_sb[:, e_ * G + t_:e_ * G + t_ + 1], 0),
                        in_=sc_sb[t_][:], in_offset=None,
                        compute_op=ALU.add))(ee, tg),
                        wait=[t_ev] + t_out_base, dma=True, slot=f"sc{tg}")
                    frf["sc"][tg] = t_sc
                    t_scatters.append(t_sc)
        frf["gT"] = t_ev

    ph4.close()

    final_waits = sch.barrier()
    with st_main:
        block = st_main.enter_context(nc.Block())
        sch.emit_all(block, "gpsimd", final_waits)
    return nc


def _split_hi_lo(a64):
    hi = a64.astype(np.float32)
    lo = (a64 - hi.astype(np.float64)).astype(np.float32)
    return hi, lo


def _rope_tables_f64(qoff):
    pos = np.arange(S, dtype=np.float64)
    inv = np.power(10000.0, -2.0 * np.arange(HD // 2, dtype=np.float64) / HD)
    ang = pos[None, :] * inv[:, None]
    cos64 = np.empty((HD, S), np.float64)
    sin64 = np.empty((HD, S), np.float64)
    cos64[0::2] = np.cos(ang)
    cos64[1::2] = np.cos(ang)
    sin64[0::2] = -np.sin(ang)
    sin64[1::2] = np.sin(ang)
    cos64 = np.roll(cos64, -qoff, axis=1)
    sin64 = np.roll(sin64, -qoff, axis=1)
    cos64 = np.concatenate([cos64, cos64], axis=0)
    sin64 = np.concatenate([sin64, sin64], axis=0)
    return cos64, sin64


def _plan_from_top_idx(TI):
    tok = TI.T
    plans = []
    maxd = 1
    for c in range(NCORES):
        bA = c // 4
        qoff = (c % 4) * OWN
        lo, hi_ = bA * S + qoff, bA * S + qoff + OWN
        per_e = []
        for ee in range(E):
            te = tok[ee]
            sel = te[(te >= lo) & (te < hi_)] - lo
            rows, cnts = np.unique(sel, return_counts=True)
            per_e.append((rows.astype(np.int64), cnts.astype(np.int64)))
            maxd = max(maxd, len(rows))
        plans.append(per_e)
    G = (maxd + 127) // 128
    return plans, G


def kernel(L0_attn_hidden_states, L0_attn_Wq, L0_attn_bq, L0_attn_Wk, L0_attn_bk,
           L0_attn_Wv, L0_attn_bv, L0_ffn_combine_weights, L0_ffn_top_idx,
           L0_ffn_W1, L0_ffn_b1, L0_ffn_W2, L0_ffn_b2):
    X = np.ascontiguousarray(np.asarray(L0_attn_hidden_states, dtype=np.float32))
    Wq = np.asarray(L0_attn_Wq, np.float32); Wk = np.asarray(L0_attn_Wk, np.float32)
    Wv = np.asarray(L0_attn_Wv, np.float32)
    bq = np.asarray(L0_attn_bq, np.float32); bk = np.asarray(L0_attn_bk, np.float32)
    bv = np.asarray(L0_attn_bv, np.float32)
    CWf = np.ascontiguousarray(np.asarray(L0_ffn_combine_weights, np.float32))
    TI = np.asarray(L0_ffn_top_idx).astype(np.int64)
    W1 = np.asarray(L0_ffn_W1, np.float32); b1 = np.asarray(L0_ffn_b1, np.float32)
    W2 = np.asarray(L0_ffn_W2, np.float32); b2 = np.asarray(L0_ffn_b2, np.float32)

    plans, G = _plan_from_top_idx(TI)

    wqt = Wq.reshape(8, 128, HP, 128).transpose(2, 1, 0, 3).reshape(HP, 128, D).copy()
    wkt = Wk.reshape(8, 128, HP, 128).transpose(2, 1, 0, 3).reshape(HP, 128, D).copy()
    wvt = Wv.reshape(8, 128, D).transpose(1, 0, 2).copy()
    bqt = bq.reshape(HP, 128).T.copy(); bkt = bk.reshape(HP, 128).T.copy()
    bvcm = np.zeros((65, H), np.float32); bvcm[0:64, :] = bv.reshape(H, 64).T
    identm = np.eye(128, dtype=np.float32)
    b1tm = np.ascontiguousarray(b1.reshape(E, 32, 128).transpose(0, 2, 1))
    w1t = np.ascontiguousarray(
        W1.reshape(E, 8, 128, 8, 512).transpose(0, 3, 2, 1, 4)).reshape(E, 8, 128, FF)
    w2t = np.ascontiguousarray(W2.reshape(E, 32, 128, 2, 512).transpose(0, 3, 1, 2, 4))

    in_maps = []
    for c in range(NCORES):
        bA = c // 4
        bB = 2 + c // 4
        qoff = (c % 4) * OWN
        xa = np.ascontiguousarray(X[bA, qoff:qoff + OWN])
        xb = np.ascontiguousarray(X[bB, qoff:qoff + OWN])
        cos64, sin64 = _rope_tables_f64(qoff)
        rc_hi, rc_lo = _split_hi_lo(cos64)
        rs_hi, rs_lo = _split_hi_lo(sin64)
        cwc = np.ascontiguousarray(CWf[bA * S + qoff: bA * S + qoff + OWN, :])
        gidxm = np.zeros((E * G, 128), np.uint32)
        cnttm = np.zeros((E * G, 128), np.float32)
        for ee in range(E):
            rows, cnts = plans[c][ee]
            n = len(rows)
            used = np.zeros(OWN, bool)
            used[rows] = True
            padrows = np.nonzero(~used)[0][: G * 128 - n]
            allrows = np.concatenate([rows, padrows])
            allcnt = np.concatenate([cnts, np.zeros(len(padrows), np.int64)])
            gidxm[ee * G:(ee + 1) * G, :] = allrows.reshape(G, 128)
            cnttm[ee * G:(ee + 1) * G, :] = allcnt.reshape(G, 128)
        in_maps.append(dict(
            xa=xa, xb=xb, wqt=wqt, wkt=wkt, wvt=wvt, bqt=bqt, bkt=bkt, bvc=bvcm,
            rc_hi=rc_hi, rc_lo=rc_lo, rs_hi=rs_hi, rs_lo=rs_lo, ident=identm,
            cw=cwc, cntt=np.ascontiguousarray(cnttm.T), gidx=np.ascontiguousarray(gidxm.T),
            b1t=b1tm, b2f=b2, w1t=w1t, w2t=w2t))

    nc = build_program(G)
    res = run_bass_kernel_spmd(nc, in_maps, core_ids=list(range(NCORES)))

    result = np.empty((B, S, D), np.float32)
    for c in range(NCORES):
        o = res.results[c]["out"]
        bA = c // 4
        bB = 2 + c // 4
        qoff = (c % 4) * OWN
        result[bA, qoff:qoff + OWN] = o[0:OWN]
        result[bB, qoff:qoff + OWN] = o[OWN:2 * OWN]
    return result
